# version 08
# baseline (speedup 1.0000x reference)
import numpy as np
import ml_dtypes
import concourse.bass as bass
import concourse.mybir as mybir
from concourse.bass_utils import run_bass_kernel_spmd

F32 = mybir.dt.float32
I32 = mybir.dt.int32
BF16 = mybir.dt.bfloat16
AF = mybir.ActivationFunctionType
ALU = mybir.AluOpType
AX = mybir.AxisListType

B = 4
N_FULL = 4096
T = 2048
D = 1024
KC = 8
H = 8
DH = 64
INNER = 512
WIN = 512
NWIN = T // WIN
HID = 2730
NPAIR = 22
DK = 128
DV = 128
CH = 128
NCH = T // CH
QK_SCALE = 8.0
PHI_SCALE = 1.0 / 16.0
LN_EPS = 1e-5

ENGINES = ("tensor", "vector", "scalar", "sync", "gpsimd")

CONST_BIAS_VALUES = [1e-5, 1e-24, 1e-30, -8.0, float(np.pi / 2), 0.5, 2.0]

DEBUG_TAPS = []
SNAP_MODE = False



def _w1_perm():
    idx = []
    for p in range(NPAIR):
        w = min(128, HID - 128 * p)
        a = np.arange(128 * p, 128 * p + w)
        idx.append(a)
        idx.append(HID + a)
    return np.concatenate(idx)


def _pair_off(p):
    return 256 * p if p < 21 else 256 * 21, min(128, HID - 128 * p)



class Ctx:
    def __init__(self, nc):
        self.nc = nc
        self.ctr = {}
        self.sems = {}
        self.stack = None
        self.ten = {}
        self.psum = []

    def sem(self, name):
        if name not in self.sems:
            h = self.nc.alloc_semaphore(name=name)
            self.sems[name] = h
            self.ctr[name] = 0
        return self.sems[name]


class Phase:

    def __init__(self, ctx, name):
        self.ctx = ctx
        self.name = name
        self.items = {e: [] for e in ENGINES}
        self.end = {}

    def w(self, eng, sem, val):
        h = self.ctx.sem(sem)
        self.items[eng].append(dict(kind="wait", sem=sem, val=int(val),
                                    fn=lambda E, h=h, v=int(val): E.wait_ge(h, v)))

    def do(self, eng, fn, inc=None, n=1, tag=""):
        item = dict(kind="do", sem=inc, val=n, tag=tag)
        if inc is not None:
            h = self.ctx.sem(inc)
            self.ctx.ctr[inc] += n

            def run(E, fn=fn, h=h, n=n):
                ins = fn(E)
                ins.then_inc(h, n)
            item["fn"] = run
            item["at"] = self.ctx.ctr[inc]
        else:
            item["fn"] = fn
        self.items[eng].append(item)
        return self.ctx.ctr.get(inc) if inc is not None else None

    def ss(self, eng):
        sems = {"tensor": "T", "vector": "V", "scalar": "S",
                "sync": "Y", "gpsimd": "G"}
        s = sems[eng]
        if self.ctx.ctr.get(s, 0) > 0:
            self.w(eng, s, self.ctx.ctr[s])

    def close(self):
        planned = {}
        for e in ENGINES:
            for it in self.items[e]:
                if it["kind"] == "do" and it["sem"] is not None:
                    planned[it["sem"]] = planned.get(it["sem"], 0) + it["val"]
        state = {s: self.ctx.ctr[s] - planned.get(s, 0) for s in self.ctx.ctr}
        heads = {e: 0 for e in ENGINES}
        progress = True
        while progress:
            progress = False
            for e in ENGINES:
                while heads[e] < len(self.items[e]):
                    it = self.items[e][heads[e]]
                    if it["kind"] == "wait" and state.get(it["sem"], 0) < it["val"]:
                        break
                    if it["kind"] == "do" and it["sem"] is not None:
                        state[it["sem"]] = state.get(it["sem"], 0) + it["val"]
                    heads[e] += 1
                    progress = True
        stuck = {e: (heads[e], len(self.items[e])) for e in ENGINES
                 if heads[e] < len(self.items[e])}
        if stuck:
            details = []
            for e, (h, tot) in stuck.items():
                it = self.items[e][h]
                details.append(f"{e}@{h}/{tot} waiting {it.get('sem')}>={it.get('val')}"
                               f" (have {state.get(it.get('sem'), 0)})")
            raise RuntimeError(f"phase {self.name}: deadlock: {details}; state={state}")

        with self.ctx.nc.named_scope(f"ph_{self.name}"):
            with self.ctx.nc.Block() as block:
                for e in ENGINES:
                    if not self.items[e]:
                        continue

                    def body(E, items=self.items[e]):
                        for it in items:
                            it["fn"](E)
                    getattr(block, e)(body)


class Joiner:

    def __init__(self, ctx):
        self.ctx = ctx
        self.esem = {"tensor": "T", "vector": "V", "scalar": "S",
                     "sync": "Y", "gpsimd": "G"}
        for s in self.esem.values():
            ctx.sem(s)

    def join(self, phase, engines=ENGINES, excl=()):
        vals = {e: self.ctx.ctr[self.esem[e]] for e in ENGINES}
        for e in engines:
            for o in ENGINES:
                if o == e or o in excl:
                    continue
                if vals[o] > 0:
                    phase.w(e, self.esem[o], vals[o])



def build_program(upto="full"):
    nc = bass.Bass("TRN2", target_bir_lowering=False, debug=False, detect_race_conditions=False)
    ctx = Ctx(nc)
    import contextlib
    ctx.stack = contextlib.ExitStack()

    dram = {}

    def din(name, shape, dt=F32):
        dram[name] = nc.dram_tensor(name, list(shape), dt, kind="ExternalInput")
        return dram[name]

    def dint(name, shape, dt=F32):
        dram[name] = nc.dram_tensor(name, list(shape), dt, kind="Internal")
        return dram[name]

    din("x_t", [D, T])
    din("xprev", [128, KC])
    din("coremask", [128, 1])
    din("toff", [1, 1])
    for l in range(2):
        din(f"wqkv{l}", [D, 3 * INNER])
        din(f"wo{l}", [INNER, D])
        din(f"w2{l}", [HID, D])
        din(f"w1p{l}", [D, 2 * HID])
        din(f"b1p{l}", [128, 2 * NPAIR])
        din(f"b2{l}", [128, KC])
        din(f"lng{l}", [128, KC])
        din(f"lnb{l}", [128, KC])
        din(f"qs{l}", [128, 1])
        din(f"ks{l}", [128, 1])
    for nm in ("wqT", "wkT", "wvT", "woT"):
        din(nm, [D, D])
    din("fmqT", [DK, DK])
    din("fmkT", [DK, DK])
    din("fmq_b", [DK, 1])
    din("fmk_b", [DK, 1])
    din("fmq_b_row", [1, DK])
    din("fmk_b_row", [1, DK])
    din("rms_w", [DK, 1])

    out = nc.dram_tensor("out", [D, T], F32, kind="ExternalOutput")
    dram["out"] = out

    dint("x_spill", [D, T])
    dint("stash_wqkv", [2, D, 3 * INNER], BF16)
    dint("stash_w1p", [2, D, 2 * HID], BF16)
    dint("stash_w2", [2, HID, D], BF16)
    dint("stash_wo", [2, INNER, D], BF16)
    dint("rowbuf", [128, 16])
    dint("rowbuf2", [128, 16])
    dint("c1_in", [128, KC])
    dint("c1_out", [256, KC])
    dint("c2_in", [128, 4 * WIN + 4 * (8 * 65)], BF16)
    dint("c2_out", [256, 4 * WIN + 4 * (8 * 65)], BF16)
    dint("c4_in", [128, 16 * DK], BF16)
    dint("c4_out", [256, 16 * DK], BF16)

    for name in DEBUG_TAPS:
        pass

    st = ctx.stack

    def sb(name, shape, dt):
        h = st.enter_context(nc.sbuf_tensor(name, list(shape), dt))
        ctx.ten[name] = h
        return h

    xch = [sb(f"x{c}", [128, T + 1], F32) for c in range(KC)]
    m_top = sb("m_top", [128, 4 * WIN], BF16)
    m_top0 = sb("m_top0", [128, 4 * WIN], BF16)
    m_tril = sb("m_tril", [128, 4 * WIN], BF16)
    m_ltril = sb("m_ltril", [128, CH], BF16)
    sv = sb("smallv", [128, 80], F32)
    cst = sb("cstv", [128, max(8, len(CONST_BIAS_VALUES))], F32)
    ones_row = sb("ones_row", [1, 128], BF16)
    ones_row_f32 = sb("ones_row_f32", [1, 128], F32)
    ones_col = sb("ones_col", [128, 1], BF16)
    ones_col_f32 = sb("ones_col_f32", [128, 1], F32)
    seg2 = sb("seg2", [128, 2], BF16)
    halfsel = sb("halfsel", [2, 128], F32)
    pmat = sb("pmat", [128, 128], BF16)
    signcol = sb("signcol", [128, 1], F32)
    seg8 = sb("seg8", [128, 8], BF16)
    for bnk in range(8):
        ctx.psum.append(st.enter_context(nc.psum_tensor(f"pb{bnk}", [128, 512], F32)))

    SV = dict(coremask=0, fmq_b=1, fmq_bn=2, fmk_b=3, fmk_bn=4, rms_w=5,
              lng0=8, lnb0=16, lng1=24, lnb1=32, b2_0=40, b2_1=48, toff=56)

    ctx.ten["xch"] = xch
    J = Joiner(ctx)

    p = Phase(ctx, "P0")
    for c in range(KC):
        p.do("sync", lambda E, c=c: E.dma_start(out=xch[c][:, 1:T + 1],
                                                in_=dram["x_t"][128 * c:128 * (c + 1), :]),
             inc="XL", n=16)
    def ld_col(dst_col, src, shape_rows=128):
        p.do("gpsimd", lambda E, d=dst_col, s=src: E.dma_start(
            out=sv[:shape_rows, d:d + 1], in_=s), inc="GD", n=16)

    ld_col(SV["coremask"], dram["coremask"][:, :])
    ld_col(SV["fmq_b"], dram["fmq_b"][:, :])
    ld_col(SV["fmk_b"], dram["fmk_b"][:, :])
    ld_col(SV["rms_w"], dram["rms_w"][:, :])
    for l in range(2):
        p.do("gpsimd", lambda E, l=l: E.dma_start(
            out=sv[:, SV[f"lng{l}"]:SV[f"lng{l}"] + KC],
            in_=dram[f"lng{l}"][:, :]), inc="GD", n=16)
        p.do("gpsimd", lambda E, l=l: E.dma_start(
            out=sv[:, SV[f"lnb{l}"]:SV[f"lnb{l}"] + KC],
            in_=dram[f"lnb{l}"][:, :]), inc="GD", n=16)
        p.do("gpsimd", lambda E, l=l: E.dma_start(
            out=sv[:, SV[f"b2_{l}"]:SV[f"b2_{l}"] + KC],
            in_=dram[f"b2{l}"][:, :]), inc="GD", n=16)
    p.do("gpsimd", lambda E: E.dma_start(out=sv[0:1, SV["toff"]:SV["toff"] + 1],
                                         in_=dram["toff"][:, :]), inc="GD", n=16)
    SV["xprev"] = 57
    p.do("gpsimd", lambda E: E.dma_start(
        out=sv[:, SV["xprev"]:SV["xprev"] + KC],
        in_=dram["xprev"][:, :]), inc="GD", n=16)
    gq = ctx.ctr["GD"]
    p.w("vector", "GD", gq)
    p.do("vector", lambda E: E.tensor_scalar_mul(sv[:, SV["fmq_bn"]:SV["fmq_bn"] + 1],
                                                 sv[:, SV["fmq_b"]:SV["fmq_b"] + 1], -1.0), inc="V")
    p.do("vector", lambda E: E.tensor_scalar_mul(sv[:, SV["fmk_bn"]:SV["fmk_bn"] + 1],
                                                 sv[:, SV["fmk_b"]:SV["fmk_b"] + 1], -1.0), inc="V")
    for i, v in enumerate(CONST_BIAS_VALUES):
        p.do("vector", lambda E, i=i, v=v: E.memset(cst[:, i:i + 1], float(v)), inc="V")
        nc.const_aps.register(F32, float(v), cst[:, i:i + 1])
    p.do("vector", lambda E: E.memset(ones_row[:, :], 1.0), inc="V")
    p.do("vector", lambda E: E.memset(ones_row_f32[:, :], 1.0), inc="V")
    p.do("vector", lambda E: E.memset(ones_col[:, :], 1.0), inc="V")
    p.do("vector", lambda E: E.memset(ones_col_f32[:, :], 1.0), inc="V")
    p.do("vector", lambda E: E.memset(seg8[:, :], 0.0), inc="V")
    for cc in range(4):
        p.do("vector", lambda E, cc=cc: E.memset(seg8[0:64, 2 * cc:2 * cc + 1], 1.0), inc="V")
        p.do("vector", lambda E, cc=cc: E.memset(seg8[64:128, 2 * cc + 1:2 * cc + 2], 1.0), inc="V")

    p.do("vector", lambda E: E.memset(seg2[:, :], 0.0), inc="V")
    p.do("vector", lambda E: E.memset(seg2[0:64, 0:1], 1.0), inc="V")
    p.do("vector", lambda E: E.memset(seg2[64:128, 1:2], 1.0), inc="V")
    p.do("gpsimd", lambda E: E.iota(halfsel[:, :], [[1, 128]], channel_multiplier=-64,
                                    allow_small_or_imprecise_dtypes=True), inc="G")
    p.w("vector", "G", ctx.ctr["G"])
    p.do("vector", lambda E: E.tensor_scalar(halfsel[:, :], halfsel[:, :],
                                             0.0, None, ALU.is_ge), inc="V")
    p.do("vector", lambda E: E.tensor_scalar(mwork[0:2, 0:128], halfsel[:, :],
                                             0.0, None, ALU.bypass), inc="V", tag="tmp")
    p.do("gpsimd", lambda E: E.iota(mwork[0:2, 128:256], [[1, 128]], channel_multiplier=-64,
                                    base=-63, allow_small_or_imprecise_dtypes=True), inc="G")
    p.w("vector", "G", ctx.ctr["G"])
    p.do("vector", lambda E: E.tensor_scalar(mwork[0:2, 128:256], mwork[0:2, 128:256],
                                             0.0, None, ALU.is_le), inc="V")
    p.do("vector", lambda E: E.tensor_mul(halfsel[:, :], mwork[0:2, 0:128],
                                          mwork[0:2, 128:256]), inc="V")
    for s4 in range(4):
        val = -1.0 if s4 % 2 == 0 else 1.0
        p.do("vector", lambda E, s4=s4, val=val: E.memset(signcol[32 * s4:32 * (s4 + 1), :], val), inc="V")

    import contextlib as _ctl
    _mwork_stack = _ctl.ExitStack()
    mwork = _mwork_stack.enter_context(nc.sbuf_tensor("mwork", [128, 4 * WIN], F32))
    p.do("vector", lambda E: E.memset(mwork[:, :], 0.0), inc="V")
    p.w("gpsimd", "V", ctx.ctr["V"])
    for s4 in range(4):
        o = 32 if s4 % 2 == 0 else -32
        p.do("gpsimd", lambda E, s4=s4, o=o: E.iota(
            mwork[32 * s4:32 * (s4 + 1), 0:128], [[1, 128]],
            base=-(32 * s4) - o, channel_multiplier=-1,
            allow_small_or_imprecise_dtypes=True), inc="G")
    gpm = ctx.ctr["G"]
    p.w("vector", "G", gpm)
    p.do("vector", lambda E: E.tensor_scalar(pmat[:, :], mwork[:, 0:128],
                                             0.0, None, ALU.is_equal), inc="V")
    p.w("gpsimd", "V", ctx.ctr["V"])
    p.do("gpsimd", lambda E: E.iota(mwork[:, 0:CH], [[1, CH]], channel_multiplier=-1,
                                    allow_small_or_imprecise_dtypes=True), inc="G")
    g1 = ctx.ctr["G"]
    p.w("vector", "G", g1)
    p.do("vector", lambda E: E.tensor_scalar(m_ltril[:, 0:CH], mwork[:, 0:CH],
                                             0.0, PHI_SCALE, ALU.is_ge, ALU.mult), inc="V")

    p.w("gpsimd", "V", ctx.ctr["V"])
    for t4 in range(4):
        p.do("gpsimd", lambda E, t4=t4: E.iota(
            mwork[:, t4 * WIN:(t4 + 1) * WIN], [[1, WIN]],
            base=-128 * t4, channel_multiplier=-1,
            allow_small_or_imprecise_dtypes=True), inc="G")
    g2 = ctx.ctr["G"]
    p.w("vector", "G", g2)
    p.do("vector", lambda E: E.tensor_scalar(m_top[:, :], mwork[:, :],
                                             0.0, None, ALU.is_le), inc="V")
    p.do("vector", lambda E: E.tensor_scalar(m_tril[:, :], mwork[:, :],
                                             0.0, None, ALU.is_ge), inc="V")
    p.w("vector", "GD", gq)
    p.do("vector", lambda E: E.tensor_scalar_mul(m_top0[:, :], m_top[:, :],
                                                 sv[:, SV["coremask"]:SV["coremask"] + 1]), inc="V")
    p.ss("vector")
    for mt in (m_top, m_top0, m_tril):
        p.do("vector", lambda E, mt=mt: E.tensor_scalar(mt[:, :], mt[:, :],
                                                        -1.0, 1e7, ALU.add, ALU.mult), inc="V")
    p.ss("vector")
    p.close()
    _mwork_stack.close()

    ctx.ten["sv"] = sv
    ctx.ten["SV"] = SV
    ctx.ten["ones_row"] = ones_row
    ctx.ten["ones_row_f32"] = ones_row_f32
    ctx.ten["ones_col"] = ones_col
    ctx.ten["ones_col_f32"] = ones_col_f32
    ctx.ten["seg8"] = seg8
    ctx.ten["m_top"] = m_top
    ctx.ten["m_top0"] = m_top0
    ctx.ten["m_tril"] = m_tril
    ctx.ten["m_ltril"] = m_ltril
    ctx.ten["seg2"] = seg2
    ctx.ten["halfsel"] = halfsel
    ctx.ten["pmat"] = pmat
    ctx.ten["signcol"] = signcol
    ctx.dram = dram
    ctx.J = J

    if upto == "p0":
        ph_finale(ctx)
        return nc, ctx

    ph_macaron_ln(ctx, "ffn", bsrc=sv[:, SV["xprev"]:SV["xprev"] + KC])
    if upto == "ln1":
        ph_finale(ctx)
        return nc, ctx

    _dbgw = [int(ch) for ch in upto[5:]] if (upto or "").startswith("attnw") else None
    ffn_upto = {"ffn1l0qkv": "qkv_f1l0", "ffn1l0qkvraw": "qkv_f1l0",
                "ffn1l0attn": "attn_f1l0",
                "ffn1l0": "mlp_f1l0", "ffn1l1": "mlp_f1l1",
                "prep": "prep_f1l0", "tbl": "tbl_f1l0",
                "mlpdbg": "mlp_f1l0"}.get(upto)
    if _dbgw is not None:
        ffn_upto = "attn_f1l0"
    fl = ph_ffn_call(ctx, True, upto=ffn_upto, raw_tap=(upto == "ffn1l0qkvraw"),
                     dbg_w1=_dbgw, mlpdbg=(upto == "mlpdbg"))
    if upto in ("ffn1l0qkv", "ffn1l0qkvraw"):
        ph_dump_qkv(ctx, fl)
        return nc, ctx
    if upto == "tbl":
        ph_dump_tabs(ctx, fl["tabs"])
        return nc, ctx
    if _dbgw is not None:
        return nc, ctx
    if upto in ("ffn1l0attn", "ffn1l0", "ffn1l1", "prep", "tbl", "mlpdbg"):
        ph_finale(ctx)
        return nc, ctx

    ph_ffn_combine(ctx, dram["x_t"])
    if upto == "ffn1":
        ph_finale(ctx)
        return nc, ctx

    c0 = ph_attn(ctx, upto=upto)
    if upto in ("attn", "attnS") or (upto or "").startswith("attnD") or (upto or "").startswith("attnE"):
        ph_finale(ctx)
        return nc, ctx

    ph_spill(ctx)
    pr = Phase(ctx, "c0recv")
    J.join(pr)
    rv = coll_recv(ctx, pr, c0, [(sv[:, SV["xprev"]:SV["xprev"] + KC],
                                  dram["c1_out"][0:128, :])])
    pr.w("vector", "CR", rv)
    pr.do("vector", lambda E: E.tensor_scalar_mul(
        sv[:, SV["xprev"]:SV["xprev"] + KC], sv[:, SV["xprev"]:SV["xprev"] + KC],
        sv[:, SV["coremask"]:SV["coremask"] + 1]), inc="V")
    pr.close()
    ph_macaron_ln(ctx, "ffn", bsrc=sv[:, SV["xprev"]:SV["xprev"] + KC])
    ph_ffn_call(ctx, False)
    ph_ffn_combine(ctx, dram["x_spill"])

    ph_finale(ctx)
    return nc, ctx



_SBUF_SEQ = [0]


def _sb(nc, name, shape, dtype):
    _SBUF_SEQ[0] += 1
    return nc.sbuf_tensor(f"{name}_{_SBUF_SEQ[0]}", shape, dtype)



def bcast_row(ctx, p, psum_tile, row_ap, nb_cols):
    ones_row = ctx.ten["ones_row"]
    p.do("tensor", lambda E: E.matmul(psum_tile[:, 0:nb_cols], ones_row[:, :],
                                      row_ap, start=True, stop=True), inc="T")


def row_recip_sqrt(ctx, p, dst_row, src_row, eps, half):
    a = -0.5 if half else -1.0
    p.do("scalar", lambda E: E.activation(dst_row, src_row, AF.Ln, bias=float(eps)), inc="S")
    p.ss("scalar")
    p.do("scalar", lambda E: E.activation(dst_row, dst_row, AF.Exp, scale=float(a)), inc="S")


def ph_macaron_ln(ctx, which, xhat=None, bsrc=None):
    nc, J = ctx.nc, ctx.J
    xch = ctx.ten["xch"]
    sv, SV = ctx.ten["sv"], ctx.ten["SV"]
    ones_row = ctx.ten["ones_row_f32"]
    ps = ctx.psum
    NB = T // 512

    import contextlib
    with contextlib.ExitStack() as st:
        tmpb = [st.enter_context(_sb(nc, f"lnt{i}", [128, 512], BF16)) for i in range(4)]
        murow = st.enter_context(_sb(nc, "lnmu", [1, T], F32))
        ssrow = st.enter_context(_sb(nc, "lnss", [1, T], F32))
        tmprow = st.enter_context(_sb(nc, "lntr", [1, T], F32))
        rrow = ssrow
        brow = murow
        tf32 = [st.enter_context(_sb(nc, f"lnf{i}", [128, 512], F32)) for i in range(2)]
        bnd = st.enter_context(_sb(nc, "lnbnd", [128, 16], F32))

        p = Phase(ctx, f"LN-{which}")
        J.join(p)
        if ctx.ctr.get("XL", 0):
            for e in ("vector", "scalar", "tensor"):
                p.w(e, "XL", ctx.ctr["XL"])

        tmm = {}
        for nb in range(NB):
            for c in range(KC):
                i = (nb * KC + c) % 2
                buf, sqb = tmpb[i], tmpb[2 + i]
                if nb * KC + c >= 2:
                    prev = tmm[nb * KC + c - 2]
                    p.w("vector", "T", prev)
                    p.w("scalar", "T", prev)
                p.do("vector", lambda E, c=c, nb=nb, buf=buf: E.tensor_copy(
                    buf[:, :], xch[c][:, 1 + nb * 512: 1 + (nb + 1) * 512]), inc="V")
                p.do("scalar", lambda E, c=c, nb=nb, sqb=sqb: E.activation(
                    sqb[:, :], xch[c][:, 1 + nb * 512: 1 + (nb + 1) * 512], AF.Square), inc="S")
                p.w("tensor", "V", ctx.ctr["V"])
                p.w("tensor", "S", ctx.ctr["S"])
                p.do("tensor", lambda E, nb=nb, buf=buf, c=c: E.matmul(
                    ps[nb][0:1, :], ctx.ten["ones_col"][:, :], buf[:, :],
                    start=(c == 0), stop=(c == KC - 1)), inc="T")
                p.do("tensor", lambda E, nb=nb, sqb=sqb, c=c: E.matmul(
                    ps[4 + nb][0:1, :], ctx.ten["ones_col"][:, :], sqb[:, :],
                    start=(c == 0), stop=(c == KC - 1)), inc="T")
                tmm[nb * KC + c] = ctx.ctr["T"]
        tstats = ctx.ctr["T"]
        p.w("vector", "T", tstats)
        for nb in range(NB):
            p.do("vector", lambda E, nb=nb: E.tensor_scalar_mul(
                murow[0:1, nb * 512:(nb + 1) * 512], ps[nb][0:1, :], 1.0 / D), inc="V")
            p.do("vector", lambda E, nb=nb: E.tensor_scalar_mul(
                ssrow[0:1, nb * 512:(nb + 1) * 512], ps[4 + nb][0:1, :], 1.0 / D), inc="V")
        p.ss("vector")
        p.do("vector", lambda E: E.tensor_mul(tmprow[0:1, :], murow[0:1, :], murow[0:1, :]), inc="V")
        p.ss("vector")
        p.do("vector", lambda E: E.tensor_sub(ssrow[0:1, :], ssrow[0:1, :], tmprow[0:1, :]), inc="V")
        vvar = ctx.ctr["V"]
        p.w("scalar", "V", vvar)
        row_recip_sqrt(ctx, p, rrow[0:1, :], ssrow[0:1, :], LN_EPS, half=True)
        srow = ctx.ctr["S"]
        p.w("vector", "S", srow)
        p.do("vector", lambda E: E.scalar_tensor_tensor(
            brow[0:1, :], murow[0:1, :], -1.0, rrow[0:1, :], ALU.mult, ALU.mult), inc="V")
        vab = ctx.ctr["V"]

        if which == "ffn":
            p.do("scalar", lambda E: E.activation(bnd[:, 8:16], bsrc, AF.Square), inc="S")
            p.w("tensor", "S", ctx.ctr["S"])
            p.w("tensor", "V", vab)
            p.do("tensor", lambda E: E.matmul(ps[4][0:1, 0:KC], ctx.ten["ones_col_f32"][:, :],
                                              bsrc, start=True, stop=True), inc="T")
            p.do("tensor", lambda E: E.matmul(ps[5][0:1, 0:KC], ctx.ten["ones_col_f32"][:, :],
                                              bnd[:, 8:16], start=True, stop=True), inc="T")
            p.w("vector", "T", ctx.ctr["T"])
            p.do("vector", lambda E: E.reduce_sum(bnd[0:1, 0:1], ps[4][0:1, 0:KC], axis=AX.X), inc="V")
            p.do("vector", lambda E: E.reduce_sum(bnd[0:1, 1:2], ps[5][0:1, 0:KC], axis=AX.X), inc="V")
            p.ss("vector")
            p.do("vector", lambda E: E.tensor_scalar_mul(bnd[0:1, 0:1], bnd[0:1, 0:1], 1.0 / D), inc="V")
            p.do("vector", lambda E: E.tensor_scalar_mul(bnd[0:1, 1:2], bnd[0:1, 1:2], 1.0 / D), inc="V")
            p.ss("vector")
            p.do("vector", lambda E: E.tensor_mul(bnd[0:1, 2:3], bnd[0:1, 0:1], bnd[0:1, 0:1]), inc="V")
            p.ss("vector")
            p.do("vector", lambda E: E.tensor_sub(bnd[0:1, 2:3], bnd[0:1, 1:2], bnd[0:1, 2:3]), inc="V")
            vb = ctx.ctr["V"]
            p.w("scalar", "V", vb)
            row_recip_sqrt(ctx, p, bnd[0:1, 3:4], bnd[0:1, 2:3], LN_EPS, half=True)
            p.w("tensor", "S", ctx.ctr["S"])
            p.w("tensor", "V", ctx.ctr["V"])
            p.do("tensor", lambda E: E.matmul(ps[6][:, 0:1], ctx.ten["ones_row_f32"][:, :],
                                              bnd[0:1, 0:1], start=True, stop=True), inc="T")
            p.do("tensor", lambda E: E.matmul(ps[6][:, 1:2], ctx.ten["ones_row_f32"][:, :],
                                              bnd[0:1, 3:4], start=True, stop=True), inc="T")
            p.w("vector", "T", ctx.ctr["T"])
            p.do("vector", lambda E: E.tensor_copy(bnd[:, 4:6], ps[6][:, 0:2]), inc="V")
            p.ss("vector")
            if SNAP_MODE and which == "ffn":
                dbnd = nc.dram_tensor(f"snap_lnb{ctx.ctr['T']}", [128, 16], F32, kind="ExternalOutput")
                p.w("sync", "V", ctx.ctr["V"] + 2 + KC)
                p.do("sync", lambda E, d=dbnd: E.dma_start(out=d[:, :], in_=bnd[:, :]), inc="Y", n=16)
            p.do("vector", lambda E: E.tensor_scalar(
                tf32[0][:, 0:KC], bsrc, bnd[:, 4:5], bnd[:, 5:6],
                ALU.subtract, ALU.mult), inc="V")
            p.ss("vector")
            p.do("vector", lambda E: E.tensor_scalar_mul(
                tf32[0][:, 0:KC], tf32[0][:, 0:KC],
                sv[:, SV["coremask"]:SV["coremask"] + 1]), inc="V")
            p.ss("vector")
            p.w("vector", "V", ctx.ctr["V"])
            for c in range(KC):
                p.do("vector", lambda E, c=c: E.tensor_copy(xch[c][:, 0:1], tf32[0][:, c:c + 1]), inc="V")

        vbfin = ctx.ctr["V"]
        vap = {}
        for nb in range(NB):
            bA = ps[nb % 2]
            bB = ps[2 + nb % 2]
            p.w("tensor", "V", vbfin)
            if nb >= 2:
                p.w("tensor", "V", vap[nb - 2])
            p.do("tensor", lambda E, nb=nb, bA=bA: E.matmul(
                bA[:, :], ones_row[:, :], rrow[0:1, nb * 512:(nb + 1) * 512],
                start=True, stop=True), inc="T")
            p.do("tensor", lambda E, nb=nb, bB=bB: E.matmul(
                bB[:, :], ones_row[:, :], brow[0:1, nb * 512:(nb + 1) * 512],
                start=True, stop=True), inc="T")
            tb = ctx.ctr["T"]
            p.w("vector", "T", tb)
            for c in range(KC):
                t = tf32[c % 2]
                p.do("vector", lambda E, c=c, nb=nb, bA=bA, t=t: E.tensor_mul(
                    t[:, :], xch[c][:, 1 + nb * 512:1 + (nb + 1) * 512], bA[:, :]), inc="V")
                p.ss("vector")
                if which == "ffn":
                    p.do("vector", lambda E, c=c, nb=nb, bB=bB, t=t: E.tensor_add(
                        xch[c][:, 1 + nb * 512:1 + (nb + 1) * 512], t[:, :], bB[:, :]), inc="V")
                else:
                    p.do("vector", lambda E, c=c, nb=nb, bB=bB, t=t: E.tensor_add(
                        xhat[c][:, nb * 512:(nb + 1) * 512], t[:, :], bB[:, :]), inc="V")
            vap[nb] = ctx.ctr["V"]
        p.close()



def ph_tables(ctx, tabs, qscol8, qscol8s, kscol, kscols):
    nc, J = ctx.nc, ctx.J
    ps = ctx.psum
    sv, SV = ctx.ten["sv"], ctx.ten["SV"]
    PI = float(np.pi)
    import contextlib
    with contextlib.ExitStack() as st:
        wsin = st.enter_context(_sb(nc, "tb_sin", [128, 512], F32))
        wcos = st.enter_context(_sb(nc, "tb_cos", [128, 512], F32))
        wsc = st.enter_context(_sb(nc, "tb_scale", [128, 512], F32))
        wtmp = st.enter_context(_sb(nc, "tb_tmp", [128, 512], F32))
        wint = st.enter_context(_sb(nc, "tb_int", [128, 512], mybir.dt.int32))
        wkf = st.enter_context(_sb(nc, "tb_kf", [128, 512], F32))
        wd = st.enter_context(_sb(nc, "tb_d", [128, 512], F32))
        rowsb = st.enter_context(_sb(nc, "tb_rows", [1, 2 * T], F32))
        colsb = st.enter_context(_sb(nc, "tb_cols", [1, 512], F32))

        p = Phase(ctx, "tables")
        J.join(p)
        p.do("gpsimd", lambda E: E.iota(colsb[0:1, 0:128], [[0, 4], [1, 32]],
                                        channel_multiplier=0,
                                        allow_small_or_imprecise_dtypes=True), inc="G")
        p.do("gpsimd", lambda E: E.iota(rowsb[0:1, 0:T], [[1, T]],
                                        channel_multiplier=0,
                                        allow_small_or_imprecise_dtypes=True), inc="G")
        g0 = ctx.ctr["G"]
        p.w("vector", "G", g0)
        p.do("vector", lambda E: E.tensor_scalar_add(rowsb[0:1, 0:T], rowsb[0:1, 0:T],
                                                     sv[0:1, SV["toff"]:SV["toff"] + 1]), inc="V")
        p.do("vector", lambda E: E.tensor_scalar(rowsb[0:1, T:2 * T], rowsb[0:1, 0:T],
                                                 -(N_FULL // 2) * 1.0, 1.0 / 256.0,
                                                 ALU.add, ALU.mult), inc="V")
        p.do("vector", lambda E: E.tensor_scalar(colsb[0:1, 256:384], colsb[0:1, 0:128],
                                                 12.8, 1.0 / 44.8, ALU.add, ALU.mult), inc="V")
        vr = ctx.ctr["V"]
        p.w("scalar", "V", vr)
        p.do("scalar", lambda E: E.activation(colsb[0:1, 128:256], colsb[0:1, 0:128], AF.Exp,
                                              scale=-float(np.log(10000.0) / 32.0)), inc="S")
        p.do("scalar", lambda E: E.activation(colsb[0:1, 256:384], colsb[0:1, 256:384], AF.Ln), inc="S")
        scols = ctx.ctr["S"]
        p.w("tensor", "S", scols)

        slast = {}
        for nb in range(T // 512):
            sl = slice(nb * 512, (nb + 1) * 512)
            pph, ppw = ps[nb % 2], ps[2 + nb % 2]
            if nb >= 2:
                p.w("tensor", "V", slast[nb - 2][0])
                p.w("tensor", "S", slast[nb - 2][1])
            p.do("tensor", lambda E, sl=sl, pph=pph: E.matmul(
                pph[:, :], colsb[0:1, 128:256], rowsb[0:1, sl], start=True, stop=True), inc="T")
            p.do("tensor", lambda E, sl=sl, ppw=ppw: E.matmul(
                ppw[:, :], colsb[0:1, 256:384], rowsb[0:1, T + sl.start:T + sl.stop],
                start=True, stop=True), inc="T")
            tmm = ctx.ctr["T"]
            p.w("scalar", "T", tmm)
            if nb >= 1:
                p.w("scalar", "V", slast[nb - 1][0])
            p.do("scalar", lambda E, ppw=ppw: E.activation(wsc[:, :], ppw[:, :], AF.Exp), inc="S")
            p.w("vector", "T", tmm)
            def rr(shift_cycles, shift_rad, dst):
                p.w("vector", "S", ctx.ctr["S"])
                p.do("vector", lambda E, pph=pph, s=shift_cycles: E.tensor_scalar(
                    wtmp[:, :], pph[:, :], 1.0 / (2 * PI), 0.5 + s, ALU.mult, ALU.add), inc="V")
                p.ss("vector")
                p.do("vector", lambda E: E.tensor_copy(wint[:, :], wtmp[:, :]), inc="V")
                p.ss("vector")
                p.do("vector", lambda E: E.tensor_copy(wkf[:, :], wint[:, :]), inc="V")
                p.ss("vector")
                p.do("vector", lambda E: E.tensor_tensor(wd[:, :], wkf[:, :], wtmp[:, :],
                                                         ALU.is_gt), inc="V")
                p.ss("vector")
                p.do("vector", lambda E: E.tensor_sub(wkf[:, :], wkf[:, :], wd[:, :]), inc="V")
                p.ss("vector")
                p.do("vector", lambda E, pph=pph: E.scalar_tensor_tensor(
                    wtmp[:, :], wkf[:, :], -2 * PI, pph[:, :], ALU.mult, ALU.add), inc="V")
                p.ss("vector")
                p.do("vector", lambda E, sr=shift_rad: E.tensor_scalar(
                    wtmp[:, :], wtmp[:, :], sr, PI, ALU.add, ALU.min), inc="V")
                p.ss("vector")
                p.do("vector", lambda E: E.tensor_scalar_max(wtmp[:, :], wtmp[:, :], -PI), inc="V")
                vv = ctx.ctr["V"]
                p.w("scalar", "V", vv)
                p.do("scalar", lambda E, dst=dst: E.activation(dst[:, :], wtmp[:, :], AF.Sin), inc="S")
            rr(0.0, 0.0, wsin)
            rr(0.25, PI / 2, wcos)
            ssin = ctx.ctr["S"]
            p.w("vector", "S", ssin)
            p.do("vector", lambda E: E.tensor_mul(wtmp[:, :], wcos[:, :], wsc[:, :]), inc="V")
            p.ss("vector")
            p.do("vector", lambda E, sl=sl: E.tensor_scalar_mul(
                tabs["cs_q"][:, sl], wtmp[:, :], qscol8), inc="V")
            p.ss("vector")
            p.do("vector", lambda E: E.tensor_mul(wtmp[:, :], wsin[:, :], wsc[:, :]), inc="V")
            p.ss("vector")
            p.do("vector", lambda E, sl=sl: E.tensor_scalar_mul(
                tabs["ss_q"][:, sl], wtmp[:, :], qscol8s), inc="V")
            vq = ctx.ctr["V"]
            p.w("scalar", "V", vq)
            p.do("scalar", lambda E, ppw=ppw: E.activation(wsc[:, :], ppw[:, :], AF.Exp,
                                                           scale=-1.0), inc="S")
            sinv = ctx.ctr["S"]
            p.w("vector", "S", sinv)
            p.do("vector", lambda E: E.tensor_mul(wtmp[:, :], wcos[:, :], wsc[:, :]), inc="V")
            p.ss("vector")
            p.do("vector", lambda E, sl=sl: E.tensor_scalar_mul(
                tabs["cs_k"][:, sl], wtmp[:, :], kscol), inc="V")
            p.ss("vector")
            p.do("vector", lambda E: E.tensor_mul(wtmp[:, :], wsin[:, :], wsc[:, :]), inc="V")
            p.ss("vector")
            p.do("vector", lambda E, sl=sl: E.tensor_scalar_mul(
                tabs["ss_k"][:, sl], wtmp[:, :], kscols), inc="V")
            slast[nb] = (ctx.ctr["V"], sinv)
        p.close()



def coll_issue(ctx, p, pairs, cc_in, cc_out, dma_eng="sync", waits=()):
    for s, v in waits:
        p.w(dma_eng, s, v)
    for dst, srcap in pairs:
        p.do(dma_eng, lambda E, dst=dst, s=srcap: E.dma_start(
            out=dst, in_=s, allow_slow_non_contiguous=True), inc="CI", n=16)
    p.w("gpsimd", "CI", ctx.ctr["CI"])
    p.do("gpsimd", lambda E: E.collective_compute(
        "AllGather", ALU.bypass,
        replica_groups=[[0, 1], [2, 3], [4, 5], [6, 7]],
        ins=[cc_in.ap().opt()], outs=[cc_out.ap().opt()]), inc="CC")
    return ctx.ctr["CC"]


def coll_recv(ctx, p, ccval, pairs, dma_eng="sync"):
    p.w(dma_eng, "CC", ccval)
    for dst, srcap in pairs:
        p.do(dma_eng, lambda E, d=dst, s=srcap: E.dma_start(out=d, in_=s),
             inc="CR", n=16)
    return ctx.ctr["CR"]



def _wsrc(ctx, name, l, first_call):
    if first_call:
        return ctx.dram[f"{name}{l}"][:, :]
    return ctx.dram[f"stash_{name}"][l, :, :]


QC_TILES = {
    0: [(0, True), (1, True), (2, False), (3, False), (4, True), (5, True)],
    1: [(2, True), (3, True), (4, False), (5, False), (6, True), (7, True)],
}


def ph_ffn_qkv(ctx, l, first_call, fl):
    nc, J = ctx.nc, ctx.J
    ps = ctx.psum
    xch = ctx.ten["xch"]
    dram = ctx.dram
    import contextlib

    qf, kf, vT = fl["qf"], fl["kf"], fl["vT"]
    tabs = fl["tabs"]
    pmat, halfsel = ctx.ten["pmat"], ctx.ten["halfsel"]

    with contextlib.ExitStack() as st:
        xfb = [st.enter_context(_sb(nc, f"xfb{c}", [128, T + 1], BF16))
               for c in range(KC)]
        wbuf = [st.enter_context(_sb(nc, f"qw{i}", [128, KC * 256], BF16))
                for i in range(2)]
        sqb = [st.enter_context(_sb(nc, f"sqb{i}", [128, 512], BF16)) for i in range(2)]
        rinv = st.enter_context(_sb(nc, "rinv", [2, 4 * 512], F32))
        rtmp = [st.enter_context(_sb(nc, f"rtmp{i}", [128, 512], BF16)) for i in range(2)]

        p = Phase(ctx, f"qkv{fl['tag']}")
        J.join(p)
        if fl.get("snapb") is not None:
            for c in range(KC):
                p.do("sync", lambda E, c=c, d=fl["snapb"]: E.dma_start(
                    out=d[:, c:c + 1], in_=xch[c][:, 0:1],
                    allow_slow_non_contiguous=True), inc="Y", n=16)

        p.do("vector", lambda E: E.memset(
            vT[:, :].rearrange("p (t h d) -> p t h d", t=16, h=8)[:, :, :, 64:65], 1.0), inc="V")
        for c in range(KC):
            p.do("vector", lambda E, c=c: E.tensor_copy(xfb[c][:, :], xch[c][:, :]), inc="V")
        vcast = ctx.ctr["V"]

        wq = _wsrc(ctx, "wqkv", l, first_call)
        weng = "gpsimd" if first_call else "sync"
        wsem = "GD" if first_call else "Y"

        mmdone = {}
        stash_done = {}
        for g in range(6):
            buf = wbuf[g % 2]
            if g >= 2:
                p.w(weng, "T", mmdone[g - 2])
                if first_call:
                    p.w(weng, f"WS{g % 2}", stash_done[g - 2])
            wsm = f"WQ{g % 2}"
            p.do(weng, lambda E, g=g, buf=buf, wq=wq: E.dma_start(
                out=buf[:, :].rearrange("p (c m) -> p c m", c=KC),
                in_=wq.rearrange("(c p) m -> p c m", p=128)[:, :, 256 * g:256 * (g + 1)]),
                inc=wsm, n=16)
            ld = ctx.ctr[wsm]
            if first_call:
                p.w("sync", wsm, ld)
                p.do("sync", lambda E, g=g, buf=buf: E.dma_start(
                    out=dram["stash_wqkv"][l, :, :].rearrange("(c p) m -> p c m", p=128)[:, :, 256 * g:256 * (g + 1)],
                    in_=buf[:, :].rearrange("p (c m) -> p c m", c=KC)), inc=f"WS{g % 2}", n=16)
            p.w("tensor", wsm, ld)
            if g == 0:
                p.w("tensor", "V", vcast)
            if g < 4:
                for mt in range(2):
                    m = g * 2 + mt
                    dst = qf[m] if m < 4 else kf[m - 4]
                    for w in range(NWIN):
                        bid = 4 + (2 * (w % 2) + mt)
                        bank = ps[bid]
                        if bid in fl.setdefault("qkv_bdone", {}):
                            p.w("tensor", "V", fl["qkv_bdone"][bid])
                        for c in range(KC):
                            roff = (1 if c < 4 else 0) + w * 512
                            p.do("tensor", lambda E, c=c, buf=buf, bank=bank, mt=mt, roff=roff: E.matmul(
                                bank[:, :], buf[:, 256 * c + 128 * mt:256 * c + 128 * mt + 128],
                                xfb[c][:, roff:roff + 512],
                                start=(c == 0), stop=(c == KC - 1)), inc="T")
                        tm = ctx.ctr["T"]
                        p.w("vector", "T", tm)
                        p.do("vector", lambda E, dst=dst, w=w, bank=bank: E.tensor_copy(
                            dst[:, w * 512:(w + 1) * 512], bank[:, :]), inc="V")
                        fl["qkv_bdone"][bid] = ctx.ctr["V"]
            else:
                mmdone[g] = ctx.ctr["T"]
            mmdone[g] = ctx.ctr["T"]
            if first_call:
                stash_done[g] = ctx.ctr[f"WS{g % 2}"]

        vevac = {}
        for tt in range(16):
            bank = ps[tt % 2]
            if tt >= 2:
                p.w("tensor", "V", vevac[tt - 2])
            for half in range(2):
                buf = wbuf[half]
                for c in range(KC):
                    roff = (1 if c < 4 else 0) + tt * 128
                    p.do("tensor", lambda E, c=c, buf=buf, bank=bank, half=half, roff=roff: E.matmul(
                        bank[:, 256 * half:256 * (half + 1)],
                        xfb[c][:, roff:roff + 128],
                        buf[:, 256 * c:256 * (c + 1)],
                        start=(c == 0), stop=(c == KC - 1)), inc="T")
            tvv = ctx.ctr["T"]
            p.w("vector", "T", tvv)
            p.do("vector", lambda E, tt=tt, bank=bank: E.tensor_copy(
                vT[:, tt * 520:(tt + 1) * 520].rearrange("p (h d) -> p h d", h=8)[:, :, 0:64],
                bank[:, :].rearrange("p (h d) -> p h d", h=8)), inc="V")
            vevac[tt] = ctx.ctr["V"]

        def norm_rotary(side):
            src4 = qf if side == "q" else kf
            cs = tabs["cs_q" if side == "q" else "cs_k"]
            ss = tabs["ss_q" if side == "q" else "ss_k"]
            for w in range(NWIN):
                sl = slice(w * 512, (w + 1) * 512)
                p.w("tensor", "V", ctx.ctr["V"])
                p.w("tensor", "S", ctx.ctr["S"])
                p.w("scalar", "T", ctx.ctr["T"])
                stat_t = {}
                for c in range(4):
                    sq = sqb[c % 2]
                    if c >= 2:
                        p.w("scalar", "T", stat_t[c - 2])
                    p.do("scalar", lambda E, c=c, sq=sq, src4=src4, sl=sl: E.activation(
                        sq[:, :], src4[c][:, sl], AF.Square), inc="S")
                    p.w("tensor", "S", ctx.ctr["S"])
                    p.do("tensor", lambda E, c=c, sq=sq: E.matmul(
                        ps[c][0:2, :], ctx.ten["seg2"][:, :], sq[:, :],
                        start=True, stop=True), inc="T")
                    stat_t[c] = ctx.ctr["T"]
                tstat = ctx.ctr["T"]
                p.w("scalar", "T", tstat)
                for c in range(4):
                    p.do("scalar", lambda E, c=c: E.activation(
                        rinv[0:2, c * 512:(c + 1) * 512], ps[c][0:2, :], AF.Ln,
                        bias=1e-24), inc="S")
                p.ss("scalar")
                for c in range(4):
                    p.do("scalar", lambda E, c=c: E.activation(
                        rinv[0:2, c * 512:(c + 1) * 512], rinv[0:2, c * 512:(c + 1) * 512],
                        AF.Exp, scale=-0.5), inc="S")
                srinv = ctx.ctr["S"]
                bdn = fl.setdefault("nr_bdone", {})
                for c in range(4):
                    bbid, pbid = 4 + c % 2, 6 + c % 2
                    bb, pb = ps[bbid], ps[pbid]
                    if pbid in bdn:
                        p.w("tensor", "V", bdn[pbid])
                    p.do("tensor", lambda E, c=c, pb=pb, src4=src4, sl=sl: E.matmul(
                        pb[:, :], pmat[:, :], src4[c][:, sl],
                        start=True, stop=True), inc="T")
                    tp = ctx.ctr["T"]
                    if bbid in bdn:
                        p.w("tensor", "V", bdn[bbid])
                    p.w("tensor", "S", srinv)
                    p.do("tensor", lambda E, c=c, bb=bb: E.matmul(
                        bb[:, :], halfsel[:, :], rinv[0:2, c * 512:(c + 1) * 512],
                        start=True, stop=True), inc="T")
                    tb = ctx.ctr["T"]
                    p.do("vector", lambda E, c=c, src4=src4, sl=sl, cs=cs: E.tensor_mul(
                        rtmp[0][:, :], src4[c][:, sl], cs[:, sl]), inc="V")
                    p.w("vector", "T", tp)
                    p.do("vector", lambda E, c=c, pb=pb, sl=sl, ss=ss: E.tensor_mul(
                        rtmp[1][:, :], pb[:, :], ss[:, sl]), inc="V")
                    bdn[pbid] = ctx.ctr["V"]
                    p.do("vector", lambda E: E.tensor_add(
                        rtmp[0][:, :], rtmp[0][:, :], rtmp[1][:, :]), inc="V")
                    p.w("vector", "T", tb)
                    p.do("vector", lambda E, c=c, bb=bb, src4=src4, sl=sl: E.tensor_mul(
                        src4[c][:, sl], rtmp[0][:, :], bb[:, :]), inc="V")
                    bdn[bbid] = ctx.ctr["V"]

        if fl.get("raw_tap"):
            p.close()
            return
        norm_rotary("q")
        norm_rotary("k")

        pairs = []
        for c in range(4):
            pairs.append((dram["c2_in"][:, c * 512:(c + 1) * 512], kf[c][:, 3 * 512:4 * 512]))
        pairs.append((dram["c2_in"][:, 4 * 512:4 * 512 + 4 * 520], vT[:, 12 * 520:16 * 520]))
        fl["c2val"] = coll_issue(ctx, p, pairs, dram["c2_in"], dram["c2_out"],
                                 waits=[("V", ctx.ctr["V"])])
        p.close()


def ph_ffn_attn(ctx, l, first_call, fl):
    nc, J = ctx.nc, ctx.J
    ps = ctx.psum
    xch = ctx.ten["xch"]
    dram = ctx.dram
    m_top, m_top0, m_tril = ctx.ten["m_top"], ctx.ten["m_top0"], ctx.ten["m_tril"]
    import contextlib

    qf, kf, vT = fl["qf"], fl["kf"], fl["vT"]
    kh, vTh = fl["kh"], fl["vTh"]

    with contextlib.ExitStack() as st:
        wo_sb = [st.enter_context(_sb(nc, f"wosb{i}", [128, 1024], BF16))
                 for i in range(4)]
        attn_sb = st.enter_context(_sb(nc, "attnsb", [128, 2048], BF16))
        Ow = [st.enter_context(_sb(nc, f"Ow{i}", [128, 512], BF16)) for i in range(4)]
        rdrow = st.enter_context(_sb(nc, "rdrow", [1, 512], BF16))
        rdf32 = st.enter_context(_sb(nc, "rdf32", [1, 512], F32))
        onum = st.enter_context(_sb(nc, "onum", [64, 512], BF16))

        p = Phase(ctx, f"fattn{fl['tag']}")
        J.join(p)

        wsrcap = _wsrc(ctx, "wo", l, first_call)
        weng = "gpsimd" if first_call else "sync"
        wsem = "GD" if first_call else "Y"
        for ic in range(4):
            p.do(weng, lambda E, ic=ic, w=wsrcap: E.dma_start(
                out=wo_sb[ic][:, :], in_=w[ic * 128:(ic + 1) * 128, :]),
                inc="WO", n=16)
        ldwo = ctx.ctr["WO"]
        if first_call:
            p.w("sync", "WO", ldwo)
            for ic in range(4):
                p.do("sync", lambda E, ic=ic: E.dma_start(
                    out=dram["stash_wo"][l, :, :][ic * 128:(ic + 1) * 128, :],
                    in_=wo_sb[ic][:, :]), inc="Y", n=16)
        p.w("tensor", "WO", ldwo)

        c2done = False
        sc_done = {}
        mk_done = {}
        odone = {}
        worder = fl["dbg_w1"] if fl.get("dbg_w1") else [1, 2, 3, 0]
        for wi, w in enumerate(worder):
            if w == 0 and not c2done:
                rp = [(kh[c][:, :], dram["c2_out"][0:128, c * 512:(c + 1) * 512]) for c in range(4)]
                rp.append((vTh[:, :], dram["c2_out"][0:128, 4 * 512:4 * 512 + 4 * 520]))
                rv = coll_recv(ctx, p, fl["c2val"], rp)
                p.w("tensor", "CR", rv)
                p.w("vector", "CR", rv)
                c2done = True
            for qc in range(2):
                qsl = slice(w * 512 + qc * 256, w * 512 + qc * 256 + 256)
                tiles = [(kt, (need or (w == 0 and kt < 4))) for kt, need in QC_TILES[qc]]
                koff = 0 if qc == 0 else 2
                p.w("tensor", "S", ctx.ctr["S"])
                p.w("tensor", "V", ctx.ctr["V"])
                vps = ctx.ctr["S"]
                p.do("vector", lambda E: E.memset(ps[7][0:64, :], 0.0), inc="V")
                rz = {}
                muld = {}
                vz0 = ctx.ctr["V"]
                for h in range(H):
                    hc, hh = h // 2, h % 2
                    for (kt, _need) in tiles:
                        bank = ps[kt - koff]
                        if kt < 4:
                            if w == 0:
                                lhs = kh[hc][hh * 64:hh * 64 + 64, kt * 128:(kt + 1) * 128]
                            else:
                                lhs = kf[hc][hh * 64:hh * 64 + 64,
                                             (w - 1) * 512 + kt * 128:(w - 1) * 512 + (kt + 1) * 128]
                        else:
                            lhs = kf[hc][hh * 64:hh * 64 + 64,
                                         w * 512 + (kt - 4) * 128:w * 512 + (kt - 3) * 128]
                        if _need:
                            if kt < 4:
                                msk = m_top0 if w == 0 else m_top
                                mslice = msk[:, kt * 512 + qc * 256:kt * 512 + qc * 256 + 256]
                            else:
                                mslice = m_tril[:, (kt - 4) * 512 + qc * 256:(kt - 4) * 512 + qc * 256 + 256]
                            p.w("vector", "S", sc_done.get((kt, h - 1), vps))
                            p.do("vector", lambda E, bank=bank, mslice=mslice: E.tensor_copy(
                                bank[:, 0:256], mslice), inc="V")
                            mk = ctx.ctr["V"]
                            p.w("tensor", "V", mk)
                            p.do("tensor", lambda E, lhs=lhs, bank=bank, qsl=qsl, hc=hc, hh=hh: E.matmul(
                                bank[:, 0:256], lhs,
                                qf[hc][hh * 64:hh * 64 + 64, qsl],
                                start=False, stop=True, skip_group_check=True), inc="T")
                        else:
                            if h > 0:
                                p.w("tensor", "S", sc_done[(kt, h - 1)])
                            p.do("tensor", lambda E, lhs=lhs, bank=bank, qsl=qsl, hc=hc, hh=hh: E.matmul(
                                bank[:, 0:256], lhs,
                                qf[hc][hh * 64:hh * 64 + 64, qsl],
                                start=True, stop=True), inc="T")
                        tsc = ctx.ctr["T"]
                        p.w("scalar", "T", tsc)
                        p.do("scalar", lambda E, bank=bank, kt=kt: E.activation(
                            attn_sb[:, kt * 256:(kt + 1) * 256], bank[:, 0:256],
                            AF.Exp, bias=-8.0), inc="S")
                        sc_done[(kt, h)] = ctx.ctr["S"]
                    sexp = ctx.ctr["S"]
                    p.w("tensor", "S", sexp)
                    oh = (h % 2) * 256
                    obr = ps[6][0:65, oh:oh + 256]
                    rbr = ps[7][0:64, oh:oh + 256]
                    if h >= 2:
                        p.w("tensor", "S", odone[h - 2][0])
                        p.w("tensor", "V", rz[h - 2])
                    else:
                        p.w("tensor", "V", vz0)
                    for i, (kt, _need) in enumerate(tiles):
                        if w == 0 and kt < 4:
                            vtile = vTh[:, kt * 520 + h * 65:kt * 520 + h * 65 + 65]
                        elif kt < 4:
                            tt = (w - 1) * 4 + kt
                            vtile = vT[:, tt * 520 + h * 65:tt * 520 + h * 65 + 65]
                        else:
                            tt = w * 4 + (kt - 4)
                            vtile = vT[:, tt * 520 + h * 65:tt * 520 + h * 65 + 65]
                        p.do("tensor", lambda E, vtile=vtile, obr=obr, kt=kt, i=i, n=len(tiles): E.matmul(
                            obr, vtile, attn_sb[:, kt * 256:(kt + 1) * 256],
                            start=(i == 0), stop=(i == n - 1)), inc="T")
                    tav = ctx.ctr["T"]
                    p.w("scalar", "T", tav)
                    if h >= 2:
                        p.w("scalar", "V", muld[h - 2])
                    p.do("scalar", lambda E, obr=obr: E.activation(
                        rdf32[0:1, 0:256], obr[64:65, :], AF.Ln, bias=1e-30), inc="S")
                    p.ss("scalar")
                    p.do("scalar", lambda E: E.activation(
                        rdrow[0:1, 0:256], rdf32[0:1, 0:256], AF.Exp, scale=-1.0), inc="S")
                    p.do("scalar", lambda E, obr=obr, oh=oh: E.copy(
                        onum[0:64, oh:oh + 256], obr[0:64, :]), inc="S")
                    snum = ctx.ctr["S"]
                    p.w("tensor", "S", snum)
                    p.do("tensor", lambda E, rbr=rbr: E.matmul(
                        rbr, ctx.ten["ones_row"][:, 0:64], rdrow[0:1, 0:256],
                        start=False, stop=True, skip_group_check=True), inc="T")
                    trb = ctx.ctr["T"]
                    p.w("vector", "T", trb)
                    p.do("vector", lambda E, rbr=rbr, hc=hc, hh=hh, qc=qc, oh=oh: E.tensor_mul(
                        Ow[hc][hh * 64:hh * 64 + 64, qc * 256:qc * 256 + 256],
                        onum[0:64, oh:oh + 256], rbr), inc="V")
                    p.do("vector", lambda E, rbr=rbr: E.memset(rbr, 0.0), inc="V")
                    rz[h] = ctx.ctr["V"]
                    muld[h] = ctx.ctr["V"]
                    odone[h] = (snum, ctx.ctr["V"])
            vdiv = ctx.ctr["V"]
            p.w("tensor", "V", vdiv)
            p.w("tensor", "S", ctx.ctr["S"])
            evd = {}
            for m in range(8):
                bank = ps[m % 4]
                if m >= 4:
                    p.w("tensor", "V", evd[m - 4])
                for ic in range(4):
                    p.do("tensor", lambda E, m=m, ic=ic, bank=bank: E.matmul(
                        bank[:, :], wo_sb[ic][:, m * 128:(m + 1) * 128],
                        Ow[ic][:, :],
                        start=(ic == 0), stop=(ic == 3)), inc="T")
                two = ctx.ctr["T"]
                p.w("vector", "T", two)
                p.do("vector", lambda E, m=m, bank=bank, w=w: E.tensor_add(
                    xch[m][:, 1 + w * 512:1 + (w + 1) * 512],
                    xch[m][:, 1 + w * 512:1 + (w + 1) * 512],
                    bank[:, :]), inc="V")
                evd[m] = ctx.ctr["V"]
            p.w("tensor", "V", ctx.ctr["V"])
            if w == 3 and not fl.get("dbg_w1"):
                pairs = [(dram["c1_in"][:, c:c + 1], xch[c][:, T:T + 1]) for c in range(KC)]
                fl["c3val"] = coll_issue(ctx, p, pairs, dram["c1_in"], dram["c1_out"],
                                         waits=[("V", ctx.ctr["V"])])
        if fl.get("dbg_w1"):
            douts = {"dbg_ow": nc.dram_tensor("dbg_ow", [4 * 128, 512], BF16, kind="ExternalOutput"),
                     "dbg_asb": nc.dram_tensor("dbg_asb", [128, 2048], BF16, kind="ExternalOutput")}
            p.w("sync", "V", ctx.ctr["V"])
            p.w("sync", "S", ctx.ctr["S"])
            for ic in range(4):
                p.do("sync", lambda E, ic=ic: E.dma_start(
                    out=douts["dbg_ow"][ic * 128:(ic + 1) * 128, :], in_=Ow[ic][:, :]), inc="Y", n=16)
            p.do("sync", lambda E: E.dma_start(out=douts["dbg_asb"][:, :], in_=attn_sb[:, :]), inc="Y", n=16)
            p.w("sync", "Y", ctx.ctr["Y"])
            p.w("sync", "CC", ctx.ctr["CC"])
        p.close()


def ph_ffn_mlp(ctx, l, first_call, fl):
    nc, J = ctx.nc, ctx.J
    ps = ctx.psum
    xch = ctx.ten["xch"]
    sv, SV = ctx.ten["sv"], ctx.ten["SV"]
    dram = ctx.dram
    ones_col = ctx.ten["ones_col"]
    ones_row_f32 = ctx.ten["ones_row_f32"]
    import contextlib

    with contextlib.ExitStack() as st:
        xhat = [st.enter_context(_sb(nc, f"xh2_{c}", [128, T], BF16)) for c in range(KC)]
        hbuf = st.enter_context(_sb(nc, "hbuf", [128, NPAIR * 1024], BF16))
        w1b = [st.enter_context(_sb(nc, f"w1b{i}", [128, KC * 256], BF16)) for i in range(2)]
        w2b = [st.enter_context(_sb(nc, f"w2b{i}", [128, NPAIR * 128], BF16)) for i in range(2)]
        gelub = [st.enter_context(_sb(nc, f"gel{i}", [128, 512], BF16)) for i in range(2)]
        tmpb = [st.enter_context(_sb(nc, f"mt{i}", [128, 512], BF16)) for i in range(4)]
        murow = st.enter_context(_sb(nc, "m_mu", [1, T], F32))
        ssrow = st.enter_context(_sb(nc, "m_ss", [1, T], F32))
        tmprow = st.enter_context(_sb(nc, "m_tr", [1, T], F32))
        rrow = ssrow
        brow = murow
        tf32 = [st.enter_context(_sb(nc, f"mf{i}", [128, 512], F32)) for i in range(2)]
        bndb = st.enter_context(_sb(nc, "m_bnd", [128, KC], F32))
        b1c = st.enter_context(_sb(nc, "b1c", [128, 2 * NPAIR], F32))

        p = Phase(ctx, f"mlp{fl['tag']}")
        J.join(p)

        rv = coll_recv(ctx, p, fl["c3val"], [(bndb[:, :], dram["c1_out"][0:128, :])])
        p.w("vector", "CR", rv)
        for c in range(KC):
            p.do("vector", lambda E, c=c: E.tensor_scalar_mul(
                xch[c][:, 0:1], bndb[:, c:c + 1],
                sv[:, SV["coremask"]:SV["coremask"] + 1]), inc="V")
        vbnd = ctx.ctr["V"]

        p.do("sync", lambda E: E.dma_start(out=b1c[:, :], in_=dram[f"b1p{l}"][:, :]), inc="WB", n=16)
        ldb1 = ctx.ctr["WB"]

        NB = T // 512
        tmm = {}
        p.w("scalar", "V", vbnd)
        p.w("vector", "V", vbnd)
        for nb in range(NB):
            for c in range(KC):
                i = (nb * KC + c) % 2
                buf, sq = tmpb[i], tmpb[2 + i]
                if nb * KC + c >= 2:
                    prev = tmm[nb * KC + c - 2]
                    p.w("vector", "T", prev)
                    p.w("scalar", "T", prev)
                roff = (1 if c < 4 else 0) + nb * 512
                p.do("vector", lambda E, c=c, buf=buf, roff=roff: E.tensor_copy(
                    buf[:, :], xch[c][:, roff:roff + 512]), inc="V")
                p.do("scalar", lambda E, c=c, sq=sq, roff=roff: E.activation(
                    sq[:, :], xch[c][:, roff:roff + 512], AF.Square), inc="S")
                p.w("tensor", "V", ctx.ctr["V"])
                p.w("tensor", "S", ctx.ctr["S"])
                p.do("tensor", lambda E, nb=nb, buf=buf, c=c: E.matmul(
                    ps[nb][0:1, :], ones_col[:, :], buf[:, :],
                    start=(c == 0), stop=(c == KC - 1)), inc="T")
                p.do("tensor", lambda E, nb=nb, sq=sq, c=c: E.matmul(
                    ps[4 + nb][0:1, :], ones_col[:, :], sq[:, :],
                    start=(c == 0), stop=(c == KC - 1)), inc="T")
                tmm[nb * KC + c] = ctx.ctr["T"]
        tstats = ctx.ctr["T"]
        p.w("vector", "T", tstats)
        for nb in range(NB):
            p.do("vector", lambda E, nb=nb: E.tensor_scalar_mul(
                murow[0:1, nb * 512:(nb + 1) * 512], ps[nb][0:1, :], 1.0 / D), inc="V")
            p.do("vector", lambda E, nb=nb: E.tensor_scalar_mul(
                ssrow[0:1, nb * 512:(nb + 1) * 512], ps[4 + nb][0:1, :], 1.0 / D), inc="V")
        p.do("vector", lambda E: E.tensor_mul(tmprow[0:1, :], murow[0:1, :], murow[0:1, :]), inc="V")
        p.do("vector", lambda E: E.tensor_sub(ssrow[0:1, :], ssrow[0:1, :], tmprow[0:1, :]), inc="V")
        vvar = ctx.ctr["V"]
        p.w("scalar", "V", vvar)
        row_recip_sqrt(ctx, p, rrow[0:1, :], ssrow[0:1, :], LN_EPS, half=True)
        srow = ctx.ctr["S"]
        p.w("vector", "S", srow)
        p.do("vector", lambda E: E.scalar_tensor_tensor(
            brow[0:1, :], murow[0:1, :], -1.0, rrow[0:1, :], ALU.mult, ALU.mult), inc="V")
        vab = ctx.ctr["V"]

        vap = {}
        for nb in range(NB):
            bA, bB = ps[nb % 2], ps[2 + nb % 2]
            p.w("tensor", "V", vab)
            if nb >= 2:
                p.w("tensor", "V", vap[nb - 2])
            p.do("tensor", lambda E, nb=nb, bA=bA: E.matmul(
                bA[:, :], ones_row_f32[:, :], rrow[0:1, nb * 512:(nb + 1) * 512],
                start=True, stop=True), inc="T")
            p.do("tensor", lambda E, nb=nb, bB=bB: E.matmul(
                bB[:, :], ones_row_f32[:, :], brow[0:1, nb * 512:(nb + 1) * 512],
                start=True, stop=True), inc="T")
            tb = ctx.ctr["T"]
            p.w("vector", "T", tb)
            for c in range(KC):
                t = tf32[c % 2]
                roff = (1 if c < 4 else 0) + nb * 512
                p.do("vector", lambda E, c=c, bA=bA, t=t, roff=roff: E.tensor_mul(
                    t[:, :], xch[c][:, roff:roff + 512], bA[:, :]), inc="V")
                p.ss("vector")
                p.do("vector", lambda E, c=c, bB=bB, t=t: E.tensor_add(
                    t[:, :], t[:, :], bB[:, :]), inc="V")
                p.ss("vector")
                p.do("vector", lambda E, c=c, nb=nb, t=t, l=l: E.tensor_scalar(
                    xhat[c][:, nb * 512:(nb + 1) * 512], t[:, :],
                    sv[:, SV[f"lng{l}"] + c:SV[f"lng{l}"] + c + 1],
                    sv[:, SV[f"lnb{l}"] + c:SV[f"lnb{l}"] + c + 1],
                    ALU.mult, ALU.add), inc="V")
            vap[nb] = ctx.ctr["V"]
        p.w("tensor", "V", ctx.ctr["V"])
        p.w("vector", "WB", ldb1)
        p.w("scalar", "WB", ldb1)
        if fl.get("mlpdbg"):
            dxh = nc.dram_tensor("dbg_xhat", [KC * 128, T], BF16, kind="ExternalOutput")
            dbn = nc.dram_tensor("dbg_bnd", [128, KC], F32, kind="ExternalOutput")
            drw = nc.dram_tensor("dbg_rows", [2, T], F32, kind="ExternalOutput")
            dx0 = nc.dram_tensor("dbg_x0", [128, KC], F32, kind="ExternalOutput")
            p.w("sync", "V", ctx.ctr["V"])
            for c in range(KC):
                p.do("sync", lambda E, c=c: E.dma_start(
                    out=dxh[c * 128:(c + 1) * 128, :], in_=xhat[c][:, :]), inc="Y", n=16)
                p.do("sync", lambda E, c=c: E.dma_start(
                    out=dx0[:, c:c + 1], in_=xch[c][:, 0:1],
                    allow_slow_non_contiguous=True), inc="Y", n=16)
            p.do("sync", lambda E: E.dma_start(out=dbn[:, :], in_=bndb[:, :]), inc="Y", n=16)
            p.do("sync", lambda E: E.dma_start(out=drw[0:1, :], in_=murow[0:1, :]), inc="Y", n=16)
            p.do("sync", lambda E: E.dma_start(out=drw[1:2, :], in_=ssrow[0:1, :]), inc="Y", n=16)

        wsrc1 = _wsrc(ctx, "w1p", l, first_call)
        wsrc2 = _wsrc(ctx, "w2", l, first_call)
        weng = "gpsimd" if first_call else "sync"
        wsem = "GD" if first_call else "Y"

        bank_user = {b: (0, vvar) for b in range(4, 8)}
        gel_user = {}
        for half in (1, 0):
            ta = half * 1024
            pairdone = {}
            w1stash = {}
            for pr in range(NPAIR):
                off, wp = _pair_off(pr)
                wcols = 2 * wp
                buf = w1b[pr % 2]
                if pr >= 2:
                    p.w(weng, "T", pairdone[pr - 2])
                    if first_call and half == 1:
                        p.w(weng, f"WS1{pr % 2}", w1stash[pr - 2])
                wsm = f"W1{pr % 2}"
                p.do(weng, lambda E, off=off, wcols=wcols, buf=buf, w=wsrc1: E.dma_start(
                    out=buf[:, 0:KC * wcols].rearrange("p (c m) -> p c m", m=wcols),
                    in_=w.rearrange("(c p) m -> p c m", p=128)[:, :, off:off + wcols]),
                    inc=wsm, n=16)
                ld = ctx.ctr[wsm]
                if first_call and half == 1:
                    p.w("sync", wsm, ld)
                    p.do("sync", lambda E, off=off, wcols=wcols, buf=buf: E.dma_start(
                        out=dram["stash_w1p"][l, :, :].rearrange("(c p) m -> p c m", p=128)[:, :, off:off + wcols],
                        in_=buf[:, 0:KC * wcols].rearrange("p (c m) -> p c m", m=wcols)),
                        inc=f"WS1{pr % 2}", n=16)
                p.w("tensor", wsm, ld)
                for nb in range(2):
                    bidx = (pr % 2) * 2 + nb * 4
                    ba = ps[bidx]
                    bg = ps[bidx + 1]
                    if bidx in bank_user:
                        p.w("tensor", "S", bank_user[bidx][0])
                        p.w("tensor", "V", bank_user[bidx][1])
                    tsl = slice(ta + nb * 512, ta + nb * 512 + 512)
                    for c in range(KC):
                        p.do("tensor", lambda E, c=c, buf=buf, ba=ba, wcols=wcols, wp=wp, tsl=tsl: E.matmul(
                            ba[0:wp, 0:512], buf[:, c * wcols:c * wcols + wp],
                            xhat[c][:, tsl], start=(c == 0), stop=(c == KC - 1)), inc="T")
                        p.do("tensor", lambda E, c=c, buf=buf, bg=bg, wcols=wcols, wp=wp, tsl=tsl: E.matmul(
                            bg[0:wp, 0:512], buf[:, c * wcols + wp:c * wcols + 2 * wp],
                            xhat[c][:, tsl], start=(c == 0), stop=(c == KC - 1)), inc="T")
                    tg = ctx.ctr["T"]
                    gb = gelub[nb]
                    p.w("scalar", "T", tg)
                    if nb in gel_user:
                        p.w("scalar", "V", gel_user[nb])
                    p.do("scalar", lambda E, bg=bg, gb=gb, pr=pr, wp=wp: E.activation(
                        gb[0:wp, :], bg[0:wp, 0:512], AF.Gelu,
                        bias=b1c[0:wp, 2 * pr + 1:2 * pr + 2]), inc="S")
                    sg = ctx.ctr["S"]
                    p.w("vector", "S", sg)
                    p.do("vector", lambda E, ba=ba, gb=gb, pr=pr, nb=nb, wp=wp: E.scalar_tensor_tensor(
                        hbuf[0:wp, pr * 1024 + nb * 512:pr * 1024 + nb * 512 + 512],
                        ba[0:wp, 0:512], b1c[0:wp, 2 * pr:2 * pr + 1], gb[0:wp, :],
                        ALU.add, ALU.mult), inc="V")
                    gel_user[nb] = ctx.ctr["V"]
                    bank_user[bidx] = (sg, ctx.ctr["V"])
                pairdone[pr] = ctx.ctr["T"]
                if first_call and half == 1:
                    w1stash[pr] = ctx.ctr[f"WS1{pr % 2}"]
            vh1 = ctx.ctr["V"]
            p.w("tensor", "V", vh1)
            p.w("tensor", "S", ctx.ctr["S"])
            mdone = {}
            w2bank = {}
            w2stash = {}
            for m in range(8):
                buf = w2b[m % 2]
                wsm = f"W2{m % 2}"
                if m >= 2:
                    p.w(weng, "T", mdone[m - 2])
                    if first_call and half == 1:
                        p.w(weng, f"WS2{m % 2}", w2stash[m - 2])
                p.do(weng, lambda E, m=m, buf=buf, w=wsrc2: E.dma_start(
                    out=buf[:, 0:21 * 128].rearrange("p (k c) -> p k c", c=128),
                    in_=w[0:21 * 128, :].rearrange("(k p) m -> p k m", p=128)[:, :, m * 128:(m + 1) * 128]),
                    inc=wsm, n=16)
                p.do(weng, lambda E, m=m, buf=buf, w=wsrc2: E.dma_start(
                    out=buf[0:42, 21 * 128:22 * 128],
                    in_=w[21 * 128:21 * 128 + 42, m * 128:(m + 1) * 128]),
                    inc=wsm, n=16)
                ld = ctx.ctr[wsm]
                if first_call and half == 1:
                    p.w("sync", wsm, ld)
                    p.do("sync", lambda E, m=m, buf=buf: E.dma_start(
                        out=dram["stash_w2"][l, :, :][0:21 * 128, :].rearrange("(k p) m -> p k m", p=128)[:, :, m * 128:(m + 1) * 128],
                        in_=buf[:, 0:21 * 128].rearrange("p (k c) -> p k c", c=128)), inc=f"WS2{m % 2}", n=16)
                    p.do("sync", lambda E, m=m, buf=buf: E.dma_start(
                        out=dram["stash_w2"][l, :, :][21 * 128:21 * 128 + 42, m * 128:(m + 1) * 128],
                        in_=buf[0:42, 21 * 128:22 * 128]), inc=f"WS2{m % 2}", n=16)
                p.w("tensor", wsm, ld)
                for nb in range(2):
                    bidx2 = 4 + (m * 2 + nb) % 4
                    bank = ps[bidx2]
                    if bidx2 in w2bank:
                        p.w("tensor", "V", w2bank[bidx2])
                    tsl = slice(ta + nb * 512, ta + nb * 512 + 512)
                    for k in range(NPAIR):
                        kw = 128 if k < 21 else 42
                        p.do("tensor", lambda E, k=k, kw=kw, buf=buf, bank=bank, nb=nb, ta=ta: E.matmul(
                            bank[:, :], buf[0:kw, k * 128:k * 128 + 128],
                            hbuf[0:kw, k * 1024 + nb * 512:k * 1024 + nb * 512 + 512],
                            start=(k == 0), stop=(k == NPAIR - 1)), inc="T")
                    tw = ctx.ctr["T"]
                    p.w("vector", "T", tw)
                    p.do("vector", lambda E, m=m, bank=bank, tsl=tsl, l=l: E.scalar_tensor_tensor(
                        xch[m][:, 1 + tsl.start:1 + tsl.stop],
                        bank[:, :], sv[:, SV[f"b2_{l}"] + m:SV[f"b2_{l}"] + m + 1],
                        xch[m][:, 1 + tsl.start:1 + tsl.stop],
                        ALU.add, ALU.add), inc="V")
                    w2bank[bidx2] = ctx.ctr["V"]
                    bank_user[bidx2] = (0, ctx.ctr["V"])
                mdone[m] = ctx.ctr["T"]
                if first_call and half == 1:
                    w2stash[m] = ctx.ctr[f"WS2{m % 2}"]
            if half == 1:
                pairs = [(dram["c1_in"][:, c:c + 1], xch[c][:, T:T + 1]) for c in range(KC)]
                fl["c1val"] = coll_issue(ctx, p, pairs, dram["c1_in"], dram["c1_out"],
                                         waits=[("V", ctx.ctr["V"])])
        p.close()


def ph_layer_prep(ctx, l, cols):
    nc, J = ctx.nc, ctx.J
    sv, SV = ctx.ten["sv"], ctx.ten["SV"]
    signcol = ctx.ten["signcol"]
    import contextlib
    with contextlib.ExitStack() as st:
        tmp = st.enter_context(_sb(nc, "lp_tmp", [128, 2], F32))
        p = Phase(ctx, f"lprep{l}")
        J.join(p)
        for nm, dst in (("qs", cols["q"]), ("ks", cols["k"])):
            dr = ctx.dram[f"{nm}{l}"]
            p.do("gpsimd", lambda E, d=dst, dr=dr: E.dma_start(out=d, in_=dr[:, :]), inc="LP", n=16)
        g = ctx.ctr["LP"]
        p.w("vector", "LP", g)
        for nm in ("q", "k"):
            src_c, dst_c = cols[nm], cols[nm + "p"]
            for s4 in range(4):
                o = 32 if s4 % 2 == 0 else -32
                p.do("vector", lambda E, s=src_c, d=dst_c, s4=s4, o=o: E.tensor_copy(
                    d[32 * s4:32 * (s4 + 1), :], s[32 * s4 + o:32 * (s4 + 1) + o, :]), inc="V")
        p.ss("vector")
        p.do("vector", lambda E: E.tensor_scalar_mul(cols["q"], cols["q"], 8.0), inc="V")
        p.do("vector", lambda E: E.tensor_scalar(cols["qp"], cols["qp"], 8.0, signcol[:, :],
                                                 ALU.mult, ALU.mult), inc="V")
        p.do("vector", lambda E: E.tensor_scalar_mul(cols["kp"], cols["kp"], signcol[:, :]), inc="V")
        p.ss("vector")
        p.close()


def ph_ffn_call(ctx, first_call, upto=None, raw_tap=False, dbg_w1=False, mlpdbg=False):
    nc = ctx.nc
    import contextlib
    c1val = None
    for l in (0, 1):
        tag = f"_f{1 if first_call else 2}l{l}"
        with contextlib.ExitStack() as ab:
            fl = dict(tag=tag, raw_tap=raw_tap, dbg_w1=dbg_w1, mlpdbg=mlpdbg)
            fl["qf"] = [ab.enter_context(_sb(nc, f"qf{c}", [128, T], BF16))
                        for c in range(4)]
            fl["kf"] = [ab.enter_context(_sb(nc, f"kf{c}", [128, T], BF16))
                        for c in range(4)]
            fl["vT"] = ab.enter_context(_sb(nc, "vTb", [128, 16 * 520], BF16))
            fl["kh"] = [ab.enter_context(_sb(nc, f"kh{c}", [128, 512], BF16))
                        for c in range(4)]
            fl["vTh"] = ab.enter_context(_sb(nc, "vTh", [128, 4 * 520], BF16))
            if SNAP_MODE:
                fl["snapb"] = nc.dram_tensor(f"snap_b{tag}", [128, KC], F32, kind="ExternalOutput")
            with contextlib.ExitStack() as tbs:
                tabs = {nm: tbs.enter_context(_sb(nc, f"tab_{nm}", [128, T], BF16))
                        for nm in ("cs_q", "ss_q", "cs_k", "ss_k")}
                colsb = tbs.enter_context(_sb(nc, "lp_cols", [128, 4], F32))
                cols = dict(q=colsb[:, 0:1], qp=colsb[:, 1:2], k=colsb[:, 2:3], kp=colsb[:, 3:4])
                fl["tabs"] = tabs
                ph_layer_prep(ctx, l, cols)
                if upto == f"prep{tag}":
                    return None
                ph_tables(ctx, tabs, cols["q"], cols["qp"], cols["k"], cols["kp"])
                if upto == f"tbl{tag}":
                    return dict(tabs=tabs, dump_tabs=True)
                if l == 1:
                    ph_boundary_recv(ctx, c1val)
                ph_ffn_qkv(ctx, l, first_call, fl)
                if upto == f"qkv{tag}":
                    return fl
                if SNAP_MODE:
                    dq = nc.dram_tensor(f"snap_qk{tag}", [8 * 128, T], BF16, kind="ExternalOutput")
                    _unused = fl.get("snapb")
                    pp = Phase(ctx, f"snapqk{tag}")
                    ctx.J.join(pp)
                    for c in range(4):
                        pp.do("sync", lambda E, c=c: E.dma_start(
                            out=dq[c * 128:(c + 1) * 128, :], in_=fl["qf"][c][:, :]), inc="Y", n=16)
                        pp.do("sync", lambda E, c=c: E.dma_start(
                            out=dq[(4 + c) * 128:(5 + c) * 128, :], in_=fl["kf"][c][:, :]), inc="Y", n=16)
                    pp.w("sync", "Y", ctx.ctr["Y"])
                    pp.close()
            ph_ffn_attn(ctx, l, first_call, fl)
            if upto == f"attn{tag}":
                return fl
            if SNAP_MODE:
                ph_snapshot(ctx, f"attn{tag}")
        ph_ffn_mlp(ctx, l, first_call, fl)
        if SNAP_MODE:
            ph_snapshot(ctx, f"mlp{tag}")
        c1val = fl.get("c1val")
        if upto == f"mlp{tag}":
            return fl
    return None


def ph_ffn_combine(ctx, src_dram):
    nc, J = ctx.nc, ctx.J
    xch = ctx.ten["xch"]
    import contextlib
    with contextlib.ExitStack() as st:
        rb = [st.enter_context(_sb(nc, f"cmb{i}", [128, 512], F32)) for i in range(4)]
        p = Phase(ctx, "combine")
        ctx.J.join(p)
        lds = {}
        for i, (c, nb) in enumerate([(c, nb) for c in range(KC) for nb in range(T // 512)]):
            buf = rb[i % 4]
            xs = f"XR{i % 4}"
            if i >= 4:
                p.w("sync", "V", lds[i - 4])
            p.do("sync", lambda E, c=c, nb=nb, buf=buf: E.dma_start(
                out=buf[:, :], in_=src_dram[128 * c:128 * (c + 1), nb * 512:(nb + 1) * 512]),
                inc=xs, n=16)
            p.w("vector", xs, ctx.ctr[xs])
            p.do("vector", lambda E, c=c, nb=nb, buf=buf: E.scalar_tensor_tensor(
                xch[c][:, 1 + nb * 512:1 + (nb + 1) * 512],
                xch[c][:, 1 + nb * 512:1 + (nb + 1) * 512], 0.5, buf[:, :],
                ALU.mult, ALU.add), inc="V")
            lds[i] = ctx.ctr["V"]
        p.close()


def ph_spill(ctx):
    p = Phase(ctx, "spill")
    ctx.J.join(p)
    xch = ctx.ten["xch"]
    for c in range(KC):
        p.do("sync", lambda E, c=c: E.dma_start(
            out=ctx.dram["x_spill"][128 * c:128 * (c + 1), :], in_=xch[c][:, 1:T + 1]),
            inc="Y", n=16)
    p.w("sync", "Y", ctx.ctr["Y"])
    p.close()



def ph_attn(ctx, upto=None):
    nc, J = ctx.nc, ctx.J
    ps = ctx.psum
    xch = ctx.ten["xch"]
    sv, SV = ctx.ten["sv"], ctx.ten["SV"]
    dram = ctx.dram
    m_ltril = ctx.ten["m_ltril"]
    ones_row = ctx.ten["ones_row"]
    ones_col = ctx.ten["ones_col"]
    import contextlib

    with contextlib.ExitStack() as st:
        xhat = [st.enter_context(_sb(nc, f"axh{c}", [128, T], BF16)) for c in range(KC)]
        ph_macaron_ln(ctx, "attn", xhat=xhat)

        W = {}
        for nm in ("wqT", "wkT", "wvT", "woT"):
            W[nm] = [st.enter_context(_sb(nc, f"a{nm}{c}", [128, 1024], BF16))
                     for c in range(KC)]
        fmq = st.enter_context(_sb(nc, "afmq", [128, 128], BF16))
        fmk = st.enter_context(_sb(nc, "afmk", [128, 128], BF16))
        fmrows = st.enter_context(_sb(nc, "afmr", [1, 256], BF16))
        S_sb = st.enter_context(_sb(nc, "aS", [128, 16 * 128], BF16))
        q_c = st.enter_context(_sb(nc, "aqc", [128, 8 * 128], BF16))
        k_c = st.enter_context(_sb(nc, "akc", [128, 8 * 128], BF16))
        vT_c = st.enter_context(_sb(nc, "avc", [128, 8 * 128], BF16))
        phiq = st.enter_context(_sb(nc, "aphq", [128, 16 * 128], BF16))
        phik = st.enter_context(_sb(nc, "aphk", [128, 16 * 128], BF16))
        phikT = st.enter_context(_sb(nc, "aphkT", [128, 16 * 128], BF16))
        Osb = st.enter_context(_sb(nc, "aOsb", [128, 8 * 128], BF16))
        asb = q_c
        Onm = k_c
        sq = phik
        msr = st.enter_context(_sb(nc, "amsr", [1, 512], F32))
        rrow = st.enter_context(_sb(nc, "arr", [1, 1024], BF16))
        imat = st.enter_context(_sb(nc, "aimat", [128, 128], BF16))
        irow = st.enter_context(_sb(nc, "airow", [1, 512], I32))
        wrow = st.enter_context(_sb(nc, "awrow", [1, 512], F32))

        p = Phase(ctx, "attn_ld")
        J.join(p)
        p.do("gpsimd", lambda E: E.iota(q_c[:, 0:128], [[1, 128]], channel_multiplier=-1,
                                        allow_small_or_imprecise_dtypes=True), inc="G")
        p.w("vector", "G", ctx.ctr["G"])
        p.do("vector", lambda E: E.tensor_scalar(imat[:, :], q_c[:, 0:128],
                                                 0.0, None, ALU.is_equal), inc="V")
        p.ss("vector")
        for nm in ("wqT", "wkT", "wvT", "woT"):
            for c in range(KC):
                p.do("gpsimd", lambda E, nm=nm, c=c: E.dma_start(
                    out=W[nm][c][:, :], in_=dram[nm][128 * c:128 * (c + 1), :]),
                    inc="AW", n=16)
        p.do("gpsimd", lambda E: E.dma_start(out=fmq[:, :], in_=dram["fmqT"][:, :]), inc="AW", n=16)
        p.do("gpsimd", lambda E: E.dma_start(out=fmk[:, :], in_=dram["fmkT"][:, :]), inc="AW", n=16)
        p.do("gpsimd", lambda E: E.dma_start(out=fmrows[0:1, 0:128], in_=dram["fmq_b_row"][:, :]), inc="AW", n=16)
        p.do("gpsimd", lambda E: E.dma_start(out=fmrows[0:1, 128:256], in_=dram["fmk_b_row"][:, :]), inc="AW", n=16)
        ldw = ctx.ctr["AW"]
        p.close()

        def proj(p, i, wname, dst):
            for h in range(H):
                bank = ps[h // 4]
                col = (h % 4) * 128
                for c in range(KC):
                    p.do("tensor", lambda E, h=h, c=c, i=i, bank=bank, col=col, wname=wname: E.matmul(
                        bank[:, col:col + 128], W[wname][c][:, h * 128:(h + 1) * 128],
                        xhat[c][:, i * CH:(i + 1) * CH],
                        start=(c == 0), stop=(c == KC - 1)), inc="T")
            tm = ctx.ctr["T"]
            p.w("vector", "T", tm)
            for b in range(2):
                p.do("vector", lambda E, b=b, dst=dst: E.tensor_copy(
                    dst[:, b * 512:(b + 1) * 512], ps[b][:, :]), inc="V")
            p.w("tensor", "V", ctx.ctr["V"])
            return ctx.ctr["V"]

        def vproj(p, i):
            for c in range(KC):
                for b in range(2):
                    p.do("tensor", lambda E, c=c, b=b, i=i: E.matmul(
                        ps[b][:, :], xhat[c][:, i * CH:(i + 1) * CH],
                        W["wvT"][c][:, b * 512:(b + 1) * 512],
                        start=(c == 0), stop=(c == KC - 1)), inc="T")
            tm = ctx.ctr["T"]
            p.w("vector", "T", tm)
            for b in range(2):
                p.do("vector", lambda E, b=b: E.tensor_copy(
                    vT_c[:, b * 512:(b + 1) * 512], ps[b][:, :]), inc="V")
            p.w("tensor", "V", ctx.ctr["V"])
            return ctx.ctr["V"]

        def hedgehog(p, src_c, fmw, brow_sl, dstp, want_T=False):
            for h in range(H):
                bank = ps[2 + h % 2]
                if h >= 2:
                    p.w("tensor", "S", hh_done[h - 2])
                p.do("tensor", lambda E, h=h, bank=bank, src_c=src_c, fmw=fmw: E.matmul(
                    bank[:, 0:128], fmw[:, :], src_c[:, h * 128:(h + 1) * 128],
                    start=True, stop=True), inc="T")
                tm = ctx.ctr["T"]
                p.w("scalar", "T", tm)
                p.do("scalar", lambda E, h=h, bank=bank, dstp=dstp: E.activation(
                    dstp[:, h * 128:(h + 1) * 128], bank[:, 0:128], AF.Exp,
                    bias=sv[:, SV["fmq_b" if fmw is fmq else "fmk_b"]:
                            SV["fmq_b" if fmw is fmq else "fmk_b"] + 1]), inc="S")
                p.do("scalar", lambda E, h=h, bank=bank, dstp=dstp: E.activation(
                    dstp[:, 1024 + h * 128:1024 + (h + 1) * 128], bank[:, 0:128], AF.Exp,
                    scale=-1.0,
                    bias=sv[:, SV["fmq_bn" if fmw is fmq else "fmk_bn"]:
                            SV["fmq_bn" if fmw is fmq else "fmk_bn"] + 1]), inc="S")
                hh_done[h] = ctx.ctr["S"]
            p.w("tensor", "S", ctx.ctr["S"])

        def hedgehogT(p, src_c, fmw, brow, dstp):
            for h in range(H):
                bank = ps[2 + h % 2]
                if h >= 2:
                    p.w("tensor", "S", hh_done[h - 2])
                p.do("tensor", lambda E, h=h, bank=bank, src_c=src_c, fmw=fmw: E.matmul(
                    bank[:, 0:128], src_c[:, h * 128:(h + 1) * 128], fmw[:, :],
                    start=True, stop=False), inc="T")
                p.do("tensor", lambda E, h=h, bank=bank, brow=brow: E.matmul(
                    bank[:, 0:128], ones_row[:, :], brow,
                    start=False, stop=True), inc="T")
                tm = ctx.ctr["T"]
                p.w("scalar", "T", tm)
                p.do("scalar", lambda E, h=h, bank=bank, dstp=dstp: E.activation(
                    dstp[:, h * 128:(h + 1) * 128], bank[:, 0:128], AF.Exp), inc="S")
                p.do("scalar", lambda E, h=h, bank=bank, dstp=dstp: E.activation(
                    dstp[:, 1024 + h * 128:1024 + (h + 1) * 128], bank[:, 0:128], AF.Exp,
                    scale=-1.0), inc="S")
                hh_done[h] = ctx.ctr["S"]
            p.w("tensor", "S", ctx.ctr["S"])

        def s_tile(hid, e):
            i = hid * 2 + e
            return ps[4 + i // 4][:, (i % 4) * 128:(i % 4) * 128 + 128]

        pA = Phase(ctx, "attn_A")
        J.join(pA)
        for e in ("tensor", "vector", "scalar"):
            pA.w(e, "AW", ldw)
        hh_done = {}
        supd = 0
        for b in range(4):
            pA.do("vector", lambda E, b=b: E.memset(ps[4 + b][:, :], 0.0), inc="V")
        for i in range(NCH):
            proj(pA, i, "wkT", k_c)
            hedgehogT(pA, k_c, fmk, fmrows[0:1, 128:256], phikT)
            vproj(pA, i)
            pA.w("tensor", "V", ctx.ctr["V"])
            for h in range(H):
                for e in range(2):
                    pA.do("tensor", lambda E, h=h, e=e, i=i: E.matmul(
                        s_tile(h, e), phikT[:, e * 1024 + h * 128:e * 1024 + (h + 1) * 128],
                        vT_c[:, h * 128:(h + 1) * 128],
                        start=False, stop=True,
                        skip_group_check=True), inc="T")
            supd = ctx.ctr["T"]
        pA.w("vector", "T", supd)
        for b in range(4):
            pA.do("vector", lambda E, b=b: E.tensor_copy(
                S_sb[:, b * 512:(b + 1) * 512], ps[4 + b][:, :]), inc="V")
        sev = ctx.ctr["V"]
        pA.w("sync", "V", sev)
        if upto == "attnS":
            dS = nc.dram_tensor("dbg_SA", [128, 2048], BF16, kind="ExternalOutput")
            pA.do("sync", lambda E: E.dma_start(out=dS[:, :], in_=S_sb[:, :]),
                  inc="Y", n=16)
            pA.w("sync", "Y", ctx.ctr["Y"])
        c4 = coll_issue(ctx, pA, [(dram["c4_in"][:, :], S_sb[:, :])],
                        dram["c4_in"], dram["c4_out"])
        pA.close()

        pB = Phase(ctx, "attn_B")
        J.join(pB)
        hh_done = {}
        rv = coll_recv(ctx, pB, c4, [(S_sb[:, :], dram["c4_out"][0:128, :])])
        pB.w("vector", "CR", rv)
        pB.do("vector", lambda E: E.tensor_scalar_mul(
            S_sb[:, :], S_sb[:, :], sv[:, SV["coremask"]:SV["coremask"] + 1]), inc="V")
        for b in range(4):
            pB.do("vector", lambda E, b=b: E.memset(ps[4 + b][:, :], 0.0), inc="V")
        vmask = ctx.ctr["V"]
        pB.w("tensor", "V", vmask)
        for h in range(H):
            for e in range(2):
                pB.do("tensor", lambda E, h=h, e=e: E.matmul(
                    s_tile(h, e), imat[:, :],
                    S_sb[:, (h * 2 + e) * 128:(h * 2 + e + 1) * 128],
                    start=False, stop=True, skip_group_check=True), inc="T")
        sinj = ctx.ctr["T"]
        supd = sinj
        for i in range(NCH):
            pB.w("vector", "T", supd)
            for b in range(4):
                pB.do("vector", lambda E, b=b: E.tensor_scalar_mul(
                    S_sb[:, b * 512:(b + 1) * 512], ps[4 + b][:, :], PHI_SCALE), inc="V")
            sev = ctx.ctr["V"]
            pB.w("tensor", "V", sev)
            proj(pB, i, "wqT", q_c)
            hedgehog(pB, q_c, fmq, None, phiq)
            proj(pB, i, "wkT", k_c)
            hedgehog(pB, k_c, fmk, None, phik)
            hedgehogT(pB, k_c, fmk, fmrows[0:1, 128:256], phikT)
            vproj(pB, i)
            vready = ctx.ctr["V"]
            sc_done = {}
            ov_done = {}
            for h in range(H):
                bank = ps[h % 2]
                if h >= 2:
                    pB.w("tensor", "V", sc_done[h - 2])
                pB.do("tensor", lambda E, h=h, bank=bank: E.matmul(
                    bank[:, 0:128], phik[:, h * 128:(h + 1) * 128],
                    phiq[:, h * 128:(h + 1) * 128], start=True, stop=False), inc="T")
                pB.do("tensor", lambda E, h=h, bank=bank: E.matmul(
                    bank[:, 0:128], phik[:, 1024 + h * 128:1024 + (h + 1) * 128],
                    phiq[:, 1024 + h * 128:1024 + (h + 1) * 128],
                    start=False, stop=True), inc="T")
                tsc = ctx.ctr["T"]
                pB.w("vector", "T", tsc)
                pB.do("vector", lambda E, h=h, bank=bank: E.tensor_mul(
                    asb[:, h * 128:(h + 1) * 128], bank[:, 0:128],
                    m_ltril[:, 0:CH]), inc="V")
                sc_done[h] = ctx.ctr["V"]
                pB.w("tensor", "V", sc_done[h])
                ob = ps[2 + h % 2]
                if h >= 2:
                    pB.w("tensor", "V", ov_done[h - 2])
                pB.do("tensor", lambda E, h=h, ob=ob: E.matmul(
                    ob[:, 0:128], vT_c[:, h * 128:(h + 1) * 128],
                    asb[:, h * 128:(h + 1) * 128], start=True, stop=False), inc="T")
                pB.do("tensor", lambda E, h=h, ob=ob: E.matmul(
                    ob[:, 0:128], S_sb[:, (h * 2) * 128:(h * 2 + 1) * 128],
                    phiq[:, h * 128:(h + 1) * 128], start=False, stop=False), inc="T")
                pB.do("tensor", lambda E, h=h, ob=ob: E.matmul(
                    ob[:, 0:128], S_sb[:, (h * 2 + 1) * 128:(h * 2 + 2) * 128],
                    phiq[:, 1024 + h * 128:1024 + (h + 1) * 128],
                    start=False, stop=True), inc="T")
                tob = ctx.ctr["T"]
                pB.w("vector", "T", tob)
                pB.do("vector", lambda E, h=h, ob=ob: E.tensor_copy(
                    Osb[:, h * 128:(h + 1) * 128], ob[:, 0:128]), inc="V")
                ov_done[h] = ctx.ctr["V"]
            if upto is not None and upto.startswith("attnD") and int(upto[5:]) == i:
                douts = {}
                for nm, buf in (("phiq", phiq), ("phik", phik), ("asb", asb),
                                ("Osb", Osb), ("Ssb", S_sb), ("vTc", vT_c)):
                    douts[nm] = nc.dram_tensor(f"dbg_{nm}", list(buf.shape), BF16,
                                               kind="ExternalOutput")
                pB.w("sync", "V", ctx.ctr["V"])
                for nm, buf in (("phiq", phiq), ("phik", phik), ("asb", asb),
                                ("Osb", Osb), ("Ssb", S_sb), ("vTc", vT_c)):
                    pB.do("sync", lambda E, nm=nm, buf=buf: E.dma_start(
                        out=douts[nm][:, :], in_=buf[:, :]), inc="Y", n=16)
                pB.w("sync", "Y", ctx.ctr["Y"])
                for e in ("tensor", "vector", "scalar"):
                    pB.w(e, "Y", ctx.ctr["Y"])
            pB.w("tensor", "V", ctx.ctr["V"])
            for h in range(H):
                for e in range(2):
                    pB.do("tensor", lambda E, h=h, e=e, i=i: E.matmul(
                        s_tile(h, e), phikT[:, e * 1024 + h * 128:e * 1024 + (h + 1) * 128],
                        vT_c[:, h * 128:(h + 1) * 128],
                        start=False, stop=True, skip_group_check=True), inc="T")
            supd = ctx.ctr["T"]
            vsq = ctx.ctr["V"]
            pB.w("scalar", "V", vsq)
            pB.do("scalar", lambda E: E.activation(sq[:, 0:1024], Osb[:, 0:1024], AF.Square), inc="S")
            ssq = ctx.ctr["S"]
            pB.w("tensor", "S", ssq)
            pB.do("tensor", lambda E: E.matmul(ps[0][0:1, :], ones_col[:, :],
                                               sq[:, 0:512], start=True, stop=True), inc="T")
            pB.do("tensor", lambda E: E.matmul(ps[1][0:1, :], ones_col[:, :],
                                               sq[:, 512:1024], start=True, stop=True), inc="T")
            tms = ctx.ctr["T"]
            pB.w("vector", "T", tms)
            for b in range(2):
                pB.do("vector", lambda E, b=b: E.tensor_scalar(
                    msr[0:1, :], ps[b][0:1, :], 1.0 / DV, 1e-5,
                    ALU.mult, ALU.add), inc="V")
                pB.ss("vector")
                pB.do("vector", lambda E: E.tensor_scalar(
                    irow[0:1, :], msr[0:1, :].bitcast(I32), 23, None,
                    ALU.logical_shift_right), inc="V")
                pB.ss("vector")
                pB.do("vector", lambda E: E.tensor_scalar(
                    irow[0:1, :], irow[0:1, :], 127, None,
                    ALU.subtract), inc="V")
                pB.ss("vector")
                pB.do("vector", lambda E: E.tensor_scalar(
                    msr[0:1, :].bitcast(I32), msr[0:1, :].bitcast(I32),
                    0x7FFFFF, 0x3F800000,
                    ALU.bitwise_and, ALU.bitwise_or), inc="V")
                vf = ctx.ctr["V"]
                pB.w("scalar", "V", vf)
                pB.do("scalar", lambda E: E.activation(wrow[0:1, :], msr[0:1, :],
                                                       AF.Ln), inc="S")
                slf = ctx.ctr["S"]
                pB.w("vector", "S", slf)
                pB.do("vector", lambda E: E.tensor_copy(msr[0:1, :], irow[0:1, :]),
                      inc="V")
                pB.ss("vector")
                pB.do("vector", lambda E: E.scalar_tensor_tensor(
                    wrow[0:1, :], msr[0:1, :], float(np.log(2.0)), wrow[0:1, :],
                    ALU.mult, ALU.add), inc="V")
                vw = ctx.ctr["V"]
                pB.w("scalar", "V", vw)
                pB.do("scalar", lambda E, b=b: E.activation(
                    rrow[0:1, b * 512:(b + 1) * 512], wrow[0:1, :], AF.Exp,
                    scale=-0.5), inc="S")
            srr = ctx.ctr["S"]
            pB.w("tensor", "S", srr)
            pB.do("tensor", lambda E: E.matmul(ps[0][:, :], ones_row[:, :],
                                               rrow[0:1, 0:512], start=True, stop=True), inc="T")
            pB.do("tensor", lambda E: E.matmul(ps[1][:, :], ones_row[:, :],
                                               rrow[0:1, 512:1024], start=True, stop=True), inc="T")
            trb = ctx.ctr["T"]
            pB.w("vector", "T", trb)
            for b in range(2):
                pB.do("vector", lambda E, b=b: E.scalar_tensor_tensor(
                    Onm[:, b * 512:(b + 1) * 512], Osb[:, b * 512:(b + 1) * 512],
                    sv[:, SV["rms_w"]:SV["rms_w"] + 1], ps[b][:, :],
                    ALU.mult, ALU.mult), inc="V")
            vnm = ctx.ctr["V"]
            if upto is not None and upto.startswith("attnE") and int(upto[5:]) == i:
                deo = {}
                for nm, buf, dt in (("Onm", Onm, BF16), ("rrow", rrow, BF16),
                                    ("msr", msr, F32), ("Osb", Osb, BF16)):
                    deo[nm] = nc.dram_tensor(f"dbg_{nm}", list(buf.shape), dt,
                                             kind="ExternalOutput")
                pB.w("sync", "V", ctx.ctr["V"])
                pB.w("sync", "S", ctx.ctr["S"])
                for nm, buf in (("Onm", Onm), ("rrow", rrow), ("msr", msr), ("Osb", Osb)):
                    pB.do("sync", lambda E, nm=nm, buf=buf: E.dma_start(
                        out=deo[nm][:, :], in_=buf[:, :]), inc="Y", n=16)
                pB.w("sync", "Y", ctx.ctr["Y"])
                for e in ("tensor", "vector", "scalar"):
                    pB.w(e, "Y", ctx.ctr["Y"])
            pB.w("tensor", "V", vnm)
            woevd = {}
            for m in range(8):
                bank = ps[m % 4]
                if m >= 4:
                    pB.w("tensor", "V", woevd[m - 4])
                for ic in range(KC):
                    pB.do("tensor", lambda E, m=m, ic=ic, bank=bank: E.matmul(
                        bank[:, 0:128], W["woT"][ic][:, m * 128:(m + 1) * 128],
                        Onm[:, ic * 128:(ic + 1) * 128],
                        start=(ic == 0), stop=(ic == KC - 1)), inc="T")
                tw = ctx.ctr["T"]
                pB.w("vector", "T", tw)
                pB.do("vector", lambda E, m=m, bank=bank, i=i: E.tensor_add(
                    xch[m][:, 1 + i * CH:1 + (i + 1) * CH],
                    xch[m][:, 1 + i * CH:1 + (i + 1) * CH], bank[:, 0:128]), inc="V")
                woevd[m] = ctx.ctr["V"]
            pB.w("tensor", "V", ctx.ctr["V"])
        pB.close()

    p = Phase(ctx, "c0issue")
    J.join(p)
    pairs = [(dram["c1_in"][:, c:c + 1], xch[c][:, T:T + 1]) for c in range(KC)]
    c0 = coll_issue(ctx, p, pairs, dram["c1_in"], dram["c1_out"],
                    waits=[("V", ctx.ctr["V"])])
    p.close()
    return c0


def ph_snapshot(ctx, name):
    nc = ctx.nc
    d = nc.dram_tensor(f"snap_{name}", [D, T], F32, kind="ExternalOutput")
    xch = ctx.ten["xch"]
    p = Phase(ctx, f"snap_{name}")
    ctx.J.join(p)
    for c in range(KC):
        p.do("sync", lambda E, c=c: E.dma_start(
            out=d[128 * c:128 * (c + 1), :], in_=xch[c][:, 1:T + 1]), inc="Y", n=16)
    p.w("sync", "Y", ctx.ctr["Y"])
    p.close()


def ph_boundary_recv(ctx, ccval):
    nc, J = ctx.nc, ctx.J
    sv, SV = ctx.ten["sv"], ctx.ten["SV"]
    xch = ctx.ten["xch"]
    import contextlib
    with contextlib.ExitStack() as st:
        bndb = st.enter_context(_sb(nc, "brc", [128, KC], F32))
        p = Phase(ctx, "bndrecv")
        J.join(p)
        rv = coll_recv(ctx, p, ccval, [(bndb[:, :], ctx.dram["c1_out"][0:128, :])])
        p.w("vector", "CR", rv)
        for c in range(KC):
            p.do("vector", lambda E, c=c: E.tensor_scalar_mul(
                xch[c][:, 0:1], bndb[:, c:c + 1],
                sv[:, SV["coremask"]:SV["coremask"] + 1]), inc="V")
        p.close()


def ph_dump_tabs(ctx, tabs):
    nc = ctx.nc
    outs = {}
    for nm in ("cs_q", "ss_q", "cs_k", "ss_k"):
        outs[nm] = nc.dram_tensor(f"dbg_{nm}", [128, T], BF16, kind="ExternalOutput")
    p = Phase(ctx, "dumptabs")
    ctx.J.join(p)
    for nm in outs:
        p.do("sync", lambda E, nm=nm: E.dma_start(out=outs[nm][:, :], in_=tabs[nm][:, :]),
             inc="Y", n=16)
    p.w("sync", "Y", ctx.ctr["Y"])
    if ctx.ctr.get("CC", 0):
        p.w("sync", "CC", ctx.ctr["CC"])
    p.close()


def ph_dump_qkv(ctx, fl):
    nc = ctx.nc
    dq = nc.dram_tensor("dbg_q", [4 * 128, T], BF16, kind="ExternalOutput")
    dk = nc.dram_tensor("dbg_k", [4 * 128, T], BF16, kind="ExternalOutput")
    dv = nc.dram_tensor("dbg_v", [128, 16 * 520], BF16, kind="ExternalOutput")
    p = Phase(ctx, "dumpqkv")
    ctx.J.join(p)
    for c in range(4):
        p.do("sync", lambda E, c=c: E.dma_start(out=dq[128 * c:128 * (c + 1), :],
                                                in_=fl["qf"][c][:, :]), inc="Y", n=16)
        p.do("sync", lambda E, c=c: E.dma_start(out=dk[128 * c:128 * (c + 1), :],
                                                in_=fl["kf"][c][:, :]), inc="Y", n=16)
    p.do("sync", lambda E: E.dma_start(out=dv[:, :], in_=fl["vT"][:, :]), inc="Y", n=16)
    p.w("sync", "Y", ctx.ctr["Y"])
    if ctx.ctr.get("CC", 0):
        p.w("sync", "CC", ctx.ctr["CC"])
    p.close()


def ph_finale(ctx, src="x"):
    xch = ctx.ten["xch"]
    p = Phase(ctx, "finale")
    ctx.J.join(p)
    for c in range(KC):
        p.do("sync", lambda E, c=c: E.dma_start(
            out=ctx.dram["out"][128 * c:128 * (c + 1), :], in_=xch[c][:, 1:T + 1]), inc="Y", n=16)
    p.w("sync", "Y", ctx.ctr["Y"])
    if ctx.ctr.get("CC", 0):
        p.w("sync", "CC", ctx.ctr["CC"])
    p.close()



def kernel(**inputs):
    x = np.asarray(inputs["L0_attn_x"])
    perm = _w1_perm()

    w_shared = {
        "wqT": np.ascontiguousarray(np.asarray(inputs["L0_attn_Wq"]).T),
        "wkT": np.ascontiguousarray(np.asarray(inputs["L0_attn_Wk"]).T),
        "wvT": np.ascontiguousarray(np.asarray(inputs["L0_attn_Wv"]).T),
        "woT": np.ascontiguousarray(np.asarray(inputs["L0_attn_Wo"]).T),
        "fmqT": np.ascontiguousarray(np.asarray(inputs["L0_attn_fmq_w"]).T),
        "fmkT": np.ascontiguousarray(np.asarray(inputs["L0_attn_fmk_w"]).T),
        "fmq_b": np.ascontiguousarray(np.asarray(inputs["L0_attn_fmq_b"]).reshape(DK, 1)),
        "fmk_b": np.ascontiguousarray(np.asarray(inputs["L0_attn_fmk_b"]).reshape(DK, 1)),
        "fmq_b_row": np.ascontiguousarray(np.asarray(inputs["L0_attn_fmq_b"]).reshape(1, DK)),
        "fmk_b_row": np.ascontiguousarray(np.asarray(inputs["L0_attn_fmk_b"]).reshape(1, DK)),
        "rms_w": np.ascontiguousarray(np.asarray(inputs["L0_attn_rms_w"]).reshape(DK, 1)),
    }
    for l in range(2):
        w_shared[f"wqkv{l}"] = np.ascontiguousarray(np.asarray(inputs["L0_ffn_wqkv"])[l])
        w_shared[f"wo{l}"] = np.ascontiguousarray(np.asarray(inputs["L0_ffn_wo"])[l])
        w_shared[f"w2{l}"] = np.ascontiguousarray(np.asarray(inputs["L0_ffn_w2"])[l])
        w_shared[f"w1p{l}"] = np.ascontiguousarray(np.asarray(inputs["L0_ffn_w1"])[l][:, perm])
        b1p = np.asarray(inputs["L0_ffn_b1"])[l][perm]
        b1cols = np.zeros((128, 2 * NPAIR), np.float32)
        col = 0
        for pr in range(NPAIR):
            off, wp = _pair_off(pr)
            b1cols[0:wp, 2 * pr] = b1p[off:off + wp]
            b1cols[0:wp, 2 * pr + 1] = b1p[off + wp:off + 2 * wp]
        w_shared[f"b1p{l}"] = b1cols
        w_shared[f"b2{l}"] = np.ascontiguousarray(np.asarray(inputs["L0_ffn_b2"])[l].reshape(KC, 128).T)
        w_shared[f"lng{l}"] = np.ascontiguousarray(np.asarray(inputs["L0_ffn_ln_g"])[l].reshape(KC, 128).T)
        w_shared[f"lnb{l}"] = np.ascontiguousarray(np.asarray(inputs["L0_ffn_ln_b"])[l].reshape(KC, 128).T)
        w_shared[f"qs{l}"] = np.ascontiguousarray(np.tile(np.asarray(inputs["L0_ffn_q_scale"])[l], 2).reshape(128, 1))
        w_shared[f"ks{l}"] = np.ascontiguousarray(np.tile(np.asarray(inputs["L0_ffn_k_scale"])[l], 2).reshape(128, 1))

    in_maps = []
    for c in range(8):
        b, h = c // 2, c % 2
        shard = np.ascontiguousarray(x[b, h * T:(h + 1) * T, :].T)
        if h == 1:
            xprev = np.ascontiguousarray(x[b, h * T - 1, :].reshape(KC, 128).T)
            cm = np.ones((128, 1), np.float32)
        else:
            xprev = np.zeros((128, KC), np.float32)
            cm = np.zeros((128, 1), np.float32)
        m = dict(w_shared)
        m["x_t"] = shard
        m["xprev"] = xprev
        m["coremask"] = cm
        m["toff"] = np.array([[h * T]], np.float32)
        in_maps.append(m)

    nc, ctx = build_program(upto=kernel._upto)
    res = run_bass_kernel_spmd(nc, in_maps, core_ids=list(range(8)),
                               **(dict(trace=True) if kernel._trace else {}))
    if kernel._trace:
        kernel._last_scope_times = res.per_core_scope_times

    if (kernel._upto in ("ffn1l0qkv", "ffn1l0qkvraw", "tbl", "mlpdbg")
            or kernel._upto.startswith("attnw") or kernel._upto.startswith("attnD")
            or kernel._upto.startswith("attnE") or kernel._upto == "attnS"):
        return res.results

    out = np.empty((B, N_FULL, D), np.float32)
    for c in range(8):
        b, h = c // 2, c % 2
        out[b, h * T:(h + 1) * T, :] = res.results[c]["out"].T
    return out


kernel._upto = "full"
kernel._trace = False
kernel._last_scope_times = None

if __name__ == "__main__":
    pass


# version 09 (final)
# speedup vs baseline: 1.0350x; 1.0350x over previous
import numpy as np
import ml_dtypes
import concourse.bass as bass
import concourse.mybir as mybir
from concourse.bass_utils import run_bass_kernel_spmd

F32 = mybir.dt.float32
I32 = mybir.dt.int32
BF16 = mybir.dt.bfloat16
AF = mybir.ActivationFunctionType
ALU = mybir.AluOpType
AX = mybir.AxisListType

B = 4
N_FULL = 4096
T = 2048
D = 1024
KC = 8
H = 8
DH = 64
INNER = 512
WIN = 512
NWIN = T // WIN
HID = 2730
NPAIR = 22
DK = 128
DV = 128
CH = 128
NCH = T // CH
QK_SCALE = 8.0
PHI_SCALE = 1.0 / 16.0
LN_EPS = 1e-5

ENGINES = ("tensor", "vector", "scalar", "sync", "gpsimd")

CONST_BIAS_VALUES = [1e-5, 1e-24, 1e-30, -8.0, float(np.pi / 2), 0.5, 2.0]

DEBUG_TAPS = []
SNAP_MODE = False



def _w1_perm():
    idx = []
    for p in range(NPAIR):
        w = min(128, HID - 128 * p)
        a = np.arange(128 * p, 128 * p + w)
        idx.append(a)
        idx.append(HID + a)
    return np.concatenate(idx)


def _pair_off(p):
    return 256 * p if p < 21 else 256 * 21, min(128, HID - 128 * p)



class Ctx:
    def __init__(self, nc):
        self.nc = nc
        self.ctr = {}
        self.sems = {}
        self.stack = None
        self.ten = {}
        self.psum = []

    def sem(self, name):
        if name not in self.sems:
            h = self.nc.alloc_semaphore(name=name)
            self.sems[name] = h
            self.ctr[name] = 0
        return self.sems[name]


class Phase:

    def __init__(self, ctx, name):
        self.ctx = ctx
        self.name = name
        self.items = {e: [] for e in ENGINES}
        self.end = {}

    def w(self, eng, sem, val):
        h = self.ctx.sem(sem)
        self.items[eng].append(dict(kind="wait", sem=sem, val=int(val),
                                    fn=lambda E, h=h, v=int(val): E.wait_ge(h, v)))

    def do(self, eng, fn, inc=None, n=1, tag=""):
        item = dict(kind="do", sem=inc, val=n, tag=tag)
        if inc is not None:
            h = self.ctx.sem(inc)
            self.ctx.ctr[inc] += n

            def run(E, fn=fn, h=h, n=n):
                ins = fn(E)
                ins.then_inc(h, n)
            item["fn"] = run
            item["at"] = self.ctx.ctr[inc]
        else:
            item["fn"] = fn
        self.items[eng].append(item)
        return self.ctx.ctr.get(inc) if inc is not None else None

    def ss(self, eng):
        sems = {"tensor": "T", "vector": "V", "scalar": "S",
                "sync": "Y", "gpsimd": "G"}
        s = sems[eng]
        if self.ctx.ctr.get(s, 0) > 0:
            self.w(eng, s, self.ctx.ctr[s])

    def close(self):
        planned = {}
        for e in ENGINES:
            for it in self.items[e]:
                if it["kind"] == "do" and it["sem"] is not None:
                    planned[it["sem"]] = planned.get(it["sem"], 0) + it["val"]
        state = {s: self.ctx.ctr[s] - planned.get(s, 0) for s in self.ctx.ctr}
        heads = {e: 0 for e in ENGINES}
        progress = True
        while progress:
            progress = False
            for e in ENGINES:
                while heads[e] < len(self.items[e]):
                    it = self.items[e][heads[e]]
                    if it["kind"] == "wait" and state.get(it["sem"], 0) < it["val"]:
                        break
                    if it["kind"] == "do" and it["sem"] is not None:
                        state[it["sem"]] = state.get(it["sem"], 0) + it["val"]
                    heads[e] += 1
                    progress = True
        stuck = {e: (heads[e], len(self.items[e])) for e in ENGINES
                 if heads[e] < len(self.items[e])}
        if stuck:
            details = []
            for e, (h, tot) in stuck.items():
                it = self.items[e][h]
                details.append(f"{e}@{h}/{tot} waiting {it.get('sem')}>={it.get('val')}"
                               f" (have {state.get(it.get('sem'), 0)})")
            raise RuntimeError(f"phase {self.name}: deadlock: {details}; state={state}")

        with self.ctx.nc.named_scope(f"ph_{self.name}"):
            with self.ctx.nc.Block() as block:
                for e in ENGINES:
                    if not self.items[e]:
                        continue

                    def body(E, items=self.items[e]):
                        for it in items:
                            it["fn"](E)
                    getattr(block, e)(body)


class Joiner:

    def __init__(self, ctx):
        self.ctx = ctx
        self.esem = {"tensor": "T", "vector": "V", "scalar": "S",
                     "sync": "Y", "gpsimd": "G"}
        for s in self.esem.values():
            ctx.sem(s)

    def join(self, phase, engines=ENGINES, excl=()):
        vals = {e: self.ctx.ctr[self.esem[e]] for e in ENGINES}
        for e in engines:
            for o in ENGINES:
                if o == e or o in excl:
                    continue
                if vals[o] > 0:
                    phase.w(e, self.esem[o], vals[o])



def build_program(upto="full"):
    nc = bass.Bass("TRN2", target_bir_lowering=False, debug=False, detect_race_conditions=False)
    ctx = Ctx(nc)
    import contextlib
    ctx.stack = contextlib.ExitStack()

    dram = {}

    def din(name, shape, dt=F32):
        dram[name] = nc.dram_tensor(name, list(shape), dt, kind="ExternalInput")
        return dram[name]

    def dint(name, shape, dt=F32):
        dram[name] = nc.dram_tensor(name, list(shape), dt, kind="Internal")
        return dram[name]

    din("x_t", [D, T])
    din("xprev", [128, KC])
    din("coremask", [128, 1])
    din("toff", [1, 1])
    for l in range(2):
        din(f"wqkv{l}", [D, 3 * INNER])
        din(f"wo{l}", [INNER, D])
        din(f"w2{l}", [HID, D])
        din(f"w1p{l}", [D, 2 * HID])
        din(f"b1p{l}", [128, 2 * NPAIR])
        din(f"b2{l}", [128, KC])
        din(f"lng{l}", [128, KC])
        din(f"lnb{l}", [128, KC])
        din(f"qs{l}", [128, 1])
        din(f"ks{l}", [128, 1])
    for nm in ("wqT", "wkT", "wvT", "woT"):
        din(nm, [D, D])
    din("fmqT", [DK, DK])
    din("fmkT", [DK, DK])
    din("fmq_b", [DK, 1])
    din("fmk_b", [DK, 1])
    din("fmq_b_row", [1, DK])
    din("fmk_b_row", [1, DK])
    din("rms_w", [DK, 1])

    out = nc.dram_tensor("out", [D, T], F32, kind="ExternalOutput")
    dram["out"] = out

    dint("x_spill", [D, T])
    dint("stash_wqkv", [2, D, 3 * INNER], BF16)
    dint("stash_w1p", [2, D, 2 * HID], BF16)
    dint("stash_w2", [2, HID, D], BF16)
    dint("stash_wo", [2, INNER, D], BF16)
    dint("rowbuf", [128, 16])
    dint("rowbuf2", [128, 16])
    dint("c1_in", [128, KC])
    dint("c1_out", [256, KC])
    dint("c2_in", [128, 4 * WIN + 4 * (8 * 65)], BF16)
    dint("c2_out", [256, 4 * WIN + 4 * (8 * 65)], BF16)
    dint("c4_in", [128, 16 * DK], BF16)
    dint("c4_out", [256, 16 * DK], BF16)

    for name in DEBUG_TAPS:
        pass

    st = ctx.stack

    def sb(name, shape, dt):
        h = st.enter_context(nc.sbuf_tensor(name, list(shape), dt))
        ctx.ten[name] = h
        return h

    xch = [sb(f"x{c}", [128, T + 1], F32) for c in range(KC)]
    m_top = sb("m_top", [128, 4 * WIN], BF16)
    m_top0 = sb("m_top0", [128, 4 * WIN], BF16)
    m_tril = sb("m_tril", [128, 4 * WIN], BF16)
    m_ltril = sb("m_ltril", [128, CH], BF16)
    sv = sb("smallv", [128, 80], F32)
    cst = sb("cstv", [128, max(8, len(CONST_BIAS_VALUES))], F32)
    ones_row = sb("ones_row", [1, 128], BF16)
    ones_row_f32 = sb("ones_row_f32", [1, 128], F32)
    ones_col = sb("ones_col", [128, 1], BF16)
    ones_col_f32 = sb("ones_col_f32", [128, 1], F32)
    seg2 = sb("seg2", [128, 2], BF16)
    halfsel = sb("halfsel", [2, 128], F32)
    pmat = sb("pmat", [128, 128], BF16)
    signcol = sb("signcol", [128, 1], F32)
    seg8 = sb("seg8", [128, 8], BF16)
    for bnk in range(8):
        ctx.psum.append(st.enter_context(nc.psum_tensor(f"pb{bnk}", [128, 512], F32)))

    SV = dict(coremask=0, fmq_b=1, fmq_bn=2, fmk_b=3, fmk_bn=4, rms_w=5,
              lng0=8, lnb0=16, lng1=24, lnb1=32, b2_0=40, b2_1=48, toff=56)

    ctx.ten["xch"] = xch
    J = Joiner(ctx)

    p = Phase(ctx, "P0")
    for c in range(KC):
        p.do("sync", lambda E, c=c: E.dma_start(out=xch[c][:, 1:T + 1],
                                                in_=dram["x_t"][128 * c:128 * (c + 1), :]),
             inc="XL", n=16)
    def ld_col(dst_col, src, shape_rows=128):
        p.do("gpsimd", lambda E, d=dst_col, s=src: E.dma_start(
            out=sv[:shape_rows, d:d + 1], in_=s), inc="GD", n=16)

    ld_col(SV["coremask"], dram["coremask"][:, :])
    ld_col(SV["fmq_b"], dram["fmq_b"][:, :])
    ld_col(SV["fmk_b"], dram["fmk_b"][:, :])
    ld_col(SV["rms_w"], dram["rms_w"][:, :])
    for l in range(2):
        p.do("gpsimd", lambda E, l=l: E.dma_start(
            out=sv[:, SV[f"lng{l}"]:SV[f"lng{l}"] + KC],
            in_=dram[f"lng{l}"][:, :]), inc="GD", n=16)
        p.do("gpsimd", lambda E, l=l: E.dma_start(
            out=sv[:, SV[f"lnb{l}"]:SV[f"lnb{l}"] + KC],
            in_=dram[f"lnb{l}"][:, :]), inc="GD", n=16)
        p.do("gpsimd", lambda E, l=l: E.dma_start(
            out=sv[:, SV[f"b2_{l}"]:SV[f"b2_{l}"] + KC],
            in_=dram[f"b2{l}"][:, :]), inc="GD", n=16)
    p.do("gpsimd", lambda E: E.dma_start(out=sv[0:1, SV["toff"]:SV["toff"] + 1],
                                         in_=dram["toff"][:, :]), inc="GD", n=16)
    SV["xprev"] = 57
    p.do("gpsimd", lambda E: E.dma_start(
        out=sv[:, SV["xprev"]:SV["xprev"] + KC],
        in_=dram["xprev"][:, :]), inc="GD", n=16)
    gq = ctx.ctr["GD"]
    p.w("vector", "GD", gq)
    p.do("vector", lambda E: E.tensor_scalar_mul(sv[:, SV["fmq_bn"]:SV["fmq_bn"] + 1],
                                                 sv[:, SV["fmq_b"]:SV["fmq_b"] + 1], -1.0), inc="V")
    p.do("vector", lambda E: E.tensor_scalar_mul(sv[:, SV["fmk_bn"]:SV["fmk_bn"] + 1],
                                                 sv[:, SV["fmk_b"]:SV["fmk_b"] + 1], -1.0), inc="V")
    for i, v in enumerate(CONST_BIAS_VALUES):
        p.do("vector", lambda E, i=i, v=v: E.memset(cst[:, i:i + 1], float(v)), inc="V")
        nc.const_aps.register(F32, float(v), cst[:, i:i + 1])
    p.do("vector", lambda E: E.memset(ones_row[:, :], 1.0), inc="V")
    p.do("vector", lambda E: E.memset(ones_row_f32[:, :], 1.0), inc="V")
    p.do("vector", lambda E: E.memset(ones_col[:, :], 1.0), inc="V")
    p.do("vector", lambda E: E.memset(ones_col_f32[:, :], 1.0), inc="V")
    p.do("vector", lambda E: E.memset(seg8[:, :], 0.0), inc="V")
    for cc in range(4):
        p.do("vector", lambda E, cc=cc: E.memset(seg8[0:64, 2 * cc:2 * cc + 1], 1.0), inc="V")
        p.do("vector", lambda E, cc=cc: E.memset(seg8[64:128, 2 * cc + 1:2 * cc + 2], 1.0), inc="V")

    p.do("vector", lambda E: E.memset(seg2[:, :], 0.0), inc="V")
    p.do("vector", lambda E: E.memset(seg2[0:64, 0:1], 1.0), inc="V")
    p.do("vector", lambda E: E.memset(seg2[64:128, 1:2], 1.0), inc="V")
    p.do("gpsimd", lambda E: E.iota(halfsel[:, :], [[1, 128]], channel_multiplier=-64,
                                    allow_small_or_imprecise_dtypes=True), inc="G")
    p.w("vector", "G", ctx.ctr["G"])
    p.do("vector", lambda E: E.tensor_scalar(halfsel[:, :], halfsel[:, :],
                                             0.0, None, ALU.is_ge), inc="V")
    p.do("vector", lambda E: E.tensor_scalar(mwork[0:2, 0:128], halfsel[:, :],
                                             0.0, None, ALU.bypass), inc="V", tag="tmp")
    p.do("gpsimd", lambda E: E.iota(mwork[0:2, 128:256], [[1, 128]], channel_multiplier=-64,
                                    base=-63, allow_small_or_imprecise_dtypes=True), inc="G")
    p.w("vector", "G", ctx.ctr["G"])
    p.do("vector", lambda E: E.tensor_scalar(mwork[0:2, 128:256], mwork[0:2, 128:256],
                                             0.0, None, ALU.is_le), inc="V")
    p.do("vector", lambda E: E.tensor_mul(halfsel[:, :], mwork[0:2, 0:128],
                                          mwork[0:2, 128:256]), inc="V")
    for s4 in range(4):
        val = -1.0 if s4 % 2 == 0 else 1.0
        p.do("vector", lambda E, s4=s4, val=val: E.memset(signcol[32 * s4:32 * (s4 + 1), :], val), inc="V")

    import contextlib as _ctl
    _mwork_stack = _ctl.ExitStack()
    mwork = _mwork_stack.enter_context(nc.sbuf_tensor("mwork", [128, 4 * WIN], F32))
    p.do("vector", lambda E: E.memset(mwork[:, :], 0.0), inc="V")
    p.w("gpsimd", "V", ctx.ctr["V"])
    for s4 in range(4):
        o = 32 if s4 % 2 == 0 else -32
        p.do("gpsimd", lambda E, s4=s4, o=o: E.iota(
            mwork[32 * s4:32 * (s4 + 1), 0:128], [[1, 128]],
            base=-(32 * s4) - o, channel_multiplier=-1,
            allow_small_or_imprecise_dtypes=True), inc="G")
    gpm = ctx.ctr["G"]
    p.w("vector", "G", gpm)
    p.do("vector", lambda E: E.tensor_scalar(pmat[:, :], mwork[:, 0:128],
                                             0.0, None, ALU.is_equal), inc="V")
    p.w("gpsimd", "V", ctx.ctr["V"])
    p.do("gpsimd", lambda E: E.iota(mwork[:, 0:CH], [[1, CH]], channel_multiplier=-1,
                                    allow_small_or_imprecise_dtypes=True), inc="G")
    g1 = ctx.ctr["G"]
    p.w("vector", "G", g1)
    p.do("vector", lambda E: E.tensor_scalar(m_ltril[:, 0:CH], mwork[:, 0:CH],
                                             0.0, PHI_SCALE, ALU.is_ge, ALU.mult), inc="V")

    p.w("gpsimd", "V", ctx.ctr["V"])
    for t4 in range(4):
        p.do("gpsimd", lambda E, t4=t4: E.iota(
            mwork[:, t4 * WIN:(t4 + 1) * WIN], [[1, WIN]],
            base=-128 * t4, channel_multiplier=-1,
            allow_small_or_imprecise_dtypes=True), inc="G")
    g2 = ctx.ctr["G"]
    p.w("vector", "G", g2)
    p.do("vector", lambda E: E.tensor_scalar(m_top[:, :], mwork[:, :],
                                             0.0, None, ALU.is_le), inc="V")
    p.do("vector", lambda E: E.tensor_scalar(m_tril[:, :], mwork[:, :],
                                             0.0, None, ALU.is_ge), inc="V")
    p.w("vector", "GD", gq)
    p.do("vector", lambda E: E.tensor_scalar_mul(m_top0[:, :], m_top[:, :],
                                                 sv[:, SV["coremask"]:SV["coremask"] + 1]), inc="V")
    p.ss("vector")
    for mt in (m_top, m_top0, m_tril):
        p.do("vector", lambda E, mt=mt: E.tensor_scalar(mt[:, :], mt[:, :],
                                                        -1.0, 1e7, ALU.add, ALU.mult), inc="V")
    p.ss("vector")
    p.close()
    _mwork_stack.close()

    ctx.ten["sv"] = sv
    ctx.ten["SV"] = SV
    ctx.ten["ones_row"] = ones_row
    ctx.ten["ones_row_f32"] = ones_row_f32
    ctx.ten["ones_col"] = ones_col
    ctx.ten["ones_col_f32"] = ones_col_f32
    ctx.ten["seg8"] = seg8
    ctx.ten["m_top"] = m_top
    ctx.ten["m_top0"] = m_top0
    ctx.ten["m_tril"] = m_tril
    ctx.ten["m_ltril"] = m_ltril
    ctx.ten["seg2"] = seg2
    ctx.ten["halfsel"] = halfsel
    ctx.ten["pmat"] = pmat
    ctx.ten["signcol"] = signcol
    ctx.dram = dram
    ctx.J = J

    if upto == "p0":
        ph_finale(ctx)
        return nc, ctx

    ph_macaron_ln(ctx, "ffn", bsrc=sv[:, SV["xprev"]:SV["xprev"] + KC])
    if upto == "ln1":
        ph_finale(ctx)
        return nc, ctx

    _dbgw = [int(ch) for ch in upto[5:]] if (upto or "").startswith("attnw") else None
    ffn_upto = {"ffn1l0qkv": "qkv_f1l0", "ffn1l0qkvraw": "qkv_f1l0",
                "ffn1l0attn": "attn_f1l0",
                "ffn1l0": "mlp_f1l0", "ffn1l1": "mlp_f1l1",
                "prep": "prep_f1l0", "tbl": "tbl_f1l0",
                "mlpdbg": "mlp_f1l0"}.get(upto)
    if _dbgw is not None:
        ffn_upto = "attn_f1l0"
    fl = ph_ffn_call(ctx, True, upto=ffn_upto, raw_tap=(upto == "ffn1l0qkvraw"),
                     dbg_w1=_dbgw, mlpdbg=(upto == "mlpdbg"))
    if upto in ("ffn1l0qkv", "ffn1l0qkvraw"):
        ph_dump_qkv(ctx, fl)
        return nc, ctx
    if upto == "tbl":
        ph_dump_tabs(ctx, fl["tabs"])
        return nc, ctx
    if _dbgw is not None:
        return nc, ctx
    if upto in ("ffn1l0attn", "ffn1l0", "ffn1l1", "prep", "tbl", "mlpdbg"):
        ph_finale(ctx)
        return nc, ctx

    ph_ffn_combine(ctx, dram["x_t"])
    if upto == "ffn1":
        ph_finale(ctx)
        return nc, ctx

    c0 = ph_attn(ctx, upto=upto)
    if upto in ("attn", "attnS") or (upto or "").startswith("attnD") or (upto or "").startswith("attnE"):
        ph_finale(ctx)
        return nc, ctx

    ph_spill(ctx)
    pr = Phase(ctx, "c0recv")
    J.join(pr)
    rv = coll_recv(ctx, pr, c0, [(sv[:, SV["xprev"]:SV["xprev"] + KC],
                                  dram["c1_out"][0:128, :])])
    pr.w("vector", "CR", rv)
    pr.do("vector", lambda E: E.tensor_scalar_mul(
        sv[:, SV["xprev"]:SV["xprev"] + KC], sv[:, SV["xprev"]:SV["xprev"] + KC],
        sv[:, SV["coremask"]:SV["coremask"] + 1]), inc="V")
    pr.close()
    ph_macaron_ln(ctx, "ffn", bsrc=sv[:, SV["xprev"]:SV["xprev"] + KC])
    ph_ffn_call(ctx, False)
    ph_ffn_combine(ctx, dram["x_spill"])

    ph_finale(ctx)
    return nc, ctx



_SBUF_SEQ = [0]


def _sb(nc, name, shape, dtype):
    _SBUF_SEQ[0] += 1
    return nc.sbuf_tensor(f"{name}_{_SBUF_SEQ[0]}", shape, dtype)



def bcast_row(ctx, p, psum_tile, row_ap, nb_cols):
    ones_row = ctx.ten["ones_row"]
    p.do("tensor", lambda E: E.matmul(psum_tile[:, 0:nb_cols], ones_row[:, :],
                                      row_ap, start=True, stop=True), inc="T")


def row_recip_sqrt(ctx, p, dst_row, src_row, eps, half):
    a = -0.5 if half else -1.0
    p.do("scalar", lambda E: E.activation(dst_row, src_row, AF.Ln, bias=float(eps)), inc="S")
    p.ss("scalar")
    p.do("scalar", lambda E: E.activation(dst_row, dst_row, AF.Exp, scale=float(a)), inc="S")


def ph_macaron_ln(ctx, which, xhat=None, bsrc=None):
    nc, J = ctx.nc, ctx.J
    xch = ctx.ten["xch"]
    sv, SV = ctx.ten["sv"], ctx.ten["SV"]
    ones_row = ctx.ten["ones_row_f32"]
    ps = ctx.psum
    NB = T // 512

    import contextlib
    with contextlib.ExitStack() as st:
        tmpb = [st.enter_context(_sb(nc, f"lnt{i}", [128, 512], BF16)) for i in range(4)]
        murow = st.enter_context(_sb(nc, "lnmu", [1, T], F32))
        ssrow = st.enter_context(_sb(nc, "lnss", [1, T], F32))
        tmprow = st.enter_context(_sb(nc, "lntr", [1, T], F32))
        rrow = ssrow
        brow = murow
        tf32 = [st.enter_context(_sb(nc, f"lnf{i}", [128, 512], F32)) for i in range(2)]
        bnd = st.enter_context(_sb(nc, "lnbnd", [128, 16], F32))

        p = Phase(ctx, f"LN-{which}")
        J.join(p)
        if ctx.ctr.get("XL", 0):
            for e in ("vector", "scalar", "tensor"):
                p.w(e, "XL", ctx.ctr["XL"])

        tmm = {}
        for nb in range(NB):
            for c in range(KC):
                i = (nb * KC + c) % 2
                buf, sqb = tmpb[i], tmpb[2 + i]
                if nb * KC + c >= 2:
                    prev = tmm[nb * KC + c - 2]
                    p.w("vector", "T", prev)
                    p.w("scalar", "T", prev)
                p.do("vector", lambda E, c=c, nb=nb, buf=buf: E.tensor_copy(
                    buf[:, :], xch[c][:, 1 + nb * 512: 1 + (nb + 1) * 512]), inc="V")
                p.do("scalar", lambda E, c=c, nb=nb, sqb=sqb: E.activation(
                    sqb[:, :], xch[c][:, 1 + nb * 512: 1 + (nb + 1) * 512], AF.Square), inc="S")
                p.w("tensor", "V", ctx.ctr["V"])
                p.w("tensor", "S", ctx.ctr["S"])
                p.do("tensor", lambda E, nb=nb, buf=buf, c=c: E.matmul(
                    ps[nb][0:1, :], ctx.ten["ones_col"][:, :], buf[:, :],
                    start=(c == 0), stop=(c == KC - 1)), inc="T")
                p.do("tensor", lambda E, nb=nb, sqb=sqb, c=c: E.matmul(
                    ps[4 + nb][0:1, :], ctx.ten["ones_col"][:, :], sqb[:, :],
                    start=(c == 0), stop=(c == KC - 1)), inc="T")
                tmm[nb * KC + c] = ctx.ctr["T"]
        tstats = ctx.ctr["T"]
        p.w("vector", "T", tstats)
        for nb in range(NB):
            p.do("vector", lambda E, nb=nb: E.tensor_scalar_mul(
                murow[0:1, nb * 512:(nb + 1) * 512], ps[nb][0:1, :], 1.0 / D), inc="V")
            p.do("vector", lambda E, nb=nb: E.tensor_scalar_mul(
                ssrow[0:1, nb * 512:(nb + 1) * 512], ps[4 + nb][0:1, :], 1.0 / D), inc="V")
        p.ss("vector")
        p.do("vector", lambda E: E.tensor_mul(tmprow[0:1, :], murow[0:1, :], murow[0:1, :]), inc="V")
        p.ss("vector")
        p.do("vector", lambda E: E.tensor_sub(ssrow[0:1, :], ssrow[0:1, :], tmprow[0:1, :]), inc="V")
        vvar = ctx.ctr["V"]
        p.w("scalar", "V", vvar)
        row_recip_sqrt(ctx, p, rrow[0:1, :], ssrow[0:1, :], LN_EPS, half=True)
        srow = ctx.ctr["S"]
        p.w("vector", "S", srow)
        p.do("vector", lambda E: E.scalar_tensor_tensor(
            brow[0:1, :], murow[0:1, :], -1.0, rrow[0:1, :], ALU.mult, ALU.mult), inc="V")
        vab = ctx.ctr["V"]

        if which == "ffn":
            p.do("scalar", lambda E: E.activation(bnd[:, 8:16], bsrc, AF.Square), inc="S")
            p.w("tensor", "S", ctx.ctr["S"])
            p.w("tensor", "V", vab)
            p.do("tensor", lambda E: E.matmul(ps[4][0:1, 0:KC], ctx.ten["ones_col_f32"][:, :],
                                              bsrc, start=True, stop=True), inc="T")
            p.do("tensor", lambda E: E.matmul(ps[5][0:1, 0:KC], ctx.ten["ones_col_f32"][:, :],
                                              bnd[:, 8:16], start=True, stop=True), inc="T")
            p.w("vector", "T", ctx.ctr["T"])
            p.do("vector", lambda E: E.reduce_sum(bnd[0:1, 0:1], ps[4][0:1, 0:KC], axis=AX.X), inc="V")
            p.do("vector", lambda E: E.reduce_sum(bnd[0:1, 1:2], ps[5][0:1, 0:KC], axis=AX.X), inc="V")
            p.ss("vector")
            p.do("vector", lambda E: E.tensor_scalar_mul(bnd[0:1, 0:1], bnd[0:1, 0:1], 1.0 / D), inc="V")
            p.do("vector", lambda E: E.tensor_scalar_mul(bnd[0:1, 1:2], bnd[0:1, 1:2], 1.0 / D), inc="V")
            p.ss("vector")
            p.do("vector", lambda E: E.tensor_mul(bnd[0:1, 2:3], bnd[0:1, 0:1], bnd[0:1, 0:1]), inc="V")
            p.ss("vector")
            p.do("vector", lambda E: E.tensor_sub(bnd[0:1, 2:3], bnd[0:1, 1:2], bnd[0:1, 2:3]), inc="V")
            vb = ctx.ctr["V"]
            p.w("scalar", "V", vb)
            row_recip_sqrt(ctx, p, bnd[0:1, 3:4], bnd[0:1, 2:3], LN_EPS, half=True)
            p.w("tensor", "S", ctx.ctr["S"])
            p.w("tensor", "V", ctx.ctr["V"])
            p.do("tensor", lambda E: E.matmul(ps[6][:, 0:1], ctx.ten["ones_row_f32"][:, :],
                                              bnd[0:1, 0:1], start=True, stop=True), inc="T")
            p.do("tensor", lambda E: E.matmul(ps[6][:, 1:2], ctx.ten["ones_row_f32"][:, :],
                                              bnd[0:1, 3:4], start=True, stop=True), inc="T")
            p.w("vector", "T", ctx.ctr["T"])
            p.do("vector", lambda E: E.tensor_copy(bnd[:, 4:6], ps[6][:, 0:2]), inc="V")
            p.ss("vector")
            if SNAP_MODE and which == "ffn":
                dbnd = nc.dram_tensor(f"snap_lnb{ctx.ctr['T']}", [128, 16], F32, kind="ExternalOutput")
                p.w("sync", "V", ctx.ctr["V"] + 2 + KC)
                p.do("sync", lambda E, d=dbnd: E.dma_start(out=d[:, :], in_=bnd[:, :]), inc="Y", n=16)
            p.do("vector", lambda E: E.tensor_scalar(
                tf32[0][:, 0:KC], bsrc, bnd[:, 4:5], bnd[:, 5:6],
                ALU.subtract, ALU.mult), inc="V")
            p.ss("vector")
            p.do("vector", lambda E: E.tensor_scalar_mul(
                tf32[0][:, 0:KC], tf32[0][:, 0:KC],
                sv[:, SV["coremask"]:SV["coremask"] + 1]), inc="V")
            p.ss("vector")
            p.w("vector", "V", ctx.ctr["V"])
            for c in range(KC):
                p.do("vector", lambda E, c=c: E.tensor_copy(xch[c][:, 0:1], tf32[0][:, c:c + 1]), inc="V")

        vbfin = ctx.ctr["V"]
        vap = {}
        for nb in range(NB):
            bA = ps[nb % 2]
            bB = ps[2 + nb % 2]
            p.w("tensor", "V", vbfin)
            if nb >= 2:
                p.w("tensor", "V", vap[nb - 2])
            p.do("tensor", lambda E, nb=nb, bA=bA: E.matmul(
                bA[:, :], ones_row[:, :], rrow[0:1, nb * 512:(nb + 1) * 512],
                start=True, stop=True), inc="T")
            p.do("tensor", lambda E, nb=nb, bB=bB: E.matmul(
                bB[:, :], ones_row[:, :], brow[0:1, nb * 512:(nb + 1) * 512],
                start=True, stop=True), inc="T")
            tb = ctx.ctr["T"]
            p.w("vector", "T", tb)
            for c in range(KC):
                t = tf32[c % 2]
                p.do("vector", lambda E, c=c, nb=nb, bA=bA, t=t: E.tensor_mul(
                    t[:, :], xch[c][:, 1 + nb * 512:1 + (nb + 1) * 512], bA[:, :]), inc="V")
                p.ss("vector")
                if which == "ffn":
                    p.do("vector", lambda E, c=c, nb=nb, bB=bB, t=t: E.tensor_add(
                        xch[c][:, 1 + nb * 512:1 + (nb + 1) * 512], t[:, :], bB[:, :]), inc="V")
                else:
                    p.do("vector", lambda E, c=c, nb=nb, bB=bB, t=t: E.tensor_add(
                        xhat[c][:, nb * 512:(nb + 1) * 512], t[:, :], bB[:, :]), inc="V")
            vap[nb] = ctx.ctr["V"]
        p.close()



def ph_tables(ctx, tabs, qscol8, qscol8s, kscol, kscols):
    nc, J = ctx.nc, ctx.J
    ps = ctx.psum
    sv, SV = ctx.ten["sv"], ctx.ten["SV"]
    PI = float(np.pi)
    import contextlib
    with contextlib.ExitStack() as st:
        wsin = st.enter_context(_sb(nc, "tb_sin", [128, 512], F32))
        wcos = st.enter_context(_sb(nc, "tb_cos", [128, 512], F32))
        wsc = st.enter_context(_sb(nc, "tb_scale", [128, 512], F32))
        wtmp = st.enter_context(_sb(nc, "tb_tmp", [128, 512], F32))
        wint = st.enter_context(_sb(nc, "tb_int", [128, 512], mybir.dt.int32))
        wkf = st.enter_context(_sb(nc, "tb_kf", [128, 512], F32))
        wd = st.enter_context(_sb(nc, "tb_d", [128, 512], F32))
        rowsb = st.enter_context(_sb(nc, "tb_rows", [1, 2 * T], F32))
        colsb = st.enter_context(_sb(nc, "tb_cols", [1, 512], F32))

        p = Phase(ctx, "tables")
        J.join(p)
        p.do("gpsimd", lambda E: E.iota(colsb[0:1, 0:128], [[0, 4], [1, 32]],
                                        channel_multiplier=0,
                                        allow_small_or_imprecise_dtypes=True), inc="G")
        p.do("gpsimd", lambda E: E.iota(rowsb[0:1, 0:T], [[1, T]],
                                        channel_multiplier=0,
                                        allow_small_or_imprecise_dtypes=True), inc="G")
        g0 = ctx.ctr["G"]
        p.w("vector", "G", g0)
        p.do("vector", lambda E: E.tensor_scalar_add(rowsb[0:1, 0:T], rowsb[0:1, 0:T],
                                                     sv[0:1, SV["toff"]:SV["toff"] + 1]), inc="V")
        p.do("vector", lambda E: E.tensor_scalar(rowsb[0:1, T:2 * T], rowsb[0:1, 0:T],
                                                 -(N_FULL // 2) * 1.0, 1.0 / 256.0,
                                                 ALU.add, ALU.mult), inc="V")
        p.do("vector", lambda E: E.tensor_scalar(colsb[0:1, 256:384], colsb[0:1, 0:128],
                                                 12.8, 1.0 / 44.8, ALU.add, ALU.mult), inc="V")
        vr = ctx.ctr["V"]
        p.w("scalar", "V", vr)
        p.do("scalar", lambda E: E.activation(colsb[0:1, 128:256], colsb[0:1, 0:128], AF.Exp,
                                              scale=-float(np.log(10000.0) / 32.0)), inc="S")
        p.do("scalar", lambda E: E.activation(colsb[0:1, 256:384], colsb[0:1, 256:384], AF.Ln), inc="S")
        scols = ctx.ctr["S"]
        p.w("tensor", "S", scols)

        slast = {}
        for nb in range(T // 512):
            sl = slice(nb * 512, (nb + 1) * 512)
            pph, ppw = ps[nb % 2], ps[2 + nb % 2]
            if nb >= 2:
                p.w("tensor", "V", slast[nb - 2][0])
                p.w("tensor", "S", slast[nb - 2][1])
            p.do("tensor", lambda E, sl=sl, pph=pph: E.matmul(
                pph[:, :], colsb[0:1, 128:256], rowsb[0:1, sl], start=True, stop=True), inc="T")
            p.do("tensor", lambda E, sl=sl, ppw=ppw: E.matmul(
                ppw[:, :], colsb[0:1, 256:384], rowsb[0:1, T + sl.start:T + sl.stop],
                start=True, stop=True), inc="T")
            tmm = ctx.ctr["T"]
            p.w("scalar", "T", tmm)
            if nb >= 1:
                p.w("scalar", "V", slast[nb - 1][0])
            p.do("scalar", lambda E, ppw=ppw: E.activation(wsc[:, :], ppw[:, :], AF.Exp), inc="S")
            p.w("vector", "T", tmm)
            def rr(shift_cycles, shift_rad, dst):
                p.w("vector", "S", ctx.ctr["S"])
                p.do("vector", lambda E, pph=pph, s=shift_cycles: E.tensor_scalar(
                    wtmp[:, :], pph[:, :], 1.0 / (2 * PI), 0.5 + s, ALU.mult, ALU.add), inc="V")
                p.ss("vector")
                p.do("vector", lambda E: E.tensor_copy(wint[:, :], wtmp[:, :]), inc="V")
                p.ss("vector")
                p.do("vector", lambda E: E.tensor_copy(wkf[:, :], wint[:, :]), inc="V")
                p.ss("vector")
                p.do("vector", lambda E: E.tensor_tensor(wd[:, :], wkf[:, :], wtmp[:, :],
                                                         ALU.is_gt), inc="V")
                p.ss("vector")
                p.do("vector", lambda E: E.tensor_sub(wkf[:, :], wkf[:, :], wd[:, :]), inc="V")
                p.ss("vector")
                p.do("vector", lambda E, pph=pph: E.scalar_tensor_tensor(
                    wtmp[:, :], wkf[:, :], -2 * PI, pph[:, :], ALU.mult, ALU.add), inc="V")
                p.ss("vector")
                p.do("vector", lambda E, sr=shift_rad: E.tensor_scalar(
                    wtmp[:, :], wtmp[:, :], sr, PI, ALU.add, ALU.min), inc="V")
                p.ss("vector")
                p.do("vector", lambda E: E.tensor_scalar_max(wtmp[:, :], wtmp[:, :], -PI), inc="V")
                vv = ctx.ctr["V"]
                p.w("scalar", "V", vv)
                p.do("scalar", lambda E, dst=dst: E.activation(dst[:, :], wtmp[:, :], AF.Sin), inc="S")
            rr(0.0, 0.0, wsin)
            rr(0.25, PI / 2, wcos)
            ssin = ctx.ctr["S"]
            p.w("vector", "S", ssin)
            p.do("vector", lambda E: E.tensor_mul(wtmp[:, :], wcos[:, :], wsc[:, :]), inc="V")
            p.ss("vector")
            p.do("vector", lambda E, sl=sl: E.tensor_scalar_mul(
                tabs["cs_q"][:, sl], wtmp[:, :], qscol8), inc="V")
            p.ss("vector")
            p.do("vector", lambda E: E.tensor_mul(wtmp[:, :], wsin[:, :], wsc[:, :]), inc="V")
            p.ss("vector")
            p.do("vector", lambda E, sl=sl: E.tensor_scalar_mul(
                tabs["ss_q"][:, sl], wtmp[:, :], qscol8s), inc="V")
            vq = ctx.ctr["V"]
            p.w("scalar", "V", vq)
            p.do("scalar", lambda E, ppw=ppw: E.activation(wsc[:, :], ppw[:, :], AF.Exp,
                                                           scale=-1.0), inc="S")
            sinv = ctx.ctr["S"]
            p.w("vector", "S", sinv)
            p.do("vector", lambda E: E.tensor_mul(wtmp[:, :], wcos[:, :], wsc[:, :]), inc="V")
            p.ss("vector")
            p.do("vector", lambda E, sl=sl: E.tensor_scalar_mul(
                tabs["cs_k"][:, sl], wtmp[:, :], kscol), inc="V")
            p.ss("vector")
            p.do("vector", lambda E: E.tensor_mul(wtmp[:, :], wsin[:, :], wsc[:, :]), inc="V")
            p.ss("vector")
            p.do("vector", lambda E, sl=sl: E.tensor_scalar_mul(
                tabs["ss_k"][:, sl], wtmp[:, :], kscols), inc="V")
            slast[nb] = (ctx.ctr["V"], sinv)
        p.close()



def coll_issue(ctx, p, pairs, cc_in, cc_out, dma_eng="sync", waits=()):
    for s, v in waits:
        p.w(dma_eng, s, v)
    for dst, srcap in pairs:
        p.do(dma_eng, lambda E, dst=dst, s=srcap: E.dma_start(
            out=dst, in_=s, allow_slow_non_contiguous=True), inc="CI", n=16)
    p.w("gpsimd", "CI", ctx.ctr["CI"])
    p.do("gpsimd", lambda E: E.collective_compute(
        "AllGather", ALU.bypass,
        replica_groups=[[0, 1], [2, 3], [4, 5], [6, 7]],
        ins=[cc_in.ap().opt()], outs=[cc_out.ap().opt()]), inc="CC")
    return ctx.ctr["CC"]


def coll_recv(ctx, p, ccval, pairs, dma_eng="sync"):
    p.w(dma_eng, "CC", ccval)
    for dst, srcap in pairs:
        p.do(dma_eng, lambda E, d=dst, s=srcap: E.dma_start(out=d, in_=s),
             inc="CR", n=16)
    return ctx.ctr["CR"]



def _wsrc(ctx, name, l, first_call):
    if first_call:
        return ctx.dram[f"{name}{l}"][:, :]
    return ctx.dram[f"stash_{name}"][l, :, :]


QC_TILES = {
    0: [(0, True), (1, True), (2, False), (3, False), (4, True), (5, True)],
    1: [(2, True), (3, True), (4, False), (5, False), (6, True), (7, True)],
}


def ph_ffn_qkv(ctx, l, first_call, fl):
    nc, J = ctx.nc, ctx.J
    ps = ctx.psum
    xch = ctx.ten["xch"]
    dram = ctx.dram
    import contextlib

    qf, kf, vT = fl["qf"], fl["kf"], fl["vT"]
    tabs = fl["tabs"]
    pmat, halfsel = ctx.ten["pmat"], ctx.ten["halfsel"]

    with contextlib.ExitStack() as st:
        xfb = [st.enter_context(_sb(nc, f"xfb{c}", [128, T + 1], BF16))
               for c in range(KC)]
        wbuf = [st.enter_context(_sb(nc, f"qw{i}", [128, KC * 256], BF16))
                for i in range(2)]
        sqb = [st.enter_context(_sb(nc, f"sqb{i}", [128, 512], BF16)) for i in range(2)]
        rinv = st.enter_context(_sb(nc, "rinv", [2, 4 * 512], F32))
        rtmp = [st.enter_context(_sb(nc, f"rtmp{i}", [128, 512], BF16)) for i in range(2)]

        p = Phase(ctx, f"qkv{fl['tag']}")
        J.join(p)
        if fl.get("snapb") is not None:
            for c in range(KC):
                p.do("sync", lambda E, c=c, d=fl["snapb"]: E.dma_start(
                    out=d[:, c:c + 1], in_=xch[c][:, 0:1],
                    allow_slow_non_contiguous=True), inc="Y", n=16)

        p.do("vector", lambda E: E.memset(
            vT[:, :].rearrange("p (t h d) -> p t h d", t=16, h=8)[:, :, :, 64:65], 1.0), inc="V")
        for c in range(KC):
            p.do("vector", lambda E, c=c: E.tensor_copy(xfb[c][:, :], xch[c][:, :]), inc="V")
        vcast = ctx.ctr["V"]

        wq = _wsrc(ctx, "wqkv", l, first_call)
        weng = "gpsimd" if first_call else "sync"
        wsem = "GD" if first_call else "Y"

        mmdone = {}
        stash_done = {}
        for g in range(6):
            buf = wbuf[g % 2]
            if g >= 2:
                p.w(weng, "T", mmdone[g - 2])
                if first_call:
                    p.w(weng, f"WS{g % 2}", stash_done[g - 2])
            wsm = f"WQ{g % 2}"
            p.do(weng, lambda E, g=g, buf=buf, wq=wq: E.dma_start(
                out=buf[:, :].rearrange("p (c m) -> p c m", c=KC),
                in_=wq.rearrange("(c p) m -> p c m", p=128)[:, :, 256 * g:256 * (g + 1)]),
                inc=wsm, n=16)
            ld = ctx.ctr[wsm]
            if first_call:
                p.w("sync", wsm, ld)
                p.do("sync", lambda E, g=g, buf=buf: E.dma_start(
                    out=dram["stash_wqkv"][l, :, :].rearrange("(c p) m -> p c m", p=128)[:, :, 256 * g:256 * (g + 1)],
                    in_=buf[:, :].rearrange("p (c m) -> p c m", c=KC)), inc=f"WS{g % 2}", n=16)
            p.w("tensor", wsm, ld)
            if g == 0:
                p.w("tensor", "V", vcast)
            if g < 4:
                for mt in range(2):
                    m = g * 2 + mt
                    dst = qf[m] if m < 4 else kf[m - 4]
                    for w in range(NWIN):
                        bid = 4 + (2 * (w % 2) + mt)
                        bank = ps[bid]
                        if bid in fl.setdefault("qkv_bdone", {}):
                            p.w("tensor", "V", fl["qkv_bdone"][bid])
                        for c in range(KC):
                            roff = (1 if c < 4 else 0) + w * 512
                            p.do("tensor", lambda E, c=c, buf=buf, bank=bank, mt=mt, roff=roff: E.matmul(
                                bank[:, :], buf[:, 256 * c + 128 * mt:256 * c + 128 * mt + 128],
                                xfb[c][:, roff:roff + 512],
                                start=(c == 0), stop=(c == KC - 1)), inc="T")
                        tm = ctx.ctr["T"]
                        p.w("vector", "T", tm)
                        p.do("vector", lambda E, dst=dst, w=w, bank=bank: E.tensor_copy(
                            dst[:, w * 512:(w + 1) * 512], bank[:, :]), inc="V")
                        fl["qkv_bdone"][bid] = ctx.ctr["V"]
            else:
                mmdone[g] = ctx.ctr["T"]
            mmdone[g] = ctx.ctr["T"]
            if first_call:
                stash_done[g] = ctx.ctr[f"WS{g % 2}"]

        vevac = {}
        for tt in range(16):
            bank = ps[tt % 2]
            if tt >= 2:
                p.w("tensor", "V", vevac[tt - 2])
            for half in range(2):
                buf = wbuf[half]
                for c in range(KC):
                    roff = (1 if c < 4 else 0) + tt * 128
                    p.do("tensor", lambda E, c=c, buf=buf, bank=bank, half=half, roff=roff: E.matmul(
                        bank[:, 256 * half:256 * (half + 1)],
                        xfb[c][:, roff:roff + 128],
                        buf[:, 256 * c:256 * (c + 1)],
                        start=(c == 0), stop=(c == KC - 1)), inc="T")
            tvv = ctx.ctr["T"]
            p.w("vector", "T", tvv)
            p.do("vector", lambda E, tt=tt, bank=bank: E.tensor_copy(
                vT[:, tt * 520:(tt + 1) * 520].rearrange("p (h d) -> p h d", h=8)[:, :, 0:64],
                bank[:, :].rearrange("p (h d) -> p h d", h=8)), inc="V")
            vevac[tt] = ctx.ctr["V"]

        def norm_rotary(side):
            src4 = qf if side == "q" else kf
            cs = tabs["cs_q" if side == "q" else "cs_k"]
            ss = tabs["ss_q" if side == "q" else "ss_k"]
            for w in range(NWIN):
                sl = slice(w * 512, (w + 1) * 512)
                p.w("tensor", "V", ctx.ctr["V"])
                p.w("tensor", "S", ctx.ctr["S"])
                p.w("scalar", "T", ctx.ctr["T"])
                stat_t = {}
                for c in range(4):
                    sq = sqb[c % 2]
                    if c >= 2:
                        p.w("scalar", "T", stat_t[c - 2])
                    p.do("scalar", lambda E, c=c, sq=sq, src4=src4, sl=sl: E.activation(
                        sq[:, :], src4[c][:, sl], AF.Square), inc="S")
                    p.w("tensor", "S", ctx.ctr["S"])
                    p.do("tensor", lambda E, c=c, sq=sq: E.matmul(
                        ps[c][0:2, :], ctx.ten["seg2"][:, :], sq[:, :],
                        start=True, stop=True), inc="T")
                    stat_t[c] = ctx.ctr["T"]
                tstat = ctx.ctr["T"]
                p.w("scalar", "T", tstat)
                for c in range(4):
                    p.do("scalar", lambda E, c=c: E.activation(
                        rinv[0:2, c * 512:(c + 1) * 512], ps[c][0:2, :], AF.Ln,
                        bias=1e-24), inc="S")
                p.ss("scalar")
                for c in range(4):
                    p.do("scalar", lambda E, c=c: E.activation(
                        rinv[0:2, c * 512:(c + 1) * 512], rinv[0:2, c * 512:(c + 1) * 512],
                        AF.Exp, scale=-0.5), inc="S")
                srinv = ctx.ctr["S"]
                bdn = fl.setdefault("nr_bdone", {})
                for c in range(4):
                    bbid, pbid = 4 + c % 2, 6 + c % 2
                    bb, pb = ps[bbid], ps[pbid]
                    if pbid in bdn:
                        p.w("tensor", "V", bdn[pbid])
                    p.do("tensor", lambda E, c=c, pb=pb, src4=src4, sl=sl: E.matmul(
                        pb[:, :], pmat[:, :], src4[c][:, sl],
                        start=True, stop=True), inc="T")
                    tp = ctx.ctr["T"]
                    if bbid in bdn:
                        p.w("tensor", "V", bdn[bbid])
                    p.w("tensor", "S", srinv)
                    p.do("tensor", lambda E, c=c, bb=bb: E.matmul(
                        bb[:, :], halfsel[:, :], rinv[0:2, c * 512:(c + 1) * 512],
                        start=True, stop=True), inc="T")
                    tb = ctx.ctr["T"]
                    p.do("vector", lambda E, c=c, src4=src4, sl=sl, cs=cs: E.tensor_mul(
                        rtmp[0][:, :], src4[c][:, sl], cs[:, sl]), inc="V")
                    p.w("vector", "T", tp)
                    p.do("vector", lambda E, c=c, pb=pb, sl=sl, ss=ss: E.tensor_mul(
                        rtmp[1][:, :], pb[:, :], ss[:, sl]), inc="V")
                    bdn[pbid] = ctx.ctr["V"]
                    p.do("vector", lambda E: E.tensor_add(
                        rtmp[0][:, :], rtmp[0][:, :], rtmp[1][:, :]), inc="V")
                    p.w("vector", "T", tb)
                    p.do("vector", lambda E, c=c, bb=bb, src4=src4, sl=sl: E.tensor_mul(
                        src4[c][:, sl], rtmp[0][:, :], bb[:, :]), inc="V")
                    bdn[bbid] = ctx.ctr["V"]

        if fl.get("raw_tap"):
            p.close()
            return
        norm_rotary("q")
        norm_rotary("k")

        pairs = []
        for c in range(4):
            pairs.append((dram["c2_in"][:, c * 512:(c + 1) * 512], kf[c][:, 3 * 512:4 * 512]))
        pairs.append((dram["c2_in"][:, 4 * 512:4 * 512 + 4 * 520], vT[:, 12 * 520:16 * 520]))
        fl["c2val"] = coll_issue(ctx, p, pairs, dram["c2_in"], dram["c2_out"],
                                 waits=[("V", ctx.ctr["V"])])
        p.close()


def ph_ffn_attn(ctx, l, first_call, fl):
    nc, J = ctx.nc, ctx.J
    ps = ctx.psum
    xch = ctx.ten["xch"]
    dram = ctx.dram
    m_top, m_top0, m_tril = ctx.ten["m_top"], ctx.ten["m_top0"], ctx.ten["m_tril"]
    import contextlib

    qf, kf, vT = fl["qf"], fl["kf"], fl["vT"]
    kh, vTh = fl["kh"], fl["vTh"]

    with contextlib.ExitStack() as st:
        wo_sb = [st.enter_context(_sb(nc, f"wosb{i}", [128, 1024], BF16))
                 for i in range(4)]
        attn_sb = st.enter_context(_sb(nc, "attnsb", [128, 2048], BF16))
        Ow = [st.enter_context(_sb(nc, f"Ow{i}", [128, 512], BF16)) for i in range(4)]
        rdrow = st.enter_context(_sb(nc, "rdrow", [1, 512], BF16))
        rdf32 = st.enter_context(_sb(nc, "rdf32", [1, 512], F32))
        onum = st.enter_context(_sb(nc, "onum", [64, 256], BF16))

        p = Phase(ctx, f"fattn{fl['tag']}")
        J.join(p)

        wsrcap = _wsrc(ctx, "wo", l, first_call)
        weng = "gpsimd" if first_call else "sync"
        wsem = "GD" if first_call else "Y"
        for ic in range(4):
            p.do(weng, lambda E, ic=ic, w=wsrcap: E.dma_start(
                out=wo_sb[ic][:, :], in_=w[ic * 128:(ic + 1) * 128, :]),
                inc="WO", n=16)
        ldwo = ctx.ctr["WO"]
        if first_call:
            p.w("sync", "WO", ldwo)
            for ic in range(4):
                p.do("sync", lambda E, ic=ic: E.dma_start(
                    out=dram["stash_wo"][l, :, :][ic * 128:(ic + 1) * 128, :],
                    in_=wo_sb[ic][:, :]), inc="Y", n=16)
        p.w("tensor", "WO", ldwo)

        c2done = False
        sc_done = {}
        mk_done = {}
        odone = {}
        worder = fl["dbg_w1"] if fl.get("dbg_w1") else [1, 2, 3, 0]
        for wi, w in enumerate(worder):
            if w == 0 and not c2done:
                rp = [(kh[c][:, :], dram["c2_out"][0:128, c * 512:(c + 1) * 512]) for c in range(4)]
                rp.append((vTh[:, :], dram["c2_out"][0:128, 4 * 512:4 * 512 + 4 * 520]))
                rv = coll_recv(ctx, p, fl["c2val"], rp)
                p.w("tensor", "CR", rv)
                p.w("vector", "CR", rv)
                c2done = True
            for qc in range(2):
                qsl = slice(w * 512 + qc * 256, w * 512 + qc * 256 + 256)
                tiles = [(kt, (need or (w == 0 and kt < 4))) for kt, need in QC_TILES[qc]]
                koff = 0 if qc == 0 else 2
                p.w("tensor", "S", ctx.ctr["S"])
                p.w("tensor", "V", ctx.ctr["V"])
                vps = ctx.ctr["S"]
                for h in range(H):
                    hc, hh = h // 2, h % 2
                    if h >= 1:
                        p.w("tensor", "S", odone[h - 1][0])
                        p.w("tensor", "V", odone[h - 1][1])
                    for (kt, _need) in tiles:
                        bank = ps[kt - koff]
                        if kt < 4:
                            if w == 0:
                                lhs = kh[hc][hh * 64:hh * 64 + 64, kt * 128:(kt + 1) * 128]
                            else:
                                lhs = kf[hc][hh * 64:hh * 64 + 64,
                                             (w - 1) * 512 + kt * 128:(w - 1) * 512 + (kt + 1) * 128]
                        else:
                            lhs = kf[hc][hh * 64:hh * 64 + 64,
                                         w * 512 + (kt - 4) * 128:w * 512 + (kt - 3) * 128]
                        if _need:
                            if kt < 4:
                                msk = m_top0 if w == 0 else m_top
                                mslice = msk[:, kt * 512 + qc * 256:kt * 512 + qc * 256 + 256]
                            else:
                                mslice = m_tril[:, (kt - 4) * 512 + qc * 256:(kt - 4) * 512 + qc * 256 + 256]
                            p.w("vector", "S", sc_done.get((kt, h - 1), vps))
                            p.do("vector", lambda E, bank=bank, mslice=mslice: E.tensor_copy(
                                bank[:, 0:256], mslice), inc="V")
                            mk = ctx.ctr["V"]
                            p.w("tensor", "V", mk)
                            p.do("tensor", lambda E, lhs=lhs, bank=bank, qsl=qsl, hc=hc, hh=hh: E.matmul(
                                bank[:, 0:256], lhs,
                                qf[hc][hh * 64:hh * 64 + 64, qsl],
                                start=False, stop=True, skip_group_check=True), inc="T")
                        else:
                            if h > 0:
                                p.w("tensor", "S", sc_done[(kt, h - 1)])
                            p.do("tensor", lambda E, lhs=lhs, bank=bank, qsl=qsl, hc=hc, hh=hh: E.matmul(
                                bank[:, 0:256], lhs,
                                qf[hc][hh * 64:hh * 64 + 64, qsl],
                                start=True, stop=True), inc="T")
                        tsc = ctx.ctr["T"]
                        p.w("scalar", "T", tsc)
                        p.do("scalar", lambda E, bank=bank, kt=kt: E.activation(
                            attn_sb[:, kt * 256:(kt + 1) * 256], bank[:, 0:256],
                            AF.Exp, bias=-8.0), inc="S")
                        sc_done[(kt, h)] = ctx.ctr["S"]
                    sexp = ctx.ctr["S"]
                    p.w("tensor", "S", sexp)
                    ob = ps[6]
                    for i, (kt, _need) in enumerate(tiles):
                        if w == 0 and kt < 4:
                            vtile = vTh[:, kt * 520 + h * 65:kt * 520 + h * 65 + 65]
                        elif kt < 4:
                            tt = (w - 1) * 4 + kt
                            vtile = vT[:, tt * 520 + h * 65:tt * 520 + h * 65 + 65]
                        else:
                            tt = w * 4 + (kt - 4)
                            vtile = vT[:, tt * 520 + h * 65:tt * 520 + h * 65 + 65]
                        p.do("tensor", lambda E, vtile=vtile, ob=ob, kt=kt, i=i, n=len(tiles): E.matmul(
                            ob[0:65, 0:256], vtile, attn_sb[:, kt * 256:(kt + 1) * 256],
                            start=(i == 0), stop=(i == n - 1)), inc="T")
                    tav = ctx.ctr["T"]
                    p.w("scalar", "T", tav)
                    p.do("scalar", lambda E, ob=ob: E.activation(
                        rdf32[0:1, 0:256], ob[64:65, 0:256], AF.Ln, bias=1e-30), inc="S")
                    p.ss("scalar")
                    p.do("scalar", lambda E: E.activation(
                        rdrow[0:1, 0:256], rdf32[0:1, 0:256], AF.Exp, scale=-1.0), inc="S")
                    srd = ctx.ctr["S"]
                    p.w("tensor", "S", srd)
                    rb = ps[7]
                    p.do("tensor", lambda E, rb=rb: E.matmul(
                        rb[0:64, 0:256], ctx.ten["ones_row"][:, 0:64], rdrow[0:1, 0:256],
                        start=True, stop=True), inc="T")
                    trb = ctx.ctr["T"]
                    p.do("scalar", lambda E, ob=ob: E.copy(onum[0:64, 0:256], ob[0:64, 0:256]),
                         inc="S")
                    snum = ctx.ctr["S"]
                    p.w("vector", "T", trb)
                    p.w("vector", "S", snum)
                    p.do("vector", lambda E, rb=rb, hc=hc, hh=hh, qc=qc: E.tensor_mul(
                        Ow[hc][hh * 64:hh * 64 + 64, qc * 256:qc * 256 + 256],
                        onum[0:64, 0:256], rb[0:64, 0:256]), inc="V")
                    odone[h] = (snum, ctx.ctr["V"])
            vdiv = ctx.ctr["V"]
            p.w("tensor", "V", vdiv)
            p.w("tensor", "S", ctx.ctr["S"])
            evd = {}
            for m in range(8):
                bank = ps[m % 4]
                if m >= 4:
                    p.w("tensor", "V", evd[m - 4])
                for ic in range(4):
                    p.do("tensor", lambda E, m=m, ic=ic, bank=bank: E.matmul(
                        bank[:, :], wo_sb[ic][:, m * 128:(m + 1) * 128],
                        Ow[ic][:, :],
                        start=(ic == 0), stop=(ic == 3)), inc="T")
                two = ctx.ctr["T"]
                p.w("vector", "T", two)
                p.do("vector", lambda E, m=m, bank=bank, w=w: E.tensor_add(
                    xch[m][:, 1 + w * 512:1 + (w + 1) * 512],
                    xch[m][:, 1 + w * 512:1 + (w + 1) * 512],
                    bank[:, :]), inc="V")
                evd[m] = ctx.ctr["V"]
            p.w("tensor", "V", ctx.ctr["V"])
            if w == 3 and not fl.get("dbg_w1"):
                pairs = [(dram["c1_in"][:, c:c + 1], xch[c][:, T:T + 1]) for c in range(KC)]
                fl["c3val"] = coll_issue(ctx, p, pairs, dram["c1_in"], dram["c1_out"],
                                         waits=[("V", ctx.ctr["V"])])
        if fl.get("dbg_w1"):
            douts = {"dbg_ow": nc.dram_tensor("dbg_ow", [4 * 128, 512], BF16, kind="ExternalOutput"),
                     "dbg_asb": nc.dram_tensor("dbg_asb", [128, 2048], BF16, kind="ExternalOutput")}
            p.w("sync", "V", ctx.ctr["V"])
            p.w("sync", "S", ctx.ctr["S"])
            for ic in range(4):
                p.do("sync", lambda E, ic=ic: E.dma_start(
                    out=douts["dbg_ow"][ic * 128:(ic + 1) * 128, :], in_=Ow[ic][:, :]), inc="Y", n=16)
            p.do("sync", lambda E: E.dma_start(out=douts["dbg_asb"][:, :], in_=attn_sb[:, :]), inc="Y", n=16)
            p.w("sync", "Y", ctx.ctr["Y"])
            p.w("sync", "CC", ctx.ctr["CC"])
        p.close()


def ph_ffn_mlp(ctx, l, first_call, fl):
    nc, J = ctx.nc, ctx.J
    ps = ctx.psum
    xch = ctx.ten["xch"]
    sv, SV = ctx.ten["sv"], ctx.ten["SV"]
    dram = ctx.dram
    ones_col = ctx.ten["ones_col"]
    ones_row_f32 = ctx.ten["ones_row_f32"]
    import contextlib

    with contextlib.ExitStack() as st:
        xhat = [st.enter_context(_sb(nc, f"xh2_{c}", [128, T], BF16)) for c in range(KC)]
        hbuf = st.enter_context(_sb(nc, "hbuf", [128, NPAIR * 1024], BF16))
        w1b = [st.enter_context(_sb(nc, f"w1b{i}", [128, KC * 256], BF16)) for i in range(2)]
        w2b = [st.enter_context(_sb(nc, f"w2b{i}", [128, NPAIR * 128], BF16)) for i in range(2)]
        gelub = [st.enter_context(_sb(nc, f"gel{i}", [128, 512], BF16)) for i in range(2)]
        tmpb = [st.enter_context(_sb(nc, f"mt{i}", [128, 512], BF16)) for i in range(4)]
        murow = st.enter_context(_sb(nc, "m_mu", [1, T], F32))
        ssrow = st.enter_context(_sb(nc, "m_ss", [1, T], F32))
        tmprow = st.enter_context(_sb(nc, "m_tr", [1, T], F32))
        rrow = ssrow
        brow = murow
        tf32 = [st.enter_context(_sb(nc, f"mf{i}", [128, 512], F32)) for i in range(2)]
        bndb = st.enter_context(_sb(nc, "m_bnd", [128, KC], F32))
        b1c = st.enter_context(_sb(nc, "b1c", [128, 2 * NPAIR], F32))

        p = Phase(ctx, f"mlp{fl['tag']}")
        J.join(p)

        rv = coll_recv(ctx, p, fl["c3val"], [(bndb[:, :], dram["c1_out"][0:128, :])])
        p.w("vector", "CR", rv)
        for c in range(KC):
            p.do("vector", lambda E, c=c: E.tensor_scalar_mul(
                xch[c][:, 0:1], bndb[:, c:c + 1],
                sv[:, SV["coremask"]:SV["coremask"] + 1]), inc="V")
        vbnd = ctx.ctr["V"]

        p.do("sync", lambda E: E.dma_start(out=b1c[:, :], in_=dram[f"b1p{l}"][:, :]), inc="WB", n=16)
        ldb1 = ctx.ctr["WB"]

        NB = T // 512
        tmm = {}
        p.w("scalar", "V", vbnd)
        p.w("vector", "V", vbnd)
        for nb in range(NB):
            for c in range(KC):
                i = (nb * KC + c) % 2
                buf, sq = tmpb[i], tmpb[2 + i]
                if nb * KC + c >= 2:
                    prev = tmm[nb * KC + c - 2]
                    p.w("vector", "T", prev)
                    p.w("scalar", "T", prev)
                roff = (1 if c < 4 else 0) + nb * 512
                p.do("vector", lambda E, c=c, buf=buf, roff=roff: E.tensor_copy(
                    buf[:, :], xch[c][:, roff:roff + 512]), inc="V")
                p.do("scalar", lambda E, c=c, sq=sq, roff=roff: E.activation(
                    sq[:, :], xch[c][:, roff:roff + 512], AF.Square), inc="S")
                p.w("tensor", "V", ctx.ctr["V"])
                p.w("tensor", "S", ctx.ctr["S"])
                p.do("tensor", lambda E, nb=nb, buf=buf, c=c: E.matmul(
                    ps[nb][0:1, :], ones_col[:, :], buf[:, :],
                    start=(c == 0), stop=(c == KC - 1)), inc="T")
                p.do("tensor", lambda E, nb=nb, sq=sq, c=c: E.matmul(
                    ps[4 + nb][0:1, :], ones_col[:, :], sq[:, :],
                    start=(c == 0), stop=(c == KC - 1)), inc="T")
                tmm[nb * KC + c] = ctx.ctr["T"]
        tstats = ctx.ctr["T"]
        p.w("vector", "T", tstats)
        for nb in range(NB):
            p.do("vector", lambda E, nb=nb: E.tensor_scalar_mul(
                murow[0:1, nb * 512:(nb + 1) * 512], ps[nb][0:1, :], 1.0 / D), inc="V")
            p.do("vector", lambda E, nb=nb: E.tensor_scalar_mul(
                ssrow[0:1, nb * 512:(nb + 1) * 512], ps[4 + nb][0:1, :], 1.0 / D), inc="V")
        p.do("vector", lambda E: E.tensor_mul(tmprow[0:1, :], murow[0:1, :], murow[0:1, :]), inc="V")
        p.do("vector", lambda E: E.tensor_sub(ssrow[0:1, :], ssrow[0:1, :], tmprow[0:1, :]), inc="V")
        vvar = ctx.ctr["V"]
        p.w("scalar", "V", vvar)
        row_recip_sqrt(ctx, p, rrow[0:1, :], ssrow[0:1, :], LN_EPS, half=True)
        srow = ctx.ctr["S"]
        p.w("vector", "S", srow)
        p.do("vector", lambda E: E.scalar_tensor_tensor(
            brow[0:1, :], murow[0:1, :], -1.0, rrow[0:1, :], ALU.mult, ALU.mult), inc="V")
        vab = ctx.ctr["V"]

        vap = {}
        for nb in range(NB):
            bA, bB = ps[nb % 2], ps[2 + nb % 2]
            p.w("tensor", "V", vab)
            if nb >= 2:
                p.w("tensor", "V", vap[nb - 2])
            p.do("tensor", lambda E, nb=nb, bA=bA: E.matmul(
                bA[:, :], ones_row_f32[:, :], rrow[0:1, nb * 512:(nb + 1) * 512],
                start=True, stop=True), inc="T")
            p.do("tensor", lambda E, nb=nb, bB=bB: E.matmul(
                bB[:, :], ones_row_f32[:, :], brow[0:1, nb * 512:(nb + 1) * 512],
                start=True, stop=True), inc="T")
            tb = ctx.ctr["T"]
            p.w("vector", "T", tb)
            for c in range(KC):
                t = tf32[c % 2]
                roff = (1 if c < 4 else 0) + nb * 512
                p.do("vector", lambda E, c=c, bA=bA, t=t, roff=roff: E.tensor_mul(
                    t[:, :], xch[c][:, roff:roff + 512], bA[:, :]), inc="V")
                p.ss("vector")
                p.do("vector", lambda E, c=c, bB=bB, t=t: E.tensor_add(
                    t[:, :], t[:, :], bB[:, :]), inc="V")
                p.ss("vector")
                p.do("vector", lambda E, c=c, nb=nb, t=t, l=l: E.tensor_scalar(
                    xhat[c][:, nb * 512:(nb + 1) * 512], t[:, :],
                    sv[:, SV[f"lng{l}"] + c:SV[f"lng{l}"] + c + 1],
                    sv[:, SV[f"lnb{l}"] + c:SV[f"lnb{l}"] + c + 1],
                    ALU.mult, ALU.add), inc="V")
            vap[nb] = ctx.ctr["V"]
        p.w("tensor", "V", ctx.ctr["V"])
        p.w("vector", "WB", ldb1)
        p.w("scalar", "WB", ldb1)
        if fl.get("mlpdbg"):
            dxh = nc.dram_tensor("dbg_xhat", [KC * 128, T], BF16, kind="ExternalOutput")
            dbn = nc.dram_tensor("dbg_bnd", [128, KC], F32, kind="ExternalOutput")
            drw = nc.dram_tensor("dbg_rows", [2, T], F32, kind="ExternalOutput")
            dx0 = nc.dram_tensor("dbg_x0", [128, KC], F32, kind="ExternalOutput")
            p.w("sync", "V", ctx.ctr["V"])
            for c in range(KC):
                p.do("sync", lambda E, c=c: E.dma_start(
                    out=dxh[c * 128:(c + 1) * 128, :], in_=xhat[c][:, :]), inc="Y", n=16)
                p.do("sync", lambda E, c=c: E.dma_start(
                    out=dx0[:, c:c + 1], in_=xch[c][:, 0:1],
                    allow_slow_non_contiguous=True), inc="Y", n=16)
            p.do("sync", lambda E: E.dma_start(out=dbn[:, :], in_=bndb[:, :]), inc="Y", n=16)
            p.do("sync", lambda E: E.dma_start(out=drw[0:1, :], in_=murow[0:1, :]), inc="Y", n=16)
            p.do("sync", lambda E: E.dma_start(out=drw[1:2, :], in_=ssrow[0:1, :]), inc="Y", n=16)

        wsrc1 = _wsrc(ctx, "w1p", l, first_call)
        wsrc2 = _wsrc(ctx, "w2", l, first_call)
        weng = "gpsimd" if first_call else "sync"
        wsem = "GD" if first_call else "Y"

        bank_user = {b: (0, vvar) for b in range(4, 8)}
        gel_user = {}
        for half in (1, 0):
            ta = half * 1024
            pairdone = {}
            w1stash = {}
            for pr in range(NPAIR):
                off, wp = _pair_off(pr)
                wcols = 2 * wp
                buf = w1b[pr % 2]
                if pr >= 2:
                    p.w(weng, "T", pairdone[pr - 2])
                    if first_call and half == 1:
                        p.w(weng, f"WS1{pr % 2}", w1stash[pr - 2])
                wsm = f"W1{pr % 2}"
                p.do(weng, lambda E, off=off, wcols=wcols, buf=buf, w=wsrc1: E.dma_start(
                    out=buf[:, 0:KC * wcols].rearrange("p (c m) -> p c m", m=wcols),
                    in_=w.rearrange("(c p) m -> p c m", p=128)[:, :, off:off + wcols]),
                    inc=wsm, n=16)
                ld = ctx.ctr[wsm]
                if first_call and half == 1:
                    p.w("sync", wsm, ld)
                    p.do("sync", lambda E, off=off, wcols=wcols, buf=buf: E.dma_start(
                        out=dram["stash_w1p"][l, :, :].rearrange("(c p) m -> p c m", p=128)[:, :, off:off + wcols],
                        in_=buf[:, 0:KC * wcols].rearrange("p (c m) -> p c m", m=wcols)),
                        inc=f"WS1{pr % 2}", n=16)
                p.w("tensor", wsm, ld)
                for nb in range(2):
                    bidx = (pr % 2) * 2 + nb * 4
                    ba = ps[bidx]
                    bg = ps[bidx + 1]
                    if bidx in bank_user:
                        p.w("tensor", "S", bank_user[bidx][0])
                        p.w("tensor", "V", bank_user[bidx][1])
                    tsl = slice(ta + nb * 512, ta + nb * 512 + 512)
                    for c in range(KC):
                        p.do("tensor", lambda E, c=c, buf=buf, ba=ba, wcols=wcols, wp=wp, tsl=tsl: E.matmul(
                            ba[0:wp, 0:512], buf[:, c * wcols:c * wcols + wp],
                            xhat[c][:, tsl], start=(c == 0), stop=(c == KC - 1)), inc="T")
                        p.do("tensor", lambda E, c=c, buf=buf, bg=bg, wcols=wcols, wp=wp, tsl=tsl: E.matmul(
                            bg[0:wp, 0:512], buf[:, c * wcols + wp:c * wcols + 2 * wp],
                            xhat[c][:, tsl], start=(c == 0), stop=(c == KC - 1)), inc="T")
                    tg = ctx.ctr["T"]
                    gb = gelub[nb]
                    p.w("scalar", "T", tg)
                    if nb in gel_user:
                        p.w("scalar", "V", gel_user[nb])
                    p.do("scalar", lambda E, bg=bg, gb=gb, pr=pr, wp=wp: E.activation(
                        gb[0:wp, :], bg[0:wp, 0:512], AF.Gelu,
                        bias=b1c[0:wp, 2 * pr + 1:2 * pr + 2]), inc="S")
                    sg = ctx.ctr["S"]
                    p.w("vector", "S", sg)
                    p.do("vector", lambda E, ba=ba, gb=gb, pr=pr, nb=nb, wp=wp: E.scalar_tensor_tensor(
                        hbuf[0:wp, pr * 1024 + nb * 512:pr * 1024 + nb * 512 + 512],
                        ba[0:wp, 0:512], b1c[0:wp, 2 * pr:2 * pr + 1], gb[0:wp, :],
                        ALU.add, ALU.mult), inc="V")
                    gel_user[nb] = ctx.ctr["V"]
                    bank_user[bidx] = (sg, ctx.ctr["V"])
                pairdone[pr] = ctx.ctr["T"]
                if first_call and half == 1:
                    w1stash[pr] = ctx.ctr[f"WS1{pr % 2}"]
            vh1 = ctx.ctr["V"]
            p.w("tensor", "V", vh1)
            p.w("tensor", "S", ctx.ctr["S"])
            mdone = {}
            w2bank = {}
            w2stash = {}
            for m in range(8):
                buf = w2b[m % 2]
                wsm = f"W2{m % 2}"
                if m >= 2:
                    p.w(weng, "T", mdone[m - 2])
                    if first_call and half == 1:
                        p.w(weng, f"WS2{m % 2}", w2stash[m - 2])
                p.do(weng, lambda E, m=m, buf=buf, w=wsrc2: E.dma_start(
                    out=buf[:, 0:21 * 128].rearrange("p (k c) -> p k c", c=128),
                    in_=w[0:21 * 128, :].rearrange("(k p) m -> p k m", p=128)[:, :, m * 128:(m + 1) * 128]),
                    inc=wsm, n=16)
                p.do(weng, lambda E, m=m, buf=buf, w=wsrc2: E.dma_start(
                    out=buf[0:42, 21 * 128:22 * 128],
                    in_=w[21 * 128:21 * 128 + 42, m * 128:(m + 1) * 128]),
                    inc=wsm, n=16)
                ld = ctx.ctr[wsm]
                if first_call and half == 1:
                    p.w("sync", wsm, ld)
                    p.do("sync", lambda E, m=m, buf=buf: E.dma_start(
                        out=dram["stash_w2"][l, :, :][0:21 * 128, :].rearrange("(k p) m -> p k m", p=128)[:, :, m * 128:(m + 1) * 128],
                        in_=buf[:, 0:21 * 128].rearrange("p (k c) -> p k c", c=128)), inc=f"WS2{m % 2}", n=16)
                    p.do("sync", lambda E, m=m, buf=buf: E.dma_start(
                        out=dram["stash_w2"][l, :, :][21 * 128:21 * 128 + 42, m * 128:(m + 1) * 128],
                        in_=buf[0:42, 21 * 128:22 * 128]), inc=f"WS2{m % 2}", n=16)
                p.w("tensor", wsm, ld)
                for nb in range(2):
                    bidx2 = 4 + (m * 2 + nb) % 4
                    bank = ps[bidx2]
                    if bidx2 in w2bank:
                        p.w("tensor", "V", w2bank[bidx2])
                    tsl = slice(ta + nb * 512, ta + nb * 512 + 512)
                    for k in range(NPAIR):
                        kw = 128 if k < 21 else 42
                        p.do("tensor", lambda E, k=k, kw=kw, buf=buf, bank=bank, nb=nb, ta=ta: E.matmul(
                            bank[:, :], buf[0:kw, k * 128:k * 128 + 128],
                            hbuf[0:kw, k * 1024 + nb * 512:k * 1024 + nb * 512 + 512],
                            start=(k == 0), stop=(k == NPAIR - 1)), inc="T")
                    tw = ctx.ctr["T"]
                    p.w("vector", "T", tw)
                    p.do("vector", lambda E, m=m, bank=bank, tsl=tsl, l=l: E.scalar_tensor_tensor(
                        xch[m][:, 1 + tsl.start:1 + tsl.stop],
                        bank[:, :], sv[:, SV[f"b2_{l}"] + m:SV[f"b2_{l}"] + m + 1],
                        xch[m][:, 1 + tsl.start:1 + tsl.stop],
                        ALU.add, ALU.add), inc="V")
                    w2bank[bidx2] = ctx.ctr["V"]
                    bank_user[bidx2] = (0, ctx.ctr["V"])
                mdone[m] = ctx.ctr["T"]
                if first_call and half == 1:
                    w2stash[m] = ctx.ctr[f"WS2{m % 2}"]
            if half == 1:
                pairs = [(dram["c1_in"][:, c:c + 1], xch[c][:, T:T + 1]) for c in range(KC)]
                fl["c1val"] = coll_issue(ctx, p, pairs, dram["c1_in"], dram["c1_out"],
                                         waits=[("V", ctx.ctr["V"])])
        p.close()


def ph_layer_prep(ctx, l, cols):
    nc, J = ctx.nc, ctx.J
    sv, SV = ctx.ten["sv"], ctx.ten["SV"]
    signcol = ctx.ten["signcol"]
    import contextlib
    with contextlib.ExitStack() as st:
        tmp = st.enter_context(_sb(nc, "lp_tmp", [128, 2], F32))
        p = Phase(ctx, f"lprep{l}")
        J.join(p)
        for nm, dst in (("qs", cols["q"]), ("ks", cols["k"])):
            dr = ctx.dram[f"{nm}{l}"]
            p.do("gpsimd", lambda E, d=dst, dr=dr: E.dma_start(out=d, in_=dr[:, :]), inc="LP", n=16)
        g = ctx.ctr["LP"]
        p.w("vector", "LP", g)
        for nm in ("q", "k"):
            src_c, dst_c = cols[nm], cols[nm + "p"]
            for s4 in range(4):
                o = 32 if s4 % 2 == 0 else -32
                p.do("vector", lambda E, s=src_c, d=dst_c, s4=s4, o=o: E.tensor_copy(
                    d[32 * s4:32 * (s4 + 1), :], s[32 * s4 + o:32 * (s4 + 1) + o, :]), inc="V")
        p.ss("vector")
        p.do("vector", lambda E: E.tensor_scalar_mul(cols["q"], cols["q"], 8.0), inc="V")
        p.do("vector", lambda E: E.tensor_scalar(cols["qp"], cols["qp"], 8.0, signcol[:, :],
                                                 ALU.mult, ALU.mult), inc="V")
        p.do("vector", lambda E: E.tensor_scalar_mul(cols["kp"], cols["kp"], signcol[:, :]), inc="V")
        p.ss("vector")
        p.close()


def ph_ffn_call(ctx, first_call, upto=None, raw_tap=False, dbg_w1=False, mlpdbg=False):
    nc = ctx.nc
    import contextlib
    c1val = None
    for l in (0, 1):
        tag = f"_f{1 if first_call else 2}l{l}"
        with contextlib.ExitStack() as ab:
            fl = dict(tag=tag, raw_tap=raw_tap, dbg_w1=dbg_w1, mlpdbg=mlpdbg)
            fl["qf"] = [ab.enter_context(_sb(nc, f"qf{c}", [128, T], BF16))
                        for c in range(4)]
            fl["kf"] = [ab.enter_context(_sb(nc, f"kf{c}", [128, T], BF16))
                        for c in range(4)]
            fl["vT"] = ab.enter_context(_sb(nc, "vTb", [128, 16 * 520], BF16))
            fl["kh"] = [ab.enter_context(_sb(nc, f"kh{c}", [128, 512], BF16))
                        for c in range(4)]
            fl["vTh"] = ab.enter_context(_sb(nc, "vTh", [128, 4 * 520], BF16))
            if SNAP_MODE:
                fl["snapb"] = nc.dram_tensor(f"snap_b{tag}", [128, KC], F32, kind="ExternalOutput")
            with contextlib.ExitStack() as tbs:
                tabs = {nm: tbs.enter_context(_sb(nc, f"tab_{nm}", [128, T], BF16))
                        for nm in ("cs_q", "ss_q", "cs_k", "ss_k")}
                colsb = tbs.enter_context(_sb(nc, "lp_cols", [128, 4], F32))
                cols = dict(q=colsb[:, 0:1], qp=colsb[:, 1:2], k=colsb[:, 2:3], kp=colsb[:, 3:4])
                fl["tabs"] = tabs
                ph_layer_prep(ctx, l, cols)
                if upto == f"prep{tag}":
                    return None
                ph_tables(ctx, tabs, cols["q"], cols["qp"], cols["k"], cols["kp"])
                if upto == f"tbl{tag}":
                    return dict(tabs=tabs, dump_tabs=True)
                if l == 1:
                    ph_boundary_recv(ctx, c1val)
                ph_ffn_qkv(ctx, l, first_call, fl)
                if upto == f"qkv{tag}":
                    return fl
                if SNAP_MODE:
                    dq = nc.dram_tensor(f"snap_qk{tag}", [8 * 128, T], BF16, kind="ExternalOutput")
                    _unused = fl.get("snapb")
                    pp = Phase(ctx, f"snapqk{tag}")
                    ctx.J.join(pp)
                    for c in range(4):
                        pp.do("sync", lambda E, c=c: E.dma_start(
                            out=dq[c * 128:(c + 1) * 128, :], in_=fl["qf"][c][:, :]), inc="Y", n=16)
                        pp.do("sync", lambda E, c=c: E.dma_start(
                            out=dq[(4 + c) * 128:(5 + c) * 128, :], in_=fl["kf"][c][:, :]), inc="Y", n=16)
                    pp.w("sync", "Y", ctx.ctr["Y"])
                    pp.close()
            ph_ffn_attn(ctx, l, first_call, fl)
            if upto == f"attn{tag}":
                return fl
            if SNAP_MODE:
                ph_snapshot(ctx, f"attn{tag}")
        ph_ffn_mlp(ctx, l, first_call, fl)
        if SNAP_MODE:
            ph_snapshot(ctx, f"mlp{tag}")
        c1val = fl.get("c1val")
        if upto == f"mlp{tag}":
            return fl
    return None


def ph_ffn_combine(ctx, src_dram):
    nc, J = ctx.nc, ctx.J
    xch = ctx.ten["xch"]
    import contextlib
    with contextlib.ExitStack() as st:
        rb = [st.enter_context(_sb(nc, f"cmb{i}", [128, 512], F32)) for i in range(4)]
        p = Phase(ctx, "combine")
        ctx.J.join(p)
        lds = {}
        for i, (c, nb) in enumerate([(c, nb) for c in range(KC) for nb in range(T // 512)]):
            buf = rb[i % 4]
            xs = f"XR{i % 4}"
            if i >= 4:
                p.w("sync", "V", lds[i - 4])
            p.do("sync", lambda E, c=c, nb=nb, buf=buf: E.dma_start(
                out=buf[:, :], in_=src_dram[128 * c:128 * (c + 1), nb * 512:(nb + 1) * 512]),
                inc=xs, n=16)
            p.w("vector", xs, ctx.ctr[xs])
            p.do("vector", lambda E, c=c, nb=nb, buf=buf: E.scalar_tensor_tensor(
                xch[c][:, 1 + nb * 512:1 + (nb + 1) * 512],
                xch[c][:, 1 + nb * 512:1 + (nb + 1) * 512], 0.5, buf[:, :],
                ALU.mult, ALU.add), inc="V")
            lds[i] = ctx.ctr["V"]
        p.close()


def ph_spill(ctx):
    p = Phase(ctx, "spill")
    ctx.J.join(p)
    xch = ctx.ten["xch"]
    for c in range(KC):
        p.do("sync", lambda E, c=c: E.dma_start(
            out=ctx.dram["x_spill"][128 * c:128 * (c + 1), :], in_=xch[c][:, 1:T + 1]),
            inc="Y", n=16)
    p.w("sync", "Y", ctx.ctr["Y"])
    p.close()



def ph_attn(ctx, upto=None):
    nc, J = ctx.nc, ctx.J
    ps = ctx.psum
    xch = ctx.ten["xch"]
    sv, SV = ctx.ten["sv"], ctx.ten["SV"]
    dram = ctx.dram
    m_ltril = ctx.ten["m_ltril"]
    ones_row = ctx.ten["ones_row"]
    ones_col = ctx.ten["ones_col"]
    import contextlib

    with contextlib.ExitStack() as st:
        xhat = [st.enter_context(_sb(nc, f"axh{c}", [128, T], BF16)) for c in range(KC)]
        ph_macaron_ln(ctx, "attn", xhat=xhat)

        W = {}
        for nm in ("wqT", "wkT", "wvT", "woT"):
            W[nm] = [st.enter_context(_sb(nc, f"a{nm}{c}", [128, 1024], BF16))
                     for c in range(KC)]
        fmq = st.enter_context(_sb(nc, "afmq", [128, 128], BF16))
        fmk = st.enter_context(_sb(nc, "afmk", [128, 128], BF16))
        fmrows = st.enter_context(_sb(nc, "afmr", [1, 256], BF16))
        S_sb = st.enter_context(_sb(nc, "aS", [128, 16 * 128], BF16))
        q_c = st.enter_context(_sb(nc, "aqc", [128, 8 * 128], BF16))
        k_c = st.enter_context(_sb(nc, "akc", [128, 8 * 128], BF16))
        vT_c = st.enter_context(_sb(nc, "avc", [128, 8 * 128], BF16))
        phiq = st.enter_context(_sb(nc, "aphq", [128, 16 * 128], BF16))
        phik = st.enter_context(_sb(nc, "aphk", [128, 16 * 128], BF16))
        phikT = st.enter_context(_sb(nc, "aphkT", [128, 16 * 128], BF16))
        Osb = st.enter_context(_sb(nc, "aOsb", [128, 8 * 128], BF16))
        asb = q_c
        Onm = k_c
        sq = phik
        msr = st.enter_context(_sb(nc, "amsr", [1, 512], F32))
        rrow = st.enter_context(_sb(nc, "arr", [1, 1024], BF16))
        imat = st.enter_context(_sb(nc, "aimat", [128, 128], BF16))
        irow = st.enter_context(_sb(nc, "airow", [1, 512], I32))
        wrow = st.enter_context(_sb(nc, "awrow", [1, 512], F32))

        p = Phase(ctx, "attn_ld")
        J.join(p)
        p.do("gpsimd", lambda E: E.iota(q_c[:, 0:128], [[1, 128]], channel_multiplier=-1,
                                        allow_small_or_imprecise_dtypes=True), inc="G")
        p.w("vector", "G", ctx.ctr["G"])
        p.do("vector", lambda E: E.tensor_scalar(imat[:, :], q_c[:, 0:128],
                                                 0.0, None, ALU.is_equal), inc="V")
        p.ss("vector")
        for nm in ("wqT", "wkT", "wvT", "woT"):
            for c in range(KC):
                p.do("gpsimd", lambda E, nm=nm, c=c: E.dma_start(
                    out=W[nm][c][:, :], in_=dram[nm][128 * c:128 * (c + 1), :]),
                    inc="AW", n=16)
        p.do("gpsimd", lambda E: E.dma_start(out=fmq[:, :], in_=dram["fmqT"][:, :]), inc="AW", n=16)
        p.do("gpsimd", lambda E: E.dma_start(out=fmk[:, :], in_=dram["fmkT"][:, :]), inc="AW", n=16)
        p.do("gpsimd", lambda E: E.dma_start(out=fmrows[0:1, 0:128], in_=dram["fmq_b_row"][:, :]), inc="AW", n=16)
        p.do("gpsimd", lambda E: E.dma_start(out=fmrows[0:1, 128:256], in_=dram["fmk_b_row"][:, :]), inc="AW", n=16)
        ldw = ctx.ctr["AW"]
        p.close()

        def proj(p, i, wname, dst):
            half_t = {}
            for h in range(H):
                bank = ps[h // 4]
                col = (h % 4) * 128
                for c in range(KC):
                    p.do("tensor", lambda E, h=h, c=c, i=i, bank=bank, col=col, wname=wname: E.matmul(
                        bank[:, col:col + 128], W[wname][c][:, h * 128:(h + 1) * 128],
                        xhat[c][:, i * CH:(i + 1) * CH],
                        start=(c == 0), stop=(c == KC - 1)), inc="T")
                if h == 3:
                    half_t[0] = ctx.ctr["T"]
            half_t[1] = ctx.ctr["T"]
            for b in range(2):
                p.w("vector", "T", half_t[b])
                p.do("vector", lambda E, b=b, dst=dst: E.tensor_copy(
                    dst[:, b * 512:(b + 1) * 512], ps[b][:, :]), inc="V")
            p.w("tensor", "V", ctx.ctr["V"])
            return ctx.ctr["V"]

        def vproj(p, i):
            for c in range(KC):
                for b in range(2):
                    p.do("tensor", lambda E, c=c, b=b, i=i: E.matmul(
                        ps[b][:, :], xhat[c][:, i * CH:(i + 1) * CH],
                        W["wvT"][c][:, b * 512:(b + 1) * 512],
                        start=(c == 0), stop=(c == KC - 1)), inc="T")
            tm = ctx.ctr["T"]
            p.w("vector", "T", tm)
            for b in range(2):
                p.do("vector", lambda E, b=b: E.tensor_copy(
                    vT_c[:, b * 512:(b + 1) * 512], ps[b][:, :]), inc="V")
            p.w("tensor", "V", ctx.ctr["V"])
            return ctx.ctr["V"]

        def hedgehog(p, src_c, fmw, brow_sl, dstp, want_T=False):
            for h in range(H):
                bank = ps[2 + h % 2]
                if h >= 2:
                    p.w("tensor", "S", hh_done[h - 2])
                p.do("tensor", lambda E, h=h, bank=bank, src_c=src_c, fmw=fmw: E.matmul(
                    bank[:, 0:128], fmw[:, :], src_c[:, h * 128:(h + 1) * 128],
                    start=True, stop=True), inc="T")
                tm = ctx.ctr["T"]
                p.w("scalar", "T", tm)
                p.do("scalar", lambda E, h=h, bank=bank, dstp=dstp: E.activation(
                    dstp[:, h * 128:(h + 1) * 128], bank[:, 0:128], AF.Exp,
                    bias=sv[:, SV["fmq_b" if fmw is fmq else "fmk_b"]:
                            SV["fmq_b" if fmw is fmq else "fmk_b"] + 1]), inc="S")
                p.do("scalar", lambda E, h=h, bank=bank, dstp=dstp: E.activation(
                    dstp[:, 1024 + h * 128:1024 + (h + 1) * 128], bank[:, 0:128], AF.Exp,
                    scale=-1.0,
                    bias=sv[:, SV["fmq_bn" if fmw is fmq else "fmk_bn"]:
                            SV["fmq_bn" if fmw is fmq else "fmk_bn"] + 1]), inc="S")
                hh_done[h] = ctx.ctr["S"]
            p.w("tensor", "S", ctx.ctr["S"])

        def hedgehogT(p, src_c, fmw, brow, dstp):
            for h in range(H):
                bank = ps[2 + h % 2]
                if h >= 2:
                    p.w("tensor", "S", hh_done[h - 2])
                p.do("tensor", lambda E, h=h, bank=bank, src_c=src_c, fmw=fmw: E.matmul(
                    bank[:, 0:128], src_c[:, h * 128:(h + 1) * 128], fmw[:, :],
                    start=True, stop=False), inc="T")
                p.do("tensor", lambda E, h=h, bank=bank, brow=brow: E.matmul(
                    bank[:, 0:128], ones_row[:, :], brow,
                    start=False, stop=True), inc="T")
                tm = ctx.ctr["T"]
                p.w("scalar", "T", tm)
                p.do("scalar", lambda E, h=h, bank=bank, dstp=dstp: E.activation(
                    dstp[:, h * 128:(h + 1) * 128], bank[:, 0:128], AF.Exp), inc="S")
                p.do("scalar", lambda E, h=h, bank=bank, dstp=dstp: E.activation(
                    dstp[:, 1024 + h * 128:1024 + (h + 1) * 128], bank[:, 0:128], AF.Exp,
                    scale=-1.0), inc="S")
                hh_done[h] = ctx.ctr["S"]
            p.w("tensor", "S", ctx.ctr["S"])

        def s_tile(hid, e):
            i = hid * 2 + e
            return ps[4 + i // 4][:, (i % 4) * 128:(i % 4) * 128 + 128]

        pA = Phase(ctx, "attn_A")
        J.join(pA)
        for e in ("tensor", "vector", "scalar"):
            pA.w(e, "AW", ldw)
        hh_done = {}
        supd = 0
        for b in range(4):
            pA.do("vector", lambda E, b=b: E.memset(ps[4 + b][:, :], 0.0), inc="V")
        for i in range(NCH):
            proj(pA, i, "wkT", k_c)
            hedgehogT(pA, k_c, fmk, fmrows[0:1, 128:256], phikT)
            vproj(pA, i)
            pA.w("tensor", "V", ctx.ctr["V"])
            for h in range(H):
                for e in range(2):
                    pA.do("tensor", lambda E, h=h, e=e, i=i: E.matmul(
                        s_tile(h, e), phikT[:, e * 1024 + h * 128:e * 1024 + (h + 1) * 128],
                        vT_c[:, h * 128:(h + 1) * 128],
                        start=False, stop=True,
                        skip_group_check=True), inc="T")
            supd = ctx.ctr["T"]
        pA.w("vector", "T", supd)
        for b in range(4):
            pA.do("vector", lambda E, b=b: E.tensor_copy(
                S_sb[:, b * 512:(b + 1) * 512], ps[4 + b][:, :]), inc="V")
        sev = ctx.ctr["V"]
        pA.w("sync", "V", sev)
        if upto == "attnS":
            dS = nc.dram_tensor("dbg_SA", [128, 2048], BF16, kind="ExternalOutput")
            pA.do("sync", lambda E: E.dma_start(out=dS[:, :], in_=S_sb[:, :]),
                  inc="Y", n=16)
            pA.w("sync", "Y", ctx.ctr["Y"])
        c4 = coll_issue(ctx, pA, [(dram["c4_in"][:, :], S_sb[:, :])],
                        dram["c4_in"], dram["c4_out"])
        pA.close()

        pB = Phase(ctx, "attn_B")
        J.join(pB)
        hh_done = {}
        rv = coll_recv(ctx, pB, c4, [(S_sb[:, :], dram["c4_out"][0:128, :])])
        pB.w("vector", "CR", rv)
        pB.do("vector", lambda E: E.tensor_scalar_mul(
            S_sb[:, :], S_sb[:, :], sv[:, SV["coremask"]:SV["coremask"] + 1]), inc="V")
        for b in range(4):
            pB.do("vector", lambda E, b=b: E.memset(ps[4 + b][:, :], 0.0), inc="V")
        vmask = ctx.ctr["V"]
        pB.w("tensor", "V", vmask)
        for h in range(H):
            for e in range(2):
                pB.do("tensor", lambda E, h=h, e=e: E.matmul(
                    s_tile(h, e), imat[:, :],
                    S_sb[:, (h * 2 + e) * 128:(h * 2 + e + 1) * 128],
                    start=False, stop=True, skip_group_check=True), inc="T")
        sinj = ctx.ctr["T"]
        supd = sinj
        for i in range(NCH):
            pB.w("vector", "T", supd)
            for b in range(4):
                pB.do("vector", lambda E, b=b: E.tensor_scalar_mul(
                    S_sb[:, b * 512:(b + 1) * 512], ps[4 + b][:, :], PHI_SCALE), inc="V")
            sev = ctx.ctr["V"]
            proj(pB, i, "wqT", q_c)
            hedgehog(pB, q_c, fmq, None, phiq)
            proj(pB, i, "wkT", k_c)
            hedgehog(pB, k_c, fmk, None, phik)
            hedgehogT(pB, k_c, fmk, fmrows[0:1, 128:256], phikT)
            vproj(pB, i)
            vready = ctx.ctr["V"]
            sc_done = {}
            ov_done = {}
            for h in range(H):
                bank = ps[h % 2]
                if h >= 2:
                    pB.w("tensor", "V", sc_done[h - 2])
                pB.do("tensor", lambda E, h=h, bank=bank: E.matmul(
                    bank[:, 0:128], phik[:, h * 128:(h + 1) * 128],
                    phiq[:, h * 128:(h + 1) * 128], start=True, stop=False), inc="T")
                pB.do("tensor", lambda E, h=h, bank=bank: E.matmul(
                    bank[:, 0:128], phik[:, 1024 + h * 128:1024 + (h + 1) * 128],
                    phiq[:, 1024 + h * 128:1024 + (h + 1) * 128],
                    start=False, stop=True), inc="T")
                tsc = ctx.ctr["T"]
                pB.w("vector", "T", tsc)
                pB.do("vector", lambda E, h=h, bank=bank: E.tensor_mul(
                    asb[:, h * 128:(h + 1) * 128], bank[:, 0:128],
                    m_ltril[:, 0:CH]), inc="V")
                sc_done[h] = ctx.ctr["V"]
                pB.w("tensor", "V", sc_done[h])
                ob = ps[2 + h % 2]
                if h >= 2:
                    pB.w("tensor", "V", ov_done[h - 2])
                pB.do("tensor", lambda E, h=h, ob=ob: E.matmul(
                    ob[:, 0:128], vT_c[:, h * 128:(h + 1) * 128],
                    asb[:, h * 128:(h + 1) * 128], start=True, stop=False), inc="T")
                pB.do("tensor", lambda E, h=h, ob=ob: E.matmul(
                    ob[:, 0:128], S_sb[:, (h * 2) * 128:(h * 2 + 1) * 128],
                    phiq[:, h * 128:(h + 1) * 128], start=False, stop=False), inc="T")
                pB.do("tensor", lambda E, h=h, ob=ob: E.matmul(
                    ob[:, 0:128], S_sb[:, (h * 2 + 1) * 128:(h * 2 + 2) * 128],
                    phiq[:, 1024 + h * 128:1024 + (h + 1) * 128],
                    start=False, stop=True), inc="T")
                tob = ctx.ctr["T"]
                pB.w("vector", "T", tob)
                pB.do("vector", lambda E, h=h, ob=ob: E.tensor_copy(
                    Osb[:, h * 128:(h + 1) * 128], ob[:, 0:128]), inc="V")
                ov_done[h] = ctx.ctr["V"]
            if upto is not None and upto.startswith("attnD") and int(upto[5:]) == i:
                douts = {}
                for nm, buf in (("phiq", phiq), ("phik", phik), ("asb", asb),
                                ("Osb", Osb), ("Ssb", S_sb), ("vTc", vT_c)):
                    douts[nm] = nc.dram_tensor(f"dbg_{nm}", list(buf.shape), BF16,
                                               kind="ExternalOutput")
                pB.w("sync", "V", ctx.ctr["V"])
                for nm, buf in (("phiq", phiq), ("phik", phik), ("asb", asb),
                                ("Osb", Osb), ("Ssb", S_sb), ("vTc", vT_c)):
                    pB.do("sync", lambda E, nm=nm, buf=buf: E.dma_start(
                        out=douts[nm][:, :], in_=buf[:, :]), inc="Y", n=16)
                pB.w("sync", "Y", ctx.ctr["Y"])
                for e in ("tensor", "vector", "scalar"):
                    pB.w(e, "Y", ctx.ctr["Y"])
            pB.w("tensor", "V", ctx.ctr["V"])
            for h in range(H):
                for e in range(2):
                    pB.do("tensor", lambda E, h=h, e=e, i=i: E.matmul(
                        s_tile(h, e), phikT[:, e * 1024 + h * 128:e * 1024 + (h + 1) * 128],
                        vT_c[:, h * 128:(h + 1) * 128],
                        start=False, stop=True, skip_group_check=True), inc="T")
            supd = ctx.ctr["T"]
            vsq = ctx.ctr["V"]
            pB.w("scalar", "V", vsq)
            pB.do("scalar", lambda E: E.activation(sq[:, 0:1024], Osb[:, 0:1024], AF.Square), inc="S")
            ssq = ctx.ctr["S"]
            pB.w("tensor", "S", ssq)
            pB.do("tensor", lambda E: E.matmul(ps[0][0:1, :], ones_col[:, :],
                                               sq[:, 0:512], start=True, stop=True), inc="T")
            pB.do("tensor", lambda E: E.matmul(ps[1][0:1, :], ones_col[:, :],
                                               sq[:, 512:1024], start=True, stop=True), inc="T")
            tms = ctx.ctr["T"]
            pB.w("vector", "T", tms)
            for b in range(2):
                pB.do("vector", lambda E, b=b: E.tensor_scalar(
                    msr[0:1, :], ps[b][0:1, :], 1.0 / DV, 1e-5,
                    ALU.mult, ALU.add), inc="V")
                pB.ss("vector")
                pB.do("vector", lambda E: E.tensor_scalar(
                    irow[0:1, :], msr[0:1, :].bitcast(I32), 23, None,
                    ALU.logical_shift_right), inc="V")
                pB.ss("vector")
                pB.do("vector", lambda E: E.tensor_scalar(
                    irow[0:1, :], irow[0:1, :], 127, None,
                    ALU.subtract), inc="V")
                pB.ss("vector")
                pB.do("vector", lambda E: E.tensor_scalar(
                    msr[0:1, :].bitcast(I32), msr[0:1, :].bitcast(I32),
                    0x7FFFFF, 0x3F800000,
                    ALU.bitwise_and, ALU.bitwise_or), inc="V")
                vf = ctx.ctr["V"]
                pB.w("scalar", "V", vf)
                pB.do("scalar", lambda E: E.activation(wrow[0:1, :], msr[0:1, :],
                                                       AF.Ln), inc="S")
                slf = ctx.ctr["S"]
                pB.w("vector", "S", slf)
                pB.do("vector", lambda E: E.tensor_copy(msr[0:1, :], irow[0:1, :]),
                      inc="V")
                pB.ss("vector")
                pB.do("vector", lambda E: E.scalar_tensor_tensor(
                    wrow[0:1, :], msr[0:1, :], float(np.log(2.0)), wrow[0:1, :],
                    ALU.mult, ALU.add), inc="V")
                vw = ctx.ctr["V"]
                pB.w("scalar", "V", vw)
                pB.do("scalar", lambda E, b=b: E.activation(
                    rrow[0:1, b * 512:(b + 1) * 512], wrow[0:1, :], AF.Exp,
                    scale=-0.5), inc="S")
            srr = ctx.ctr["S"]
            pB.w("tensor", "S", srr)
            pB.do("tensor", lambda E: E.matmul(ps[0][:, :], ones_row[:, :],
                                               rrow[0:1, 0:512], start=True, stop=True), inc="T")
            pB.do("tensor", lambda E: E.matmul(ps[1][:, :], ones_row[:, :],
                                               rrow[0:1, 512:1024], start=True, stop=True), inc="T")
            trb = ctx.ctr["T"]
            pB.w("vector", "T", trb)
            for b in range(2):
                pB.do("vector", lambda E, b=b: E.scalar_tensor_tensor(
                    Onm[:, b * 512:(b + 1) * 512], Osb[:, b * 512:(b + 1) * 512],
                    sv[:, SV["rms_w"]:SV["rms_w"] + 1], ps[b][:, :],
                    ALU.mult, ALU.mult), inc="V")
            vnm = ctx.ctr["V"]
            if upto is not None and upto.startswith("attnE") and int(upto[5:]) == i:
                deo = {}
                for nm, buf, dt in (("Onm", Onm, BF16), ("rrow", rrow, BF16),
                                    ("msr", msr, F32), ("Osb", Osb, BF16)):
                    deo[nm] = nc.dram_tensor(f"dbg_{nm}", list(buf.shape), dt,
                                             kind="ExternalOutput")
                pB.w("sync", "V", ctx.ctr["V"])
                pB.w("sync", "S", ctx.ctr["S"])
                for nm, buf in (("Onm", Onm), ("rrow", rrow), ("msr", msr), ("Osb", Osb)):
                    pB.do("sync", lambda E, nm=nm, buf=buf: E.dma_start(
                        out=deo[nm][:, :], in_=buf[:, :]), inc="Y", n=16)
                pB.w("sync", "Y", ctx.ctr["Y"])
                for e in ("tensor", "vector", "scalar"):
                    pB.w(e, "Y", ctx.ctr["Y"])
            pB.w("tensor", "V", vnm)
            woevd = {}
            for m in range(8):
                bank = ps[m % 4]
                if m >= 4:
                    pB.w("tensor", "V", woevd[m - 4])
                for ic in range(KC):
                    pB.do("tensor", lambda E, m=m, ic=ic, bank=bank: E.matmul(
                        bank[:, 0:128], W["woT"][ic][:, m * 128:(m + 1) * 128],
                        Onm[:, ic * 128:(ic + 1) * 128],
                        start=(ic == 0), stop=(ic == KC - 1)), inc="T")
                tw = ctx.ctr["T"]
                pB.w("vector", "T", tw)
                pB.do("vector", lambda E, m=m, bank=bank, i=i: E.tensor_add(
                    xch[m][:, 1 + i * CH:1 + (i + 1) * CH],
                    xch[m][:, 1 + i * CH:1 + (i + 1) * CH], bank[:, 0:128]), inc="V")
                woevd[m] = ctx.ctr["V"]
            pB.w("tensor", "V", ctx.ctr["V"])
        pB.close()

    p = Phase(ctx, "c0issue")
    J.join(p)
    pairs = [(dram["c1_in"][:, c:c + 1], xch[c][:, T:T + 1]) for c in range(KC)]
    c0 = coll_issue(ctx, p, pairs, dram["c1_in"], dram["c1_out"],
                    waits=[("V", ctx.ctr["V"])])
    p.close()
    return c0


def ph_snapshot(ctx, name):
    nc = ctx.nc
    d = nc.dram_tensor(f"snap_{name}", [D, T], F32, kind="ExternalOutput")
    xch = ctx.ten["xch"]
    p = Phase(ctx, f"snap_{name}")
    ctx.J.join(p)
    for c in range(KC):
        p.do("sync", lambda E, c=c: E.dma_start(
            out=d[128 * c:128 * (c + 1), :], in_=xch[c][:, 1:T + 1]), inc="Y", n=16)
    p.w("sync", "Y", ctx.ctr["Y"])
    p.close()


def ph_boundary_recv(ctx, ccval):
    nc, J = ctx.nc, ctx.J
    sv, SV = ctx.ten["sv"], ctx.ten["SV"]
    xch = ctx.ten["xch"]
    import contextlib
    with contextlib.ExitStack() as st:
        bndb = st.enter_context(_sb(nc, "brc", [128, KC], F32))
        p = Phase(ctx, "bndrecv")
        J.join(p)
        rv = coll_recv(ctx, p, ccval, [(bndb[:, :], ctx.dram["c1_out"][0:128, :])])
        p.w("vector", "CR", rv)
        for c in range(KC):
            p.do("vector", lambda E, c=c: E.tensor_scalar_mul(
                xch[c][:, 0:1], bndb[:, c:c + 1],
                sv[:, SV["coremask"]:SV["coremask"] + 1]), inc="V")
        p.close()


def ph_dump_tabs(ctx, tabs):
    nc = ctx.nc
    outs = {}
    for nm in ("cs_q", "ss_q", "cs_k", "ss_k"):
        outs[nm] = nc.dram_tensor(f"dbg_{nm}", [128, T], BF16, kind="ExternalOutput")
    p = Phase(ctx, "dumptabs")
    ctx.J.join(p)
    for nm in outs:
        p.do("sync", lambda E, nm=nm: E.dma_start(out=outs[nm][:, :], in_=tabs[nm][:, :]),
             inc="Y", n=16)
    p.w("sync", "Y", ctx.ctr["Y"])
    if ctx.ctr.get("CC", 0):
        p.w("sync", "CC", ctx.ctr["CC"])
    p.close()


def ph_dump_qkv(ctx, fl):
    nc = ctx.nc
    dq = nc.dram_tensor("dbg_q", [4 * 128, T], BF16, kind="ExternalOutput")
    dk = nc.dram_tensor("dbg_k", [4 * 128, T], BF16, kind="ExternalOutput")
    dv = nc.dram_tensor("dbg_v", [128, 16 * 520], BF16, kind="ExternalOutput")
    p = Phase(ctx, "dumpqkv")
    ctx.J.join(p)
    for c in range(4):
        p.do("sync", lambda E, c=c: E.dma_start(out=dq[128 * c:128 * (c + 1), :],
                                                in_=fl["qf"][c][:, :]), inc="Y", n=16)
        p.do("sync", lambda E, c=c: E.dma_start(out=dk[128 * c:128 * (c + 1), :],
                                                in_=fl["kf"][c][:, :]), inc="Y", n=16)
    p.do("sync", lambda E: E.dma_start(out=dv[:, :], in_=fl["vT"][:, :]), inc="Y", n=16)
    p.w("sync", "Y", ctx.ctr["Y"])
    if ctx.ctr.get("CC", 0):
        p.w("sync", "CC", ctx.ctr["CC"])
    p.close()


def ph_finale(ctx, src="x"):
    xch = ctx.ten["xch"]
    p = Phase(ctx, "finale")
    ctx.J.join(p)
    for c in range(KC):
        p.do("sync", lambda E, c=c: E.dma_start(
            out=ctx.dram["out"][128 * c:128 * (c + 1), :], in_=xch[c][:, 1:T + 1]), inc="Y", n=16)
    p.w("sync", "Y", ctx.ctr["Y"])
    if ctx.ctr.get("CC", 0):
        p.w("sync", "CC", ctx.ctr["CC"])
    p.close()



def kernel(**inputs):
    x = np.asarray(inputs["L0_attn_x"])
    perm = _w1_perm()

    w_shared = {
        "wqT": np.ascontiguousarray(np.asarray(inputs["L0_attn_Wq"]).T),
        "wkT": np.ascontiguousarray(np.asarray(inputs["L0_attn_Wk"]).T),
        "wvT": np.ascontiguousarray(np.asarray(inputs["L0_attn_Wv"]).T),
        "woT": np.ascontiguousarray(np.asarray(inputs["L0_attn_Wo"]).T),
        "fmqT": np.ascontiguousarray(np.asarray(inputs["L0_attn_fmq_w"]).T),
        "fmkT": np.ascontiguousarray(np.asarray(inputs["L0_attn_fmk_w"]).T),
        "fmq_b": np.ascontiguousarray(np.asarray(inputs["L0_attn_fmq_b"]).reshape(DK, 1)),
        "fmk_b": np.ascontiguousarray(np.asarray(inputs["L0_attn_fmk_b"]).reshape(DK, 1)),
        "fmq_b_row": np.ascontiguousarray(np.asarray(inputs["L0_attn_fmq_b"]).reshape(1, DK)),
        "fmk_b_row": np.ascontiguousarray(np.asarray(inputs["L0_attn_fmk_b"]).reshape(1, DK)),
        "rms_w": np.ascontiguousarray(np.asarray(inputs["L0_attn_rms_w"]).reshape(DK, 1)),
    }
    for l in range(2):
        w_shared[f"wqkv{l}"] = np.ascontiguousarray(np.asarray(inputs["L0_ffn_wqkv"])[l])
        w_shared[f"wo{l}"] = np.ascontiguousarray(np.asarray(inputs["L0_ffn_wo"])[l])
        w_shared[f"w2{l}"] = np.ascontiguousarray(np.asarray(inputs["L0_ffn_w2"])[l])
        w_shared[f"w1p{l}"] = np.ascontiguousarray(np.asarray(inputs["L0_ffn_w1"])[l][:, perm])
        b1p = np.asarray(inputs["L0_ffn_b1"])[l][perm]
        b1cols = np.zeros((128, 2 * NPAIR), np.float32)
        col = 0
        for pr in range(NPAIR):
            off, wp = _pair_off(pr)
            b1cols[0:wp, 2 * pr] = b1p[off:off + wp]
            b1cols[0:wp, 2 * pr + 1] = b1p[off + wp:off + 2 * wp]
        w_shared[f"b1p{l}"] = b1cols
        w_shared[f"b2{l}"] = np.ascontiguousarray(np.asarray(inputs["L0_ffn_b2"])[l].reshape(KC, 128).T)
        w_shared[f"lng{l}"] = np.ascontiguousarray(np.asarray(inputs["L0_ffn_ln_g"])[l].reshape(KC, 128).T)
        w_shared[f"lnb{l}"] = np.ascontiguousarray(np.asarray(inputs["L0_ffn_ln_b"])[l].reshape(KC, 128).T)
        w_shared[f"qs{l}"] = np.ascontiguousarray(np.tile(np.asarray(inputs["L0_ffn_q_scale"])[l], 2).reshape(128, 1))
        w_shared[f"ks{l}"] = np.ascontiguousarray(np.tile(np.asarray(inputs["L0_ffn_k_scale"])[l], 2).reshape(128, 1))

    in_maps = []
    for c in range(8):
        b, h = c // 2, c % 2
        shard = np.ascontiguousarray(x[b, h * T:(h + 1) * T, :].T)
        if h == 1:
            xprev = np.ascontiguousarray(x[b, h * T - 1, :].reshape(KC, 128).T)
            cm = np.ones((128, 1), np.float32)
        else:
            xprev = np.zeros((128, KC), np.float32)
            cm = np.zeros((128, 1), np.float32)
        m = dict(w_shared)
        m["x_t"] = shard
        m["xprev"] = xprev
        m["coremask"] = cm
        m["toff"] = np.array([[h * T]], np.float32)
        in_maps.append(m)

    nc, ctx = build_program(upto=kernel._upto)
    res = run_bass_kernel_spmd(nc, in_maps, core_ids=list(range(8)),
                               **(dict(trace=True) if kernel._trace else {}))
    if kernel._trace:
        kernel._last_scope_times = res.per_core_scope_times

    if (kernel._upto in ("ffn1l0qkv", "ffn1l0qkvraw", "tbl", "mlpdbg")
            or kernel._upto.startswith("attnw") or kernel._upto.startswith("attnD")
            or kernel._upto.startswith("attnE") or kernel._upto == "attnS"):
        return res.results

    out = np.empty((B, N_FULL, D), np.float32)
    for c in range(8):
        b, h = c // 2, c % 2
        out[b, h * T:(h + 1) * T, :] = res.results[c]["out"].T
    return out


kernel._upto = "full"
kernel._trace = False
kernel._last_scope_times = None

if __name__ == "__main__":
    pass
